# version 03
# baseline (speedup 1.0000x reference)
import numpy as np
import ml_dtypes
import concourse.bass as bass
import concourse.mybir as mybir
from concourse.bass_utils import run_bass_kernel_spmd
from concourse.alu_op_type import AluOpType as A

F32 = mybir.dt.float32
BF16 = mybir.dt.bfloat16
I32 = mybir.dt.int32
AX = mybir.AxisListType.X
AF = mybir.ActivationFunctionType

S = 2048
D = 1024
DC = D // 128
NT = S // 128
INNER = 4096
IC = INNER // 128
HEADS = 8
HD = 128
FD = 128
BLK = 32
NSEQ = 2
NCORE = 8
SCALE0 = 1024 ** -0.5
NEG = -1.0e30

RW = 16384
R0, R1, R2, R3, R4 = 0, RW, 2 * RW, 3 * RW, 4 * RW


class Ctx:
    ENGS = ("sync", "gpsimd", "vector", "scalar", "tensor")

    def __init__(self, nc, dbg):
        self.nc = nc
        self.dbg = set(dbg or ())
        self.prog = {e: [] for e in self.ENGS}
        self.cnt = {}
        self.sems = {}
        self.dbg_out = {}
        self.waited = {}
        self.prior_waited = {}

    def emit(self, eng, fn):
        self.prog[eng].append(fn)

    def inc(self, sem, amt=1):
        self.cnt[sem] = self.cnt.get(sem, 0) + amt
        return self.cnt[sem]

    def val(self, sem):
        return self.cnt.get(sem, 0)

    def op(self, eng, meth, *args, inc=None, inc_amt=1, **kw):
        sems = self.sems
        if inc is not None:
            self.inc(inc, inc_amt)
        if eng == "tensor" and meth == "nop" and inc is not None:
            meth = "drain"
        fence = inc is not None and eng in ("vector", "scalar")
        pre_drain = eng in ("vector", "scalar") and meth not in ("nop", "drain")

        skip_op = eng in ("vector", "scalar") and meth == "nop" and inc is not None

        def fn(e, meth=meth, args=args, kw=kw, inc=inc, inc_amt=inc_amt,
               fence=fence, pre_drain=pre_drain, skip_op=skip_op):
            if pre_drain:
                e.drain()
            if skip_op:
                e.drain()
                e.drain().then_inc(sems[inc], inc_amt)
                return
            r = getattr(e, meth)(*args, **kw)
            if inc is not None:
                if fence:
                    e.drain().then_inc(sems[inc], inc_amt)
                else:
                    r.then_inc(sems[inc], inc_amt)
        self.emit(eng, fn)

    def fence(self, eng):
        sems = self.sems
        n = self.inc("f" + eng)

        def fn(e, n=n, s="f" + eng):
            e.drain()
            e.drain().then_inc(sems[s], 1)
            e.wait_ge(sems[s], n)
        self.emit(eng, fn)

    def wait(self, eng, sem, v=None):
        v = self.val(sem) if v is None else v
        self.waited.setdefault(sem, set()).add(v)
        sems = self.sems
        self.emit(eng, lambda e, s=sem, v=v: e.wait_ge(sems[s], v))

    def dma(self, eng, out, in_, inc="sD", **kw):
        if self.val(inc) in self.prior_waited.get(inc, ()):  
            self.wait(eng, inc)
        self.op(eng, "dma_start", inc=inc, inc_amt=16, out=out, in_=in_, **kw)

    def barrier(self):
        self.wait("sync", "sD")
        self.wait("sync", "sSp")
        self.wait("gpsimd", "sG")
        for e in self.ENGS:
            self.op(e, "nop", inc="b" + e)
        for e in self.ENGS:
            for o in self.ENGS:
                if o != e:
                    self.wait(e, "b" + o)

    def tap(self, name, src_ap):
        if name not in self.dbg:
            return
        self.barrier()
        self.dma("sync", self.dbg_out[name], src_ap)
        self.barrier()


def _hl(ctx, dst_hi, dst_lo, src_f32, fence=False):
    ctx.op("vector", "tensor_copy", out=dst_hi, in_=src_f32)
    if fence:
        ctx.fence("vector")
    ctx.op("vector", "tensor_tensor", out=dst_lo, in0=src_f32, in1=dst_hi,
           op=A.subtract)


def _mm3_acc(ctx, lhsT_h, lhsT_l, rhs_h, rhs_l, out, first, last):
    ctx.op("tensor", "matmul", out=out, lhsT=lhsT_h, rhs=rhs_h,
           start=first, stop=False)
    ctx.op("tensor", "matmul", out=out, lhsT=lhsT_h, rhs=rhs_l,
           start=False, stop=False)
    ctx.op("tensor", "matmul", out=out, lhsT=lhsT_l, rhs=rhs_h,
           start=False, stop=last)


def build_core(dbg=None, stop=None):
    nc = bass.Bass("TRN2", target_bir_lowering=False)
    ctx = Ctx(nc, dbg)

    x_in = nc.dram_tensor("x", [NSEQ * S, D], F32, kind="ExternalInput").ap()
    out = nc.dram_tensor("out", [NSEQ * S, D], F32, kind="ExternalOutput").ap()

    win = {}
    for nm, shp in [
        ("wqT", [D, D]), ("wkT", [D, D]), ("wvT", [D, D]),
        ("wgT", [D, INNER]), ("wuT", [D, INNER]), ("wdT", [INNER, D]),
        ("wqkvT", [D, 3 * D]), ("woT", [D, D]), ("projT", [D, FD]),
        ("W1", [D, INNER]), ("W2", [INNER, D]), ("wrT", [D, 1]),
    ]:
        win[nm] = nc.dram_tensor(nm, shp, F32, kind="ExternalInput").ap()
    cin = {}
    for nm, shp, dtp in [
        ("identf", [128, 128], F32), ("identb", [128, 128], BF16),
        ("onesf", [1, 128], F32), ("onescol", [128, 1], F32),
        ("masks", [128, 4 * 512], F32), ("b1fm", [128, IC], F32),
        ("b2fm", [128, DC], F32), ("boutr", [1, D], F32),
        ("bconst", [128, 1], F32),
    ]:
        cin[nm] = nc.dram_tensor(nm, shp, dtp, kind="ExternalInput").ap()

    dt = nc.dram_tensor
    p = {}
    for nm, shp in [
        ("wqT_h", [D, D]), ("wqT_l", [D, D]), ("wkT_h", [D, D]), ("wkT_l", [D, D]),
        ("wvT_h", [D, D]), ("wvT_l", [D, D]),
        ("wqkvT_h", [D, 3 * D]), ("wqkvT_l", [D, 3 * D]),
        ("woT_h", [D, D]), ("woT_l", [D, D]),
        ("projT_h", [D, FD]), ("projT_l", [D, FD]),
        ("W1_h", [D, INNER]), ("W1_l", [D, INNER]),
        ("W2_h", [INNER, D]), ("W2_l", [INNER, D]),
        ("wrT_h", [D, 1]), ("wrT_l", [D, 1]),
        ("wgq", [D, INNER]), ("wuq", [D, INNER]), ("wdq", [INNER, D]),
    ]:
        p[nm] = dt("p_" + nm, shp, BF16).ap()
    sc = {}
    for nm, shp, dtp in [
        ("xnq", [NT, DC, 128, 128], BF16),
        ("q0_h", [D, S], BF16), ("q0_l", [D, S], BF16),
        ("v0_h", [S, D], BF16), ("v0_l", [S, D], BF16),
        ("mid", [S, INNER], F32),
        ("g0", [S, D], F32),
        ("x1", [S, D], F32),
        ("xn1", [S, D], F32),
        ("srow", [NT, 128], F32),
        ("q1_h", [D, S], BF16), ("q1_l", [D, S], BF16),
        ("k1_h", [D, S], BF16), ("k1_l", [D, S], BF16),
        ("g1", [S, D], F32),
        ("absr", [128, NT], F32),
    ]:
        sc[nm] = dt("s_" + nm, shp, dtp).ap()

    for nm in (dbg or ()):
        shp, dtp = {
            "stats": ([128, 64], F32), "xnfm_hi": ([128, RW], BF16),
            "xnfm_lo": ([128, RW], BF16), "xnq": ([NT * DC * 128, 128], BF16),
            "kfm_hi": ([128, RW], BF16), "kfm_lo": ([128, RW], BF16),
            "q0_h": ([D, S], BF16), "q0_l": ([D, S], BF16),
            "v0_l": ([S, D], BF16),
            "v0_h": ([S, D], BF16), "mid": ([S, INNER], F32),
            "g0": ([S, D], F32), "x1": ([S, D], F32), "xn1": ([S, D], F32),
            "srow": ([NT, 128], F32), "wtok": ([128, NT], F32),
            "g1": ([S, D], F32), "attnfm_hi": ([128, RW], BF16),
            "wgq": ([D, INNER], BF16), "wdq": ([INNER, D], BF16),
            "wqT_h": ([D, D], BF16), "wqT_l": ([D, D], BF16),
            "absr": ([128, NT], F32),
            "wstat": ([128, 16], F32),
        }[nm]
        ctx.dbg_out[nm] = dt("dbg_" + nm, shp, dtp, kind="ExternalOutput").ap()

    import contextlib
    es = contextlib.ExitStack()
    with es:
        E = es.enter_context
        arb = E(nc.sbuf_tensor("arb", [128, 5 * RW], BF16))
        arf = E(nc.sbuf_tensor("arf", [128, 9216], F32))
        masks = E(nc.sbuf_tensor("sb_masks", [128, 4 * 512], F32))
        identf = E(nc.sbuf_tensor("sb_identf", [128, 128], F32))
        identb = E(nc.sbuf_tensor("sb_identb", [128, 128], BF16))
        onesf = E(nc.sbuf_tensor("sb_onesf", [1, 128], F32))
        onescol = E(nc.sbuf_tensor("sb_onescol", [128, 1], F32))
        tiny = E(nc.sbuf_tensor("tiny", [128, 112], F32))
        bconst = E(nc.sbuf_tensor("sb_bconst", [128, 1], F32))
        nc.const_aps.register(F32, 1e-5, bconst[:, 0:1])
        ps = E(nc.psum_tensor("ps", [128, 4096], F32))
        for s_name in ("sD", "sW", "sSp", "sT", "sV", "sC", "sG",
                       "fvector", "fscalar",
                       *("b" + e for e in Ctx.ENGS)):
            ctx.sems[s_name] = E(nc.semaphore(s_name))

        st = dict(stop=stop, x_in=x_in, out=out, win=win, p=p, sc=sc, arb=arb, arf=arf,
                  masks=masks, identf=identf, identb=identb, onesf=onesf,
                  onescol=onescol, brow=arf[0:1, 5120:6144], tiny=tiny,
                  bconst=bconst,
                  trow=arf[0:1, 4096:8256], ps=ps,
                  cin=cin)
        bld_all(ctx, st)
        ctx.prior_waited = {k: set(v) for k, v in ctx.waited.items()}
        ctx.prog = {e: [] for e in Ctx.ENGS}
        ctx.cnt = {}
        ctx.waited = {}
        bld_all(ctx, st)

        blk = E(nc.Block())

        def mk(engname):
            def run(e):
                for fn in ctx.prog[engname]:
                    fn(e)
            return run
        blk.sync(mk("sync"))
        blk.gpsimd(mk("gpsimd"))
        blk.vector(mk("vector"))
        blk.scalar(mk("scalar"))
        blk.tensor(mk("tensor"))
    return nc


def bld_all(ctx, st):
    cin = st["cin"]
    ctx.dma("sync", st["masks"][:], cin["masks"])
    ctx.dma("sync", st["identf"][:], cin["identf"])
    ctx.dma("sync", st["identb"][:], cin["identb"])
    ctx.dma("sync", st["onesf"][:], cin["onesf"])
    ctx.dma("sync", st["onescol"][:], cin["onescol"])
    ctx.dma("sync", st["bconst"][:], cin["bconst"])
    ctx.wait("sync", "sD")
    for e in ("vector", "scalar", "tensor"):
        ctx.wait(e, "sD")

    stop = st.get("stop")
    def done():
        ctx.wait("sync", "sSp")
        ctx.wait("sync", "sD")
        ctx.wait("sync", "sG")
    phase_W(ctx, st)
    ctx.barrier()
    if stop == "W":
        return done()
    nseq = 1 if stop else NSEQ
    for s in range(nseq):
        for nm, fn in [("A", phase_A), ("C", phase_C), ("B", phase_B),
                       ("D", phase_D), ("E", phase_E), ("F", phase_F),
                       ("G", phase_G), ("H", phase_H)]:
            fn(ctx, st, s)
            ctx.barrier()
            if stop == nm or (nm == "C" and stop in ("Cq", "Ck")):
                return done()
    done()



def phase_W(ctx, st):
    arb, arf, ps, tiny = st["arb"], st["arf"], st["ps"], st["tiny"]
    onesf, onescol = st["onesf"], st["onescol"]
    win, p = st["win"], st["p"]

    f32buf = arb[:, R2:R2 + 8192].bitcast(F32)
    for nm, w in [("wqT", D), ("wkT", D), ("wvT", D), ("wqkvT", 3 * D),
                  ("woT", D), ("projT", FD), ("W1", INNER), ("W2", D),
                  ("wrT", 1)]:
        src = win[nm]
        nchunk = src.shape[0] // 128
        for c in range(nchunk):
            fb = f32buf[:, 0:w]
            hb = arb[:, R3:R3 + w]
            lb = arb[:, R3 + w:R3 + 2 * w]
            ctx.dma("sync", fb, src[128 * c:128 * (c + 1), :])
            ctx.wait("gpsimd", "sV")
            ctx.wait("gpsimd", "sSp")
            ctx.dma("gpsimd", hb, src[128 * c:128 * (c + 1), :], inc="sG")
            ctx.wait("vector", "sD")
            ctx.wait("vector", "sG")
            ctx.wait("vector", "sSp")
            ctx.op("vector", "tensor_tensor", out=lb, in0=fb, in1=hb,
                   op=A.subtract, inc="sV")
            ctx.wait("sync", "sV")
            ctx.dma("sync", p[nm + "_h"][128 * c:128 * (c + 1), :], hb, inc="sSp")
            ctx.dma("sync", p[nm + "_l"][128 * c:128 * (c + 1), :], lb, inc="sSp")
            ctx.wait("sync", "sSp")
    ctx.tap("wqT_h", p["wqT_h"])
    ctx.tap("wqT_l", p["wqT_l"])

    for nm, w, outnm in [("wgT", INNER, "wgq"), ("wuT", INNER, "wuq"),
                         ("wdT", D, "wdq")]:
        src = win[nm]
        nchunk = src.shape[0] // 128
        nelem = float(src.shape[0] * src.shape[1])
        acc = tiny[:, 64:65]
        trash = arf[:, 4096:4096 + w]
        for c in range(nchunk):
            fb = f32buf[:, 0:w]
            ctx.wait("sync", "sC")
            ctx.dma("sync", fb, src[128 * c:128 * (c + 1), :])
            ctx.wait("scalar", "sD")
            ctx.op("scalar", "activation", out=trash, in_=fb, func=AF.Abs,
                   accum_out=tiny[:, 65:66], inc="sC")
            ctx.wait("vector", "sC")
            if c == 0:
                ctx.op("vector", "tensor_copy", out=acc, in_=tiny[:, 65:66],
                       inc="sV")
            else:
                ctx.op("vector", "tensor_tensor", out=acc, in0=acc,
                       in1=tiny[:, 65:66], op=A.add, inc="sV")
            ctx.fence("vector")
            ctx.wait("scalar", "sV")
        ctx.wait("tensor", "sV")
        ctx.op("tensor", "matmul", out=ps[0:1, 2048:2049], lhsT=onescol[:],
               rhs=acc, start=True, stop=True, inc="sT")
        ctx.wait("vector", "sT")
        ctx.op("vector", "tensor_scalar", out=tiny[0:1, 66:67],
               in0=ps[0:1, 2048:2049], scalar1=1.0 / nelem, scalar2=1e-5,
               op0=A.mult, op1=A.max)
        ctx.fence("vector")
        ctx.op("vector", "tensor_scalar", out=tiny[0:1, 67:68],
               in0=tiny[0:1, 66:67], scalar1=0.5, scalar2=None, op0=A.mult,
               inc="sV")
        ctx.wait("tensor", "sV")
        ctx.op("tensor", "matmul", out=ps[:, 2052:2053], lhsT=onesf[:],
               rhs=tiny[0:1, 66:67], start=True, stop=True)
        ctx.op("tensor", "matmul", out=ps[:, 2053:2054], lhsT=onesf[:],
               rhs=tiny[0:1, 67:68], start=True, stop=True, inc="sT")
        ctx.wait("vector", "sT")
        mw_slot = {"wgq": 68, "wuq": 69, "wdq": 70}[outnm]
        ctx.op("vector", "tensor_copy", out=tiny[:, mw_slot:mw_slot + 1],
               in_=ps[:, 2052:2053])
        thr = tiny[:, 71:72]
        nthr = tiny[:, 72:73]
        ctx.op("vector", "tensor_copy", out=thr, in_=ps[:, 2053:2054])
        ctx.fence("vector")
        ctx.op("vector", "tensor_scalar", out=nthr, in0=thr, scalar1=-1.0,
               scalar2=None, op0=A.mult, inc="sV")
        ctx.fence("vector")
        for c in range(nchunk):
            fb = f32buf[:, 0:w]
            c1 = arf[:, 0:w]
            qb = arb[:, R3:R3 + w]
            ctx.wait("sync", "sV")
            ctx.wait("sync", "sSp")
            ctx.dma("sync", fb, src[128 * c:128 * (c + 1), :])
            ctx.wait("vector", "sD")
            ctx.op("vector", "tensor_scalar", out=c1, in0=fb, scalar1=thr,
                   scalar2=None, op0=A.is_ge)
            ctx.op("vector", "scalar_tensor_tensor", out=qb, in0=fb, scalar=nthr,
                   in1=c1, op0=A.is_le, op1=A.subtract, inc="sV")
            ctx.wait("sync", "sV")
            ctx.dma("sync", p[outnm][128 * c:128 * (c + 1), :], qb, inc="sSp")
    ctx.tap("wstat", tiny[:, 64:80])
    ctx.tap("wgq", p["wgq"])
    ctx.tap("wdq", p["wdq"])



def _ln_fm(ctx, st, src_dram, base, quant, xn_tok_dram=None, sseq=0):
    arb, arf, tiny, ps, identf = (st["arb"], st["arf"], st["tiny"], st["ps"],
                                  st["identf"])
    xb = [arf[:, 0:1024], arf[:, 1024:2048]]
    trash = arf[:, 2048:3072]
    m_all = tiny[:, 0:16]
    ssq_all = tiny[:, 16:32]
    rstd = tiny[:, 32:48]

    for t in range(NT):
        b = xb[t % 2]
        ctx.wait("sync", "sC")
        ctx.dma("sync", b, src_dram[base + 128 * t:base + 128 * (t + 1), :])
        ctx.wait("scalar", "sD")
        ctx.op("scalar", "activation", out=trash, in_=b, func=AF.Copy,
               accum_out=m_all[:, t:t + 1])
        ctx.op("scalar", "activation", out=trash, in_=b, func=AF.Square,
               accum_out=ssq_all[:, t:t + 1], inc="sC")
    ctx.wait("vector", "sC")
    ctx.op("vector", "tensor_scalar", out=m_all, in0=m_all, scalar1=1.0 / D,
           scalar2=None, op0=A.mult)
    ctx.op("vector", "tensor_scalar", out=ssq_all, in0=ssq_all, scalar1=1.0 / D,
           scalar2=None, op0=A.mult)
    ctx.fence("vector")
    v2 = tiny[:, 48:64]
    ctx.op("vector", "tensor_tensor", out=v2, in0=m_all, in1=m_all, op=A.mult)
    ctx.fence("vector")
    ctx.op("vector", "tensor_tensor", out=v2, in0=ssq_all, in1=v2,
           op=A.subtract, inc="sV")
    ctx.wait("scalar", "sV")
    ctx.op("scalar", "activation", out=v2, in_=v2, func=AF.Sqrt, bias=1e-5,
           scale=1.0, inc="sC")
    ctx.wait("vector", "sC")
    ctx.op("vector", "reciprocal", out=rstd, in_=v2, inc="sV")
    ctx.fence("vector")

    xnt = arf[:, 2048:3072]
    for t in range(NT):
        b = xb[t % 2]
        ctx.wait("sync", "sV")
        ctx.dma("sync", b, src_dram[base + 128 * t:base + 128 * (t + 1), :])
        ctx.wait("vector", "sD")
        ctx.wait("vector", "sSp")
        ctx.op("vector", "tensor_scalar", out=xnt, in0=b,
               scalar1=m_all[:, t:t + 1], scalar2=rstd[:, t:t + 1],
               op0=A.subtract, op1=A.mult, inc="sV")
        if xn_tok_dram is not None:
            ctx.wait("sync", "sV")
            ctx.dma("sync", xn_tok_dram[128 * t:128 * (t + 1), :], xnt,
                    inc="sSp")
        ctx.wait("tensor", "sV")
        for c in range(DC):
            bank = (c // 4) % 2
            off = 512 * bank + 128 * (c % 4)
            ctx.op("tensor", "transpose", out=ps[:, off:off + 128],
                   in_=xnt[:, 128 * c:128 * (c + 1)], identity=identf[:],
                   inc="sT" if c == DC - 1 else None)
        ctx.wait("vector", "sT")
        for c in range(DC):
            bank = (c // 4) % 2
            off = 512 * bank + 128 * (c % 4)
            a0 = R0 + 2048 * c + 128 * t
            ctx.op("vector", "tensor_copy", out=arb[:, a0:a0 + 128],
                   in_=ps[:, off:off + 128])
        for c in range(DC):
            bank = (c // 4) % 2
            off = 512 * bank + 128 * (c % 4)
            a0 = R0 + 2048 * c + 128 * t
            a1 = R1 + 2048 * c + 128 * t
            ctx.op("vector", "tensor_tensor", out=arb[:, a1:a1 + 128],
                   in0=ps[:, off:off + 128], in1=arb[:, a0:a0 + 128],
                   op=A.subtract)
        ctx.op("vector", "nop", inc="sV")
        if quant:
            _quant_tile(ctx, st, xnt, t)
    ctx.op("vector", "nop", inc="sV")
    ctx.op("scalar", "nop", inc="sC")


def _quant_tile(ctx, st, xnt, t):
    arb, arf, tiny, ps, identb, sc = (st["arb"], st["arf"], st["tiny"],
                                      st["ps"], st["identb"], st["sc"])
    am = tiny[:, 80:81]
    sq = tiny[:, 81:82]
    inv = tiny[:, 82:83]
    ctx.op("vector", "reduce_max", out=am, in_=xnt, axis=AX,
           apply_absolute_value=True)
    ctx.fence("vector")
    ctx.op("vector", "tensor_scalar", out=am, in0=am, scalar1=1e-5,
           scalar2=None, op0=A.max)
    ctx.fence("vector")
    ctx.op("vector", "reciprocal", out=sq, in_=am)
    ctx.fence("vector")
    ctx.op("vector", "tensor_scalar", out=sq, in0=sq, scalar1=127.0,
           scalar2=None, op0=A.mult)
    ctx.fence("vector")
    ctx.op("vector", "reciprocal", out=inv, in_=sq)
    ctx.fence("vector")
    ctx.op("vector", "tensor_copy", out=tiny[:, 84 + t:85 + t], in_=inv)
    v = arf[:, 3072:4096]
    qi = arf[:, 4096:5120].bitcast(I32)
    qm1 = arf[:, 5120:6144]
    dlt = arf[:, 6144:7168]
    ctx.op("vector", "tensor_scalar", out=v, in0=xnt, scalar1=sq, scalar2=None,
           op0=A.mult)
    ctx.op("vector", "tensor_copy", out=qi, in_=v, inc="sV")
    ctx.wait("scalar", "sV")
    ctx.op("scalar", "activation", out=qm1, in_=qi, func=AF.Copy, bias=-1.0,
           scale=1.0, inc="sC")
    ctx.wait("vector", "sC")
    ctx.op("vector", "tensor_tensor", out=dlt, in0=v, in1=qm1, op=A.subtract)
    ctx.op("vector", "scalar_tensor_tensor", out=v, in0=dlt, scalar=1.5,
           in1=qm1, op0=A.is_ge, op1=A.add)
    xq = arb[:, R4 + 2048:R4 + 3072]
    ctx.op("vector", "scalar_tensor_tensor", out=xq, in0=dlt, scalar=0.5,
           in1=v, op0=A.is_ge, op1=A.add, inc="sV")
    ctx.wait("tensor", "sV")
    for c in range(DC):
        bank = 4 + (c // 4) % 2
        off = 512 * bank + 64 * (c % 4)
        ctx.op("tensor", "transpose", out=ps[:, off:off + 64].bitcast(BF16),
               in_=xq[:, 128 * c:128 * (c + 1)], identity=identb[:],
               inc="sT" if c == DC - 1 else None)
    ctx.wait("vector", "sT")
    ctx.wait("vector", "sSp")
    stg = arb[:, R4 + 3072:R4 + 4096]
    for c in range(DC):
        bank = 4 + (c // 4) % 2
        off = 512 * bank + 64 * (c % 4)
        ctx.op("vector", "tensor_copy", out=stg[:, 128 * c:128 * (c + 1)],
               in_=ps[:, off:off + 64].bitcast(BF16),
               inc="sV" if c == DC - 1 else None)
    ctx.wait("sync", "sV")
    ctx.dma("sync", sc["xnq"][t].rearrange("c p n -> p c n"),
            stg.rearrange("p (c n) -> p c n", c=DC), inc="sSp")


def phase_A(ctx, st, s):
    _ln_fm(ctx, st, st["x_in"], s * S, quant=True)
    ctx.wait("sync", "sV")
    ctx.dma("sync", st["sc"]["absr"], st["tiny"][:, 84:84 + NT], inc="sSp")
    ctx.tap("stats", st["tiny"][:, 0:64])
    ctx.tap("xnfm_hi", st["arb"][:, R0:R0 + RW])
    ctx.tap("xnfm_lo", st["arb"][:, R1:R1 + RW])
    ctx.tap("xnq", st["sc"]["xnq"].rearrange("t c p n -> (t c p) n"))
    ctx.tap("absr", st["sc"]["absr"])



def phase_C(ctx, st, s):
    arb, arf, ps, sc, p = st["arb"], st["arf"], st["ps"], st["sc"], st["p"]
    wbuf_h = arb[:, R4:R4 + 8192]
    wbuf_l = arb[:, R4 + 8192:R4 + 16384]
    ev_h = arf[:, 0:256].bitcast(BF16)
    ev_l = arf[:, 256:512].bitcast(BF16)

    stop = st.get("stop") or ""
    mats = [("wq", "q"), ("wk", "k"), ("wv", "v")]
    if stop == "Cq":
        mats = mats[:1]
    elif stop == "Ck":
        mats = mats[:2]
    for nm, kind in mats:
        ctx.wait("sync", "sT")
        ctx.dma("sync", wbuf_h[:].rearrange("p (c n) -> p c n", c=DC),
                p[nm + "T_h"].rearrange("(c p) n -> p c n", p=128))
        ctx.dma("sync", wbuf_l[:].rearrange("p (c n) -> p c n", c=DC),
                p[nm + "T_l"].rearrange("(c p) n -> p c n", p=128))
        ctx.wait("tensor", "sD")

        if kind in ("q", "k"):
            for m in range(DC):
                for n in range(4):
                    bank = (m * 4 + n) % 4
                    ob = ps[:, 512 * bank:512 * (bank + 1)]
                    ctx.wait("tensor", "sV")
                    for c in range(DC):
                        _mm3_acc(ctx,
                                 wbuf_h[:, 1024 * c + 128 * m:1024 * c + 128 * m + 128],
                                 wbuf_l[:, 1024 * c + 128 * m:1024 * c + 128 * m + 128],
                                 arb[:, R0 + 2048 * c + 512 * n:R0 + 2048 * c + 512 * (n + 1)],
                                 arb[:, R1 + 2048 * c + 512 * n:R1 + 2048 * c + 512 * (n + 1)],
                                 ob, first=(c == 0), last=(c == DC - 1))
                    ctx.op("tensor", "nop", inc="sT")
                    ctx.wait("vector", "sT")
                    if kind == "k":
                        a2 = R2 + 2048 * m + 512 * n
                        a3 = R3 + 2048 * m + 512 * n
                        _hl(ctx, arb[:, a2:a2 + 512], arb[:, a3:a3 + 512], ob)
                        ctx.op("vector", "nop", inc="sV")
                    else:
                        ctx.wait("vector", "sSp")
                        _hl(ctx, ev_h[:, 0:512], ev_l[:, 0:512], ob)
                        ctx.op("vector", "nop", inc="sV")
                        ctx.wait("sync", "sV")
                        ctx.dma("sync", sc["q0_h"][128 * m:128 * (m + 1), 512 * n:512 * (n + 1)], ev_h[:, 0:512], inc="sSp")
                        ctx.dma("sync", sc["q0_l"][128 * m:128 * (m + 1), 512 * n:512 * (n + 1)], ev_l[:, 0:512], inc="sSp")
        else:
            for t in range(NT):
                for b in range(2):
                    bank = (t * 2 + b) % 4
                    ob = ps[:, 512 * bank:512 * (bank + 1)]
                    ctx.wait("tensor", "sV")
                    for c in range(DC):
                        xh = arb[:, R0 + 2048 * c + 128 * t:R0 + 2048 * c + 128 * t + 128]
                        xl = arb[:, R1 + 2048 * c + 128 * t:R1 + 2048 * c + 128 * t + 128]
                        wh = wbuf_h[:, 1024 * c + 512 * b:1024 * c + 512 * (b + 1)]
                        wl = wbuf_l[:, 1024 * c + 512 * b:1024 * c + 512 * (b + 1)]
                        _mm3_acc(ctx, xh, xl, wh, wl, ob,
                                 first=(c == 0), last=(c == DC - 1))
                    ctx.op("tensor", "nop", inc="sT")
                    ctx.wait("vector", "sT")
                    ctx.wait("vector", "sSp")
                    _hl(ctx, ev_h[:, 0:512], ev_l[:, 0:512], ob)
                    ctx.op("vector", "nop", inc="sV")
                    ctx.wait("sync", "sV")
                    ctx.dma("sync", sc["v0_h"][128 * t:128 * (t + 1), 512 * b:512 * (b + 1)], ev_h[:, 0:512], inc="sSp")
                    ctx.dma("sync", sc["v0_l"][128 * t:128 * (t + 1), 512 * b:512 * (b + 1)], ev_l[:, 0:512], inc="sSp")
    ctx.tap("kfm_hi", arb[:, R2:R2 + RW])
    ctx.tap("kfm_lo", arb[:, R3:R3 + RW])
    ctx.tap("q0_h", sc["q0_h"])
    ctx.tap("q0_l", sc["q0_l"])
    ctx.tap("v0_h", sc["v0_h"])
    ctx.tap("v0_l", sc["v0_l"])



def phase_B(ctx, st, s):
    arb, arf, tiny, ps, sc, p, identb = (st["arb"], st["arf"], st["tiny"],
                                         st["ps"], st["sc"], st["p"],
                                         st["identb"])
    ctx.dma("sync", arb[:, R0:R0 + RW].rearrange("p (t c n) -> p t c n",
                                                 t=NT, c=DC),
            sc["xnq"].rearrange("t c p n -> p t c n"))
    ctx.dma("sync", tiny[:, 84:84 + NT], sc["absr"])
    ctx.wait("vector", "sD")
    for t in range(NT):
        for j, slot in ((0, 68), (1, 69)):
            ctx.op("vector", "scalar_tensor_tensor",
                   out=tiny[:, 16 * j + t:16 * j + t + 1],
                   in0=tiny[:, slot:slot + 1], scalar=-1.0,
                   in1=tiny[:, 84 + t:85 + t], op0=A.mult, op1=A.mult)
    ctx.op("vector", "nop", inc="sV")
    ctx.wait("tensor", "sD")
    ctx.wait("scalar", "sV")

    wg = arb[:, R1:R1 + 4096]
    wu = arb[:, R1 + 4096:R1 + 8192]
    gsil = arf[:, 0:512]
    mid = arf[:, 512:1024]

    for g in range(8):
        ctx.wait("sync", "sT")
        ctx.dma("sync", wg[:].rearrange("p (c n) -> p c n", c=DC),
                p["wgq"].rearrange("(c p) n -> p c n", p=128)[:, :, 512 * g:512 * (g + 1)])
        ctx.dma("sync", wu[:].rearrange("p (c n) -> p c n", c=DC),
                p["wuq"].rearrange("(c p) n -> p c n", p=128)[:, :, 512 * g:512 * (g + 1)])
        ctx.wait("tensor", "sD")
        for t in range(NT):
            bg = (2 * t) % 8
            bu = (2 * t + 1) % 8
            obg = ps[:, 512 * bg:512 * (bg + 1)]
            obu = ps[:, 512 * bu:512 * (bu + 1)]
            ctx.wait("tensor", "sV")
            for c in range(DC):
                xq = arb[:, R0 + 1024 * t + 128 * c:R0 + 1024 * t + 128 * c + 128]
                ctx.op("tensor", "matmul", out=obg, lhsT=xq,
                       rhs=wg[:, 512 * c:512 * (c + 1)],
                       start=(c == 0), stop=(c == DC - 1))
                ctx.op("tensor", "matmul", out=obu, lhsT=xq,
                       rhs=wu[:, 512 * c:512 * (c + 1)],
                       start=(c == 0), stop=(c == DC - 1))
            ctx.op("tensor", "nop", inc="sT")
            ctx.wait("scalar", "sT")
            ctx.wait("scalar", "sV")
            ctx.op("scalar", "activation", out=gsil, in_=obg, func=AF.Silu,
                   bias=0.0, scale=tiny[:, t:t + 1], inc="sC")
            ctx.wait("vector", "sC")
            ctx.wait("vector", "sSp")
            ctx.op("vector", "scalar_tensor_tensor", out=mid, in0=obu,
                   scalar=tiny[:, 16 + t:17 + t], in1=gsil,
                   op0=A.mult, op1=A.mult, inc="sV")
            ctx.wait("sync", "sV")
            ctx.dma("sync", sc["mid"][128 * t:128 * (t + 1), 512 * g:512 * (g + 1)],
                    mid, inc="sSp")
    ctx.tap("mid", sc["mid"])
    ctx.barrier()

    wd_a = arb[:, R1:R1 + RW]
    wd_b = arb[:, R4:R4 + RW]
    ctx.dma("sync", wd_a[:].rearrange("p (c n) -> p c n", c=16),
            p["wdq"].rearrange("(c p) n -> p c n", p=128)[:, 0:16, :])
    ctx.dma("sync", wd_b[:].rearrange("p (c n) -> p c n", c=16),
            p["wdq"].rearrange("(c p) n -> p c n", p=128)[:, 16:32, :])
    ctx.wait("tensor", "sD")

    xq_stg = arb[:, R0:R0 + 4096]
    mfm = arb[:, R0 + 4096:R0 + 8192]
    midb = arf[:, 0:4096]
    vv = arf[:, 4096:8192]
    for t in range(NT):
        ctx.wait("sync", "sV")
        ctx.dma("sync", midb, sc["mid"][128 * t:128 * (t + 1), :])
        ctx.wait("vector", "sD")
        am = tiny[:, 80:81]
        sq = tiny[:, 81:82]
        inv = tiny[:, 82:83]
        combd = tiny[:, 83:84]
        ctx.op("vector", "reduce_max", out=am, in_=midb, axis=AX,
               apply_absolute_value=True)
        ctx.fence("vector")
        ctx.op("vector", "tensor_scalar", out=am, in0=am, scalar1=1e-5,
               scalar2=None, op0=A.max)
        ctx.fence("vector")
        ctx.op("vector", "reciprocal", out=sq, in_=am)
        ctx.fence("vector")
        ctx.op("vector", "tensor_scalar", out=sq, in0=sq, scalar1=127.0,
               scalar2=None, op0=A.mult)
        ctx.fence("vector")
        ctx.op("vector", "reciprocal", out=inv, in_=sq)
        ctx.fence("vector")
        ctx.op("vector", "scalar_tensor_tensor", out=combd, in0=tiny[:, 70:71],
               scalar=-1.0, in1=inv, op0=A.mult, op1=A.mult)
        ctx.op("vector", "tensor_scalar", out=vv, in0=midb, scalar1=sq,
               scalar2=None, op0=A.mult)
        qi = midb.bitcast(I32)
        ctx.op("vector", "tensor_copy", out=qi, in_=vv, inc="sV")
        ctx.wait("scalar", "sSp")
        for q in range(4):
            vs = vv[:, 1024 * q:1024 * (q + 1)]
            qs = qi[:, 1024 * q:1024 * (q + 1)]
            qm1 = arf[:, 8192:9216]
            ctx.wait("scalar", "sV")
            ctx.op("scalar", "activation", out=qm1, in_=qs, func=AF.Copy,
                   bias=-1.0, scale=1.0, inc="sC")
            ctx.wait("vector", "sC")
            dlt = qs.bitcast(F32)
            ctx.op("vector", "tensor_tensor", out=dlt, in0=vs, in1=qm1,
                   op=A.subtract)
            ctx.op("vector", "scalar_tensor_tensor", out=vs, in0=dlt,
                   scalar=1.5, in1=qm1, op0=A.is_ge, op1=A.add)
            ctx.op("vector", "scalar_tensor_tensor",
                   out=xq_stg[:, 1024 * q:1024 * (q + 1)],
                   in0=dlt, scalar=0.5, in1=vs, op0=A.is_ge, op1=A.add,
                   inc="sV")
        ctx.wait("tensor", "sV")
        for c in range(IC):
            bank = (c // 8) % 2
            off = 512 * bank + 64 * (c % 8)
            ctx.op("tensor", "transpose",
                   out=ps[:, off:off + 64].bitcast(BF16),
                   in_=xq_stg[:, 128 * c:128 * (c + 1)], identity=identb[:],
                   inc="sT" if (c % 8 == 7) else None)
            if c % 8 == 7:
                ctx.wait("vector", "sT")
                for cc in range(c - 7, c + 1):
                    off2 = 512 * ((cc // 8) % 2) + 64 * (cc % 8)
                    ctx.op("vector", "tensor_copy",
                           out=mfm[:, 128 * cc:128 * (cc + 1)],
                           in_=ps[:, off2:off2 + 64].bitcast(BF16),
                           inc="sV" if cc == c else None)
                ctx.wait("tensor", "sV")
        ob0 = ps[:, 2048:2560]
        ob1 = ps[:, 2560:3072]
        for c in range(IC):
            lt = mfm[:, 128 * c:128 * (c + 1)]
            wslab = wd_a if c < 16 else wd_b
            cc = c % 16
            ctx.op("tensor", "matmul", out=ob0, lhsT=lt,
                   rhs=wslab[:, 1024 * cc:1024 * cc + 512],
                   start=(c == 0), stop=(c == IC - 1))
            ctx.op("tensor", "matmul", out=ob1, lhsT=lt,
                   rhs=wslab[:, 1024 * cc + 512:1024 * cc + 1024],
                   start=(c == 0), stop=(c == IC - 1))
        ctx.op("tensor", "nop", inc="sT")
        ctx.wait("scalar", "sT")
        ctx.wait("scalar", "sSp")
        g0t = arf[:, 8192:9216]
        ctx.op("scalar", "activation", out=g0t[:, 0:512], in_=ob0,
               func=AF.Silu, bias=0.0, scale=combd)
        ctx.op("scalar", "activation", out=g0t[:, 512:1024], in_=ob1,
               func=AF.Silu, bias=0.0, scale=combd, inc="sC")
        ctx.wait("sync", "sC")
        ctx.dma("sync", sc["g0"][128 * t:128 * (t + 1), :], g0t, inc="sSp")
        ctx.op("vector", "nop", inc="sV")
    ctx.tap("g0", sc["g0"])



def phase_D(ctx, st, s):
    arb, arf, tiny, ps, sc, masks, identb = (st["arb"], st["arf"], st["tiny"],
                                             st["ps"], st["sc"], st["masks"],
                                             st["identb"])
    base = s * S
    ctx.dma("sync", arb[:, R0:R0 + RW].rearrange("p (t n) -> p t n", t=NT),
            sc["v0_h"].rearrange("(t p) n -> p t n", p=128))
    ctx.dma("sync", arb[:, R1:R1 + RW].rearrange("p (t n) -> p t n", t=NT),
            sc["v0_l"].rearrange("(t p) n -> p t n", p=128))
    vload = ctx.val("sD")
    qb_h = [arb[:, R4:R4 + 1024], arb[:, R4 + 1024:R4 + 2048]]
    qb_l = [arb[:, R4 + 2048:R4 + 3072], arb[:, R4 + 3072:R4 + 4096]]
    pt_h = arb[:, R4 + 4096:R4 + 6144]
    pt_l = arb[:, R4 + 6144:R4 + 8192]
    ph = arb[:, R4 + 8192:R4 + 10240]
    pl = arb[:, R4 + 10240:R4 + 12288]
    srow = arf[:, 0:2048]
    prow = arf[:, 2048:4096]
    xg = [arf[:, 4096:5120], arf[:, 5120:6144]]
    x1t = arf[:, 6144:7168]

    for t in range(NT):
        qb, ql = qb_h[t % 2], qb_l[t % 2]
        ctx.wait("sync", "sT")
        ctx.dma("sync", qb[:].rearrange("p (c n) -> p c n", c=DC),
                sc["q0_h"].rearrange("(c p) n -> p c n", p=128)[:, :, 128 * t:128 * (t + 1)])
        ctx.dma("sync", ql[:].rearrange("p (c n) -> p c n", c=DC),
                sc["q0_l"].rearrange("(c p) n -> p c n", p=128)[:, :, 128 * t:128 * (t + 1)])
        ctx.wait("sync", "sV")
        ctx.dma("sync", xg[0], st["x_in"][base + 128 * t:base + 128 * (t + 1), :])
        ctx.dma("sync", xg[1], sc["g0"][128 * t:128 * (t + 1), :])
        qv = ctx.val("sD")
        ngrp = (t + 4) // 4
        ctx.wait("tensor", "sD")
        for gi in range(ngrp):
            bank = gi % 2
            ob = ps[:, 512 * bank:512 * (bank + 1)]
            ctx.wait("tensor", "sV")
            for c in range(DC):
                _mm3_acc(ctx,
                         qb[:, 128 * c:128 * (c + 1)], ql[:, 128 * c:128 * (c + 1)],
                         arb[:, R2 + 2048 * c + 512 * gi:R2 + 2048 * c + 512 * (gi + 1)],
                         arb[:, R3 + 2048 * c + 512 * gi:R3 + 2048 * c + 512 * (gi + 1)],
                         ob, first=(c == 0), last=(c == DC - 1))
            ctx.op("tensor", "nop", inc="sT")
            ctx.wait("vector", "sT")
            if gi == ngrp - 1:
                mi = t % 4
                ctx.op("vector", "scalar_tensor_tensor",
                       out=srow[:, 512 * gi:512 * (gi + 1)], in0=ob,
                       scalar=SCALE0, in1=masks[:, 512 * mi:512 * (mi + 1)],
                       op0=A.mult, op1=A.add, inc="sV")
            else:
                ctx.op("vector", "tensor_scalar",
                       out=srow[:, 512 * gi:512 * (gi + 1)], in0=ob,
                       scalar1=SCALE0, scalar2=None, op0=A.mult, inc="sV")
        keylen = 512 * ngrp
        nk = t + 1
        ctx.op("vector", "reduce_max", out=tiny[:, 80:81],
               in_=srow[:, 0:keylen], axis=AX)
        ctx.fence("vector")
        ctx.op("vector", "tensor_scalar", out=tiny[:, 81:82],
               in0=tiny[:, 80:81], scalar1=-1.0, scalar2=None, op0=A.mult,
               inc="sV")
        ctx.wait("scalar", "sV")
        ctx.op("scalar", "activation", out=prow[:, 0:keylen],
               in_=srow[:, 0:keylen], func=AF.Exp, bias=tiny[:, 81:82],
               scale=1.0, accum_out=tiny[:, 82:83], inc="sC")
        ctx.wait("vector", "sC")
        ctx.op("vector", "reciprocal", out=tiny[:, 83:84], in_=tiny[:, 82:83])
        ctx.fence("vector")
        _hl(ctx, ph[:, 0:keylen], pl[:, 0:keylen], prow[:, 0:keylen])
        ctx.op("vector", "nop", inc="sV")
        ctx.wait("tensor", "sV")
        for j in range(nk):
            bank = 2 + (j % 2)
            o1 = 512 * bank
            ctx.op("tensor", "transpose", out=ps[:, o1:o1 + 64].bitcast(BF16),
                   in_=ph[:, 128 * j:128 * (j + 1)], identity=identb[:])
            ctx.op("tensor", "transpose",
                   out=ps[:, o1 + 64:o1 + 128].bitcast(BF16),
                   in_=pl[:, 128 * j:128 * (j + 1)], identity=identb[:],
                   inc="sT")
            ctx.wait("vector", "sT")
            ctx.op("vector", "tensor_copy", out=pt_h[:, 128 * j:128 * (j + 1)],
                   in_=ps[:, o1:o1 + 64].bitcast(BF16))
            ctx.op("vector", "tensor_copy", out=pt_l[:, 128 * j:128 * (j + 1)],
                   in_=ps[:, o1 + 64:o1 + 128].bitcast(BF16), inc="sV")
            ctx.wait("tensor", "sV")
        ctx.wait("tensor", "sD")
        for b in range(2):
            ob = ps[:, 2048 + 512 * b:2048 + 512 * (b + 1)]
            for j in range(nk):
                _mm3_acc(ctx,
                         pt_h[:, 128 * j:128 * (j + 1)],
                         pt_l[:, 128 * j:128 * (j + 1)],
                         arb[:, R0 + 1024 * j + 512 * b:R0 + 1024 * j + 512 * (b + 1)],
                         arb[:, R1 + 1024 * j + 512 * b:R1 + 1024 * j + 512 * (b + 1)],
                         ob, first=(j == 0), last=(j == nk - 1))
        ctx.op("tensor", "nop", inc="sT")
        ctx.wait("vector", "sT")
        ctx.wait("vector", "sD")
        ctx.wait("vector", "sSp")
        for b in range(2):
            ob = ps[:, 2048 + 512 * b:2048 + 512 * (b + 1)]
            sl = slice(512 * b, 512 * (b + 1))
            ctx.op("vector", "tensor_scalar", out=x1t[:, sl], in0=ob,
                   scalar1=tiny[:, 83:84], scalar2=None, op0=A.mult)
            ctx.op("vector", "tensor_tensor", out=x1t[:, sl], in0=x1t[:, sl],
                   in1=xg[1][:, sl], op=A.mult)
            ctx.op("vector", "tensor_tensor", out=x1t[:, sl], in0=x1t[:, sl],
                   in1=xg[0][:, sl], op=A.add)
        ctx.op("vector", "nop", inc="sV")
        ctx.wait("sync", "sV")
        ctx.dma("sync", sc["x1"][128 * t:128 * (t + 1), :], x1t, inc="sSp")
    ctx.tap("x1", sc["x1"])



def phase_E(ctx, st, s):
    arb, arf, tiny, ps, sc, p, onesf, trow = (st["arb"], st["arf"], st["tiny"],
                                              st["ps"], st["sc"], st["p"],
                                              st["onesf"], st["trow"])
    _ln_fm(ctx, st, sc["x1"], 0, quant=False, xn_tok_dram=sc["xn1"])
    ctx.barrier()

    wrh = arb[:, R4:R4 + 8].rearrange("p (c n) -> p c n", c=DC)
    wrl = arb[:, R4 + 8:R4 + 16].rearrange("p (c n) -> p c n", c=DC)
    ctx.dma("sync", wrh, p["wrT_h"].rearrange("(c p) n -> p c n", p=128),
            allow_slow_non_contiguous=True)
    ctx.dma("sync", wrl, p["wrT_l"].rearrange("(c p) n -> p c n", p=128),
            allow_slow_non_contiguous=True)
    ctx.wait("tensor", "sD")
    for n in range(4):
        ob = ps[0:1, 512 * n:512 * (n + 1)]
        for c in range(DC):
            _mm3_acc(ctx, wrh[:, c, :], wrl[:, c, :],
                     arb[:, R0 + 2048 * c + 512 * n:R0 + 2048 * c + 512 * (n + 1)],
                     arb[:, R1 + 2048 * c + 512 * n:R1 + 2048 * c + 512 * (n + 1)],
                     ob, first=(c == 0), last=(c == DC - 1))
    ctx.op("tensor", "nop", inc="sT")
    ctx.wait("vector", "sT")
    srw = trow[0:1, 0:S]
    ctx.op("vector", "tensor_copy", out=srw, in_=ps[0:1, 0:2048], inc="sV")
    ctx.tap("srow", sc["srow"])

    lo = trow[0:1, S:S + 1]
    hi = trow[0:1, S + 1:S + 2]
    mid = trow[0:1, S + 2:S + 3]
    cnt = trow[0:1, S + 3:S + 4]
    pred = trow[0:1, S + 4:S + 5]
    tmp = trow[0:1, S + 5:S + 6]
    cmpb = trow[0:1, S + 8:S + 8 + S]
    tmp2 = trow[0:1, S + 6:S + 7]
    ctx.op("vector", "reduce_max", out=hi, in_=srw, axis=AX,
           apply_absolute_value=True)
    ctx.fence("vector")
    ctx.op("vector", "tensor_scalar", out=hi, in0=hi, scalar1=1.0,
           scalar2=None, op0=A.add)
    ctx.fence("vector")
    ctx.op("vector", "tensor_scalar", out=lo, in0=hi, scalar1=-1.0,
           scalar2=None, op0=A.mult)
    ctx.fence("vector")
    ctx.op("vector", "tensor_tensor", out=mid, in0=lo, in1=hi, op=A.add)
    ctx.fence("vector")
    ctx.op("vector", "tensor_scalar", out=mid, in0=mid, scalar1=0.5,
           scalar2=None, op0=A.mult)
    ctx.fence("vector")
    K = S // 2
    for _ in range(44):
        ctx.op("vector", "tensor_scalar", out=cmpb, in0=srw, scalar1=mid,
               scalar2=None, op0=A.is_ge)
        ctx.op("vector", "reduce_sum", out=cnt, in_=cmpb, axis=AX)
        ctx.op("vector", "tensor_tensor", out=tmp, in0=mid, in1=lo,
               op=A.subtract)
        ctx.op("vector", "tensor_tensor", out=tmp2, in0=hi, in1=mid,
               op=A.subtract)
        ctx.fence("vector")
        ctx.op("vector", "scalar_tensor_tensor", out=pred, in0=cnt,
               scalar=float(K), in1=tmp, op0=A.is_ge, op1=A.mult)
        ctx.op("vector", "scalar_tensor_tensor", out=tmp2, in0=cnt,
               scalar=float(K), in1=tmp2, op0=A.is_ge, op1=A.mult)
        ctx.fence("vector")
        ctx.op("vector", "tensor_tensor", out=lo, in0=lo, in1=pred, op=A.add)
        ctx.op("vector", "tensor_tensor", out=hi, in0=mid, in1=tmp2, op=A.add)
        ctx.fence("vector")
        ctx.op("vector", "tensor_tensor", out=mid, in0=lo, in1=hi, op=A.add)
        ctx.fence("vector")
        ctx.op("vector", "tensor_scalar", out=mid, in0=mid, scalar1=0.5,
               scalar2=None, op0=A.mult)
        ctx.fence("vector")
    ctx.op("vector", "nop", inc="sV")
    ctx.wait("tensor", "sV")
    ctx.op("tensor", "matmul", out=ps[:, 3072:3073], lhsT=onesf[:], rhs=lo,
           start=True, stop=True, inc="sT")
    ctx.wait("sync", "sV")
    ctx.dma("sync", sc["srow"].rearrange("(o t) p -> o t p", o=1),
            srw.rearrange("o (t p) -> o t p", p=128), inc="sSp")
    ctx.wait("sync", "sSp")
    ctx.dma("sync", arf[0:16, 3072:3200], sc["srow"])
    sv = ctx.val("sD")
    ctx.wait("tensor", "sD")
    ctx.op("tensor", "transpose", out=ps[:, 1536:1552],
           in_=arf[0:16, 3072:3200], identity=st["identf"][0:16, 0:16],
           inc="sT")
    ctx.wait("vector", "sT")
    ctx.op("vector", "tensor_copy", out=tiny[:, 16:16 + NT],
           in_=ps[:, 1536:1552])
    ctx.fence("vector")
    thr = tiny[:, 32:33]
    ctx.op("vector", "tensor_copy", out=thr, in_=ps[:, 3072:3073], inc="sV")
    ctx.fence("vector")
    ctx.wait("scalar", "sV")
    sig = tiny[:, 48:48 + NT]
    ctx.op("scalar", "activation", out=sig, in_=tiny[:, 16:16 + NT],
           func=AF.Sigmoid, inc="sC")
    ctx.wait("vector", "sC")
    msk = tiny[:, 0:NT]
    ctx.op("vector", "tensor_scalar", out=msk, in0=tiny[:, 16:16 + NT],
           scalar1=thr, scalar2=None, op0=A.is_ge)
    ctx.fence("vector")
    ctx.op("vector", "tensor_tensor", out=msk, in0=msk, in1=sig, op=A.mult,
           inc="sV")
    ctx.tap("wtok", msk)



def phase_F(ctx, st, s):
    arb, arf, tiny, ps, sc, p, identf = (st["arb"], st["arf"], st["tiny"],
                                         st["ps"], st["sc"], st["p"],
                                         st["identf"])
    cin = st["cin"]
    gl = arf[:, 0:512]
    upd = arf[:, 512:1024]
    updtok = arf[:, 1024:5120]
    x1g = arf[:, 5120:6144]
    b1fm = arf[:, 6144:6176]
    b2fm = arf[:, 6176:6184]
    g1t = arf[:, 6208:7232]
    g1o = arf[:, 7232:8256]
    ctx.dma("sync", b1fm, cin["b1fm"])
    ctx.dma("sync", b2fm, cin["b2fm"])
    w1h = [arb[:, R4:R4 + 1024], arb[:, R4 + 1024:R4 + 2048]]
    w1l = [arb[:, R4 + 2048:R4 + 3072], arb[:, R4 + 3072:R4 + 4096]]
    w2h = arb[:, R4 + 4096:R4 + 8192]
    w2l = arb[:, R4 + 8192:R4 + 12288]
    hfm_h = arb[:, R2:R2 + RW]
    hfm_l = arb[:, R3:R3 + RW]
    ctx.wait("tensor", "sD")
    ctx.wait("scalar", "sD")

    for grp in range(4):
        for fc in range(IC):
            pb = fc % 2
            ctx.wait("sync", "sT")
            ctx.dma("sync", w1h[pb][:].rearrange("p (c n) -> p c n", c=DC),
                    p["W1_h"].rearrange("(c p) n -> p c n", p=128)[:, :, 128 * fc:128 * (fc + 1)])
            ctx.dma("sync", w1l[pb][:].rearrange("p (c n) -> p c n", c=DC),
                    p["W1_l"].rearrange("(c p) n -> p c n", p=128)[:, :, 128 * fc:128 * (fc + 1)])
            ctx.wait("tensor", "sD")
            bank = fc % 2
            ob = ps[:, 512 * bank:512 * (bank + 1)]
            ctx.wait("tensor", "sV")
            for c in range(DC):
                _mm3_acc(ctx,
                         w1h[pb][:, 128 * c:128 * (c + 1)],
                         w1l[pb][:, 128 * c:128 * (c + 1)],
                         arb[:, R0 + 2048 * c + 512 * grp:R0 + 2048 * c + 512 * (grp + 1)],
                         arb[:, R1 + 2048 * c + 512 * grp:R1 + 2048 * c + 512 * (grp + 1)],
                         ob, first=(c == 0), last=(c == DC - 1))
            ctx.op("tensor", "nop", inc="sT")
            ctx.wait("scalar", "sT")
            ctx.wait("scalar", "sV")
            ctx.op("scalar", "activation", out=gl, in_=ob,
                   func=AF.Gelu_apprx_tanh, bias=b1fm[:, fc:fc + 1],
                   scale=1.0, inc="sC")
            ctx.wait("vector", "sC")
            _hl(ctx, hfm_h[:, 512 * fc:512 * (fc + 1)],
                hfm_l[:, 512 * fc:512 * (fc + 1)], gl)
            ctx.op("vector", "nop", inc="sV")
        for dc in range(DC):
            ctx.wait("sync", "sT")
            ctx.dma("sync", w2h[:].rearrange("p (c n) -> p c n", c=IC),
                    p["W2_h"].rearrange("(c p) n -> p c n", p=128)[:, :, 128 * dc:128 * (dc + 1)])
            ctx.dma("sync", w2l[:].rearrange("p (c n) -> p c n", c=IC),
                    p["W2_l"].rearrange("(c p) n -> p c n", p=128)[:, :, 128 * dc:128 * (dc + 1)])
            ctx.wait("tensor", "sD")
            bank = 2 + dc % 2
            ob = ps[:, 512 * bank:512 * (bank + 1)]
            ctx.wait("tensor", "sC")
            ctx.wait("tensor", "sV")
            for c in range(IC):
                _mm3_acc(ctx,
                         w2h[:, 128 * c:128 * (c + 1)],
                         w2l[:, 128 * c:128 * (c + 1)],
                         hfm_h[:, 512 * c:512 * (c + 1)],
                         hfm_l[:, 512 * c:512 * (c + 1)],
                         ob, first=(c == 0), last=(c == IC - 1))
            ctx.op("tensor", "nop", inc="sT")
            ctx.wait("vector", "sT")
            ctx.op("vector", "tensor_scalar", out=upd, in0=ob,
                   scalar1=b2fm[:, dc:dc + 1], scalar2=None, op0=A.add,
                   inc="sC")
            ctx.wait("tensor", "sC")
            for j in range(4):
                bk = 4 + j % 2
                ctx.op("tensor", "transpose", out=ps[:, 512 * bk:512 * bk + 128],
                       in_=upd[:, 128 * j:128 * (j + 1)], identity=identf[:],
                       inc="sT")
                ctx.wait("vector", "sT")
                ctx.op("vector", "tensor_copy",
                       out=updtok[:, 1024 * j + 128 * dc:1024 * j + 128 * (dc + 1)],
                       in_=ps[:, 512 * bk:512 * bk + 128], inc="sV")
                ctx.wait("tensor", "sV")
        for j in range(4):
            t = 4 * grp + j
            ctx.wait("sync", "sV")
            ctx.dma("sync", x1g, sc["xn1"][128 * t:128 * (t + 1), :])
            ctx.wait("vector", "sD")
            ctx.op("vector", "scalar_tensor_tensor", out=g1t,
                   in0=updtok[:, 1024 * j:1024 * (j + 1)],
                   scalar=tiny[:, t:t + 1], in1=x1g, op0=A.mult, op1=A.add,
                   inc="sV")
            ctx.wait("scalar", "sV")
            ctx.wait("scalar", "sSp")
            ctx.op("scalar", "activation", out=g1o, in_=g1t, func=AF.Silu,
                   inc="sC")
            ctx.wait("sync", "sC")
            ctx.dma("sync", sc["g1"][128 * t:128 * (t + 1), :], g1o, inc="sSp")
    ctx.tap("g1", sc["g1"])



def phase_G(ctx, st, s):
    arb, arf, ps, sc, p = st["arb"], st["arf"], st["ps"], st["sc"], st["p"]
    ev_h = arf[:, 0:256].bitcast(BF16)
    ev_l = arf[:, 256:512].bitcast(BF16)
    wch = [arb[:, R4:R4 + 1024], arb[:, R4 + 1024:R4 + 2048]]
    wcl = [arb[:, R4 + 2048:R4 + 3072], arb[:, R4 + 3072:R4 + 4096]]

    for m in range(16):
        pb = m % 2
        ctx.wait("sync", "sT")
        ctx.dma("sync", wch[pb][:].rearrange("p (c n) -> p c n", c=DC),
                p["wqkvT_h"].rearrange("(c p) n -> p c n", p=128)[:, :, 128 * m:128 * (m + 1)])
        ctx.dma("sync", wcl[pb][:].rearrange("p (c n) -> p c n", c=DC),
                p["wqkvT_l"].rearrange("(c p) n -> p c n", p=128)[:, :, 128 * m:128 * (m + 1)])
        ctx.wait("tensor", "sD")
        dst_h = sc["q1_h"] if m < 8 else sc["k1_h"]
        dst_l = sc["q1_l"] if m < 8 else sc["k1_l"]
        mm = m % 8
        for n in range(4):
            bank = n % 4
            ob = ps[:, 512 * bank:512 * (bank + 1)]
            ctx.wait("tensor", "sV")
            for c in range(DC):
                _mm3_acc(ctx, wch[pb][:, 128 * c:128 * (c + 1)],
                         wcl[pb][:, 128 * c:128 * (c + 1)],
                         arb[:, R0 + 2048 * c + 512 * n:R0 + 2048 * c + 512 * (n + 1)],
                         arb[:, R1 + 2048 * c + 512 * n:R1 + 2048 * c + 512 * (n + 1)],
                         ob, first=(c == 0), last=(c == DC - 1))
            ctx.op("tensor", "nop", inc="sT")
            ctx.wait("vector", "sT")
            ctx.wait("vector", "sSp")
            _hl(ctx, ev_h[:, 0:512], ev_l[:, 0:512], ob)
            ctx.op("vector", "nop", inc="sV")
            ctx.wait("sync", "sV")
            ctx.dma("sync", dst_h[128 * mm:128 * (mm + 1), 512 * n:512 * (n + 1)],
                    ev_h[:, 0:512], inc="sSp")
            ctx.dma("sync", dst_l[128 * mm:128 * (mm + 1), 512 * n:512 * (n + 1)],
                    ev_l[:, 0:512], inc="sSp")
    vwh = arb[:, R4:R4 + 8192]
    vwl = arb[:, R4 + 8192:R4 + 16384]
    ctx.wait("sync", "sT")
    ctx.dma("sync", vwh[:].rearrange("p (c n) -> p c n", c=DC),
            p["wqkvT_h"].rearrange("(c p) n -> p c n", p=128)[:, :, 2048:3072])
    ctx.dma("sync", vwl[:].rearrange("p (c n) -> p c n", c=DC),
            p["wqkvT_l"].rearrange("(c p) n -> p c n", p=128)[:, :, 2048:3072])
    ctx.wait("tensor", "sD")
    for t in range(NT):
        for b in range(2):
            bank = (t * 2 + b) % 4
            ob = ps[:, 512 * bank:512 * (bank + 1)]
            ctx.wait("tensor", "sV")
            for c in range(DC):
                xh = arb[:, R0 + 2048 * c + 128 * t:R0 + 2048 * c + 128 * t + 128]
                xl = arb[:, R1 + 2048 * c + 128 * t:R1 + 2048 * c + 128 * t + 128]
                _mm3_acc(ctx, xh, xl,
                         vwh[:, 1024 * c + 512 * b:1024 * c + 512 * (b + 1)],
                         vwl[:, 1024 * c + 512 * b:1024 * c + 512 * (b + 1)],
                         ob, first=(c == 0), last=(c == DC - 1))
            ctx.op("tensor", "nop", inc="sT")
            ctx.wait("vector", "sT")
            a2 = R2 + 1024 * t + 512 * b
            a3 = R3 + 1024 * t + 512 * b
            _hl(ctx, arb[:, a2:a2 + 512], arb[:, a3:a3 + 512], ob)
            ctx.op("vector", "nop", inc="sV")



def phase_H(ctx, st, s):
    arb, arf, tiny, ps, sc, p, identb = (st["arb"], st["arf"], st["tiny"],
                                         st["ps"], st["sc"], st["p"],
                                         st["identb"])
    onesf, brow = st["onesf"], st["brow"]
    cin = st["cin"]
    base = s * S
    pj_h = arb[:, R4:R4 + 1024]
    pj_l = arb[:, R4 + 1024:R4 + 2048]
    ctx.dma("sync", pj_h[:].rearrange("p (c n) -> p c n", c=DC),
            p["projT_h"].rearrange("(c p) n -> p c n", p=128))
    ctx.dma("sync", pj_l[:].rearrange("p (c n) -> p c n", c=DC),
            p["projT_l"].rearrange("(c p) n -> p c n", p=128))
    ctx.dma("sync", brow[:], cin["boutr"])
    hr = R4 + 2048
    qh_h = arb[:, hr:hr + 2048]
    qh_l = arb[:, hr + 2048:hr + 4096]
    kh_h = arb[:, hr + 4096:hr + 6144]
    kh_l = arb[:, hr + 6144:hr + 8192]
    qp_h, qp_l, kp_h, kp_l = qh_h, qh_l, kh_h, kh_l
    at_h = arb[:, R4 + 10240:R4 + 10368]
    at_l = arb[:, R4 + 10368:R4 + 10496]
    ks_h = arb[:, R4 + 10496:R4 + 10560]
    ks_l = arb[:, R4 + 10560:R4 + 10624]
    an_h = arb[:, R4 + 10624:R4 + 10752]
    an_l = arb[:, R4 + 10752:R4 + 10880]
    qpf = arf[:, 0:2048]
    kpf = arf[:, 2048:4096]
    ksf = arf[:, 4096:4160]
    zbuf = arf[:, 4160:4176]
    atf = arf[:, 4304:4432]
    att = arf[:, 4432:4560]
    rlu = arf[:, 4560:5072]
    ctx.wait("tensor", "sD")

    for h in range(HEADS):
        ctx.wait("sync", "sT")
        ctx.dma("sync", qh_h, sc["q1_h"][128 * h:128 * (h + 1), :])
        ctx.dma("sync", qh_l, sc["q1_l"][128 * h:128 * (h + 1), :])
        ctx.dma("sync", kh_h, sc["k1_h"][128 * h:128 * (h + 1), :])
        ctx.dma("sync", kh_l, sc["k1_l"][128 * h:128 * (h + 1), :])
        ctx.wait("tensor", "sD")
        for src_h, src_l, dst in ((qh_h, qh_l, qpf), (kh_h, kh_l, kpf)):
            for n in range(4):
                bank = n % 4
                ob = ps[:, 512 * bank:512 * (bank + 1)]
                ctx.wait("tensor", "sV")
                _mm3_acc(ctx, pj_h[:, 128 * h:128 * (h + 1)],
                         pj_l[:, 128 * h:128 * (h + 1)],
                         src_h[:, 512 * n:512 * (n + 1)],
                         src_l[:, 512 * n:512 * (n + 1)],
                         ob, first=True, last=True)
                ctx.op("tensor", "nop", inc="sT")
                ctx.wait("vector", "sT")
                sl = slice(512 * n, 512 * (n + 1))
                ctx.op("vector", "tensor_scalar", out=dst[:, sl], in0=ob,
                       scalar1=0.0, scalar2=None, op0=A.min, inc="sV")
                ctx.wait("scalar", "sV")
                ctx.op("scalar", "activation", out=dst[:, sl], in_=dst[:, sl],
                       func=AF.Exp)
                ctx.op("scalar", "activation", out=rlu, in_=ob, func=AF.Relu,
                       inc="sC")
                ctx.wait("vector", "sC")
                ctx.op("vector", "tensor_tensor", out=dst[:, sl],
                       in0=dst[:, sl], in1=rlu, op=A.add, inc="sV")
        _hl(ctx, qp_h, qp_l, qpf)
        _hl(ctx, kp_h, kp_l, kpf)
        ctx.op("vector", "tensor_reduce", out=ksf,
               in_=kpf.rearrange("p (b s) -> p b s", s=BLK), axis=AX, op=A.add)
        ctx.fence("vector")
        _hl(ctx, ks_h, ks_l, ksf, fence=True)
        ctx.op("vector", "nop", inc="sV")
        ctx.wait("tensor", "sV")
        for gi in range(NT):
            c0 = 128 * gi
            obA = ps[:, 2048:2176]
            ctx.op("tensor", "matmul", out=obA, lhsT=kp_h[:, c0:c0 + 128],
                   rhs=qp_h[:, c0:c0 + 128], start=True, stop=False)
            ctx.op("tensor", "matmul", out=obA, lhsT=kp_h[:, c0:c0 + 128],
                   rhs=qp_l[:, c0:c0 + 128], start=False, stop=False)
            ctx.op("tensor", "matmul", out=obA, lhsT=kp_l[:, c0:c0 + 128],
                   rhs=qp_h[:, c0:c0 + 128], start=False, stop=True)
            for b in range(4):
                zo = ps[32 * b:32 * (b + 1), 2560:2561]
                tp = (0, 32 * b)
                ctx.op("tensor", "matmul", out=zo,
                       lhsT=qp_h[:, c0 + 32 * b:c0 + 32 * (b + 1)],
                       rhs=ks_h[:, 4 * gi + b:4 * gi + b + 1],
                       tile_position=tp, start=True, stop=False)
                ctx.op("tensor", "matmul", out=zo,
                       lhsT=qp_h[:, c0 + 32 * b:c0 + 32 * (b + 1)],
                       rhs=ks_l[:, 4 * gi + b:4 * gi + b + 1],
                       tile_position=tp, start=False, stop=False)
                ctx.op("tensor", "matmul", out=zo,
                       lhsT=qp_l[:, c0 + 32 * b:c0 + 32 * (b + 1)],
                       rhs=ks_h[:, 4 * gi + b:4 * gi + b + 1],
                       tile_position=tp, start=False, stop=True)
            ctx.op("tensor", "nop", inc="sT")
            ctx.wait("vector", "sT")
            ctx.op("vector", "tensor_copy", out=atf, in_=obA)
            ctx.fence("vector")
            _hl(ctx, at_h, at_l, atf, fence=True)
            ctx.op("vector", "tensor_scalar", out=zbuf[:, 0:1],
                   in0=ps[:, 2560:2561], scalar1=1e-8, scalar2=None, op0=A.add)
            ctx.fence("vector")
            ctx.op("vector", "reciprocal", out=zbuf[:, 1:2], in_=zbuf[:, 0:1],
                   inc="sV")
            ctx.fence("vector")
            ctx.wait("tensor", "sV")
            obN = ps[:, 3072:3200]
            for b in range(4):
                vh = arb[32 * b:32 * (b + 1),
                         R2 + 1024 * gi + 128 * h:R2 + 1024 * gi + 128 * (h + 1)]
                vl = arb[32 * b:32 * (b + 1),
                         R3 + 1024 * gi + 128 * h:R3 + 1024 * gi + 128 * (h + 1)]
                ah = at_h[32 * b:32 * (b + 1), 32 * b:32 * (b + 1)]
                al = at_l[32 * b:32 * (b + 1), 32 * b:32 * (b + 1)]
                on = obN[32 * b:32 * (b + 1), :]
                tp = (32 * b, 32 * b)
                ctx.op("tensor", "matmul", out=on, lhsT=ah, rhs=vh,
                       tile_position=tp, start=True, stop=False)
                ctx.op("tensor", "matmul", out=on, lhsT=ah, rhs=vl,
                       tile_position=tp, start=False, stop=False)
                ctx.op("tensor", "matmul", out=on, lhsT=al, rhs=vh,
                       tile_position=tp, start=False, stop=True)
            ctx.op("tensor", "nop", inc="sT")
            ctx.wait("vector", "sT")
            ctx.op("vector", "tensor_scalar", out=att, in0=obN,
                   scalar1=zbuf[:, 1:2], scalar2=None, op0=A.mult)
            ctx.fence("vector")
            _hl(ctx, an_h, an_l, att, fence=True)
            ctx.op("vector", "nop", inc="sV")
            ctx.wait("tensor", "sV")
            ctx.op("tensor", "transpose", out=ps[:, 3584:3648].bitcast(BF16),
                   in_=an_h, identity=identb[:])
            ctx.op("tensor", "transpose", out=ps[:, 3648:3712].bitcast(BF16),
                   in_=an_l, identity=identb[:], inc="sT")
            ctx.wait("vector", "sT")
            a0 = R0 + 2048 * h + 128 * gi
            a1 = R1 + 2048 * h + 128 * gi
            ctx.op("vector", "tensor_copy", out=arb[:, a0:a0 + 128],
                   in_=ps[:, 3584:3648].bitcast(BF16))
            ctx.op("vector", "tensor_copy", out=arb[:, a1:a1 + 128],
                   in_=ps[:, 3648:3712].bitcast(BF16), inc="sV")
            ctx.wait("tensor", "sV")
    ctx.tap("attnfm_hi", arb[:, R0:R0 + RW])

    woh = arb[:, R4:R4 + 8192]
    wol = arb[:, R4 + 8192:R4 + 16384]
    ctx.wait("sync", "sT")
    ctx.dma("sync", woh[:].rearrange("p (c n) -> p c n", c=DC),
            p["woT_h"].rearrange("(c p) n -> p c n", p=128))
    ctx.dma("sync", wol[:].rearrange("p (c n) -> p c n", c=DC),
            p["woT_l"].rearrange("(c p) n -> p c n", p=128))
    ctx.wait("tensor", "sD")
    x1t = arf[:, 0:1024]
    g1t = arf[:, 1024:2048]
    ot = arf[:, 2048:3072]
    for t in range(NT):
        ctx.wait("sync", "sV")
        ctx.dma("sync", x1t, sc["x1"][128 * t:128 * (t + 1), :])
        ctx.dma("sync", g1t, sc["g1"][128 * t:128 * (t + 1), :])
        dv = ctx.val("sD")
        for b in range(2):
            ob = ps[:, 512 * b:512 * (b + 1)]
            ctx.wait("tensor", "sV")
            ctx.op("tensor", "matmul", out=ob, lhsT=onesf[:],
                   rhs=brow[0:1, 512 * b:512 * (b + 1)], start=True, stop=False)
            for c in range(DC):
                _mm3_acc(ctx,
                         arb[:, R0 + 2048 * c + 128 * t:R0 + 2048 * c + 128 * t + 128],
                         arb[:, R1 + 2048 * c + 128 * t:R1 + 2048 * c + 128 * t + 128],
                         woh[:, 1024 * c + 512 * b:1024 * c + 512 * (b + 1)],
                         wol[:, 1024 * c + 512 * b:1024 * c + 512 * (b + 1)],
                         ob, first=False, last=(c == DC - 1))
            ctx.op("tensor", "nop", inc="sT")
        ctx.wait("vector", "sT")
        ctx.wait("vector", "sD")
        ctx.wait("vector", "sSp")
        for b in range(2):
            ob = ps[:, 512 * b:512 * (b + 1)]
            sl = slice(512 * b, 512 * (b + 1))
            ctx.op("vector", "tensor_tensor", out=ot[:, sl], in0=ob,
                   in1=g1t[:, sl], op=A.mult)
            ctx.op("vector", "tensor_tensor", out=ot[:, sl], in0=ot[:, sl],
                   in1=x1t[:, sl], op=A.add)
        ctx.op("vector", "nop", inc="sV")
        ctx.wait("sync", "sV")
        ctx.dma("sync", st["out"][base + 128 * t:base + 128 * (t + 1), :], ot,
                inc="sSp")



def _mk_consts():
    identf = np.eye(128, dtype=np.float32)
    identb = np.eye(128, dtype=ml_dtypes.bfloat16)
    onesf = np.ones((1, 128), dtype=np.float32)
    onescol = np.ones((128, 1), dtype=np.float32)
    masks = np.zeros((128, 4 * 512), dtype=np.float32)
    for pp in range(4):
        m = np.zeros((128, 512), dtype=np.float32)
        m[:, 128 * (pp + 1):] = NEG
        m[:, 128 * pp:128 * (pp + 1)] = np.triu(
            np.full((128, 128), NEG, dtype=np.float32), 1)
        masks[:, 512 * pp:512 * (pp + 1)] = m
    return identf, identb, onesf, onescol, masks


def kernel(L0_attn_x, L0_attn_WQ, L0_attn_WK, L0_attn_WV, L0_ffn_w_gate,
           L0_ffn_w_up, L0_ffn_w_down, L1_attn_w_qkv, L1_attn_w_out,
           L1_attn_b_out, L1_attn_proj, L1_ffn_w_router, L1_ffn_W1, L1_ffn_b1,
           L1_ffn_W2, L1_ffn_b2, _dbg=None, _cores=None, _stop=None):
    x = np.ascontiguousarray(np.asarray(L0_attn_x, dtype=np.float32))
    identf, identb, onesf, onescol, masks = _mk_consts()
    T = lambda a: np.ascontiguousarray(np.asarray(a, dtype=np.float32).T)
    C = lambda a: np.ascontiguousarray(np.asarray(a, dtype=np.float32))
    w = dict(
        wqT=T(L0_attn_WQ), wkT=T(L0_attn_WK), wvT=T(L0_attn_WV),
        wgT=T(L0_ffn_w_gate), wuT=T(L0_ffn_w_up), wdT=T(L0_ffn_w_down),
        wqkvT=T(L1_attn_w_qkv), woT=T(L1_attn_w_out),
        projT=C(L1_attn_proj).reshape(D, FD),
        W1=C(L1_ffn_W1), W2=C(L1_ffn_W2),
        wrT=C(L1_ffn_w_router).reshape(D, 1),
        identf=identf, identb=identb, onesf=onesf, onescol=onescol,
        masks=masks,
        b1fm=np.ascontiguousarray(C(L1_ffn_b1).reshape(IC, 128).T),
        b2fm=np.ascontiguousarray(C(L1_ffn_b2).reshape(DC, 128).T),
        boutr=C(L1_attn_b_out).reshape(1, D),
        bconst=np.full((128, 1), 1e-5, dtype=np.float32),
    )
    nc = build_core(dbg=_dbg, stop=_stop)
    ncores = NCORE if _cores is None else _cores
    in_maps = []
    for c in range(ncores):
        m = dict(w)
        m["x"] = np.ascontiguousarray(x[NSEQ * c:NSEQ * (c + 1)].reshape(NSEQ * S, D))
        in_maps.append(m)
    res = run_bass_kernel_spmd(nc, in_maps, core_ids=list(range(ncores)))
    if _dbg:
        return res
    outs = [np.asarray(r["out"]).reshape(NSEQ, S, D) for r in res.results]
    return np.concatenate(outs, axis=0).astype(np.float32)


# version 04
# speedup vs baseline: 1.0030x; 1.0030x over previous
import numpy as np
import ml_dtypes
import concourse.bass as bass
import concourse.mybir as mybir
from concourse.bass_utils import run_bass_kernel_spmd
from concourse.alu_op_type import AluOpType as A

F32 = mybir.dt.float32
BF16 = mybir.dt.bfloat16
I32 = mybir.dt.int32
AX = mybir.AxisListType.X
AF = mybir.ActivationFunctionType

S = 2048
D = 1024
DC = D // 128
NT = S // 128
INNER = 4096
IC = INNER // 128
HEADS = 8
HD = 128
FD = 128
BLK = 32
NSEQ = 2
NCORE = 8
SCALE0 = 1024 ** -0.5
NEG = -1.0e30

RW = 16384
R0, R1, R2, R3, R4 = 0, RW, 2 * RW, 3 * RW, 4 * RW


class Ctx:
    ENGS = ("sync", "gpsimd", "vector", "scalar", "tensor")

    def __init__(self, nc, dbg):
        self.nc = nc
        self.dbg = set(dbg or ())
        self.prog = {e: [] for e in self.ENGS}
        self.cnt = {}
        self.sems = {}
        self.dbg_out = {}
        self.waited = {}
        self.prior_waited = {}

    def emit(self, eng, fn):
        self.prog[eng].append(fn)

    def inc(self, sem, amt=1):
        self.cnt[sem] = self.cnt.get(sem, 0) + amt
        return self.cnt[sem]

    def val(self, sem):
        return self.cnt.get(sem, 0)

    def op(self, eng, meth, *args, inc=None, inc_amt=1, **kw):
        sems = self.sems
        if inc is not None:
            self.inc(inc, inc_amt)
        if eng == "tensor" and meth == "nop" and inc is not None:
            meth = "drain"
        fence = inc is not None and eng in ("vector", "scalar")
        pre_drain = eng in ("vector", "scalar") and meth not in ("nop", "drain")

        skip_op = eng in ("vector", "scalar") and meth == "nop" and inc is not None

        def fn(e, meth=meth, args=args, kw=kw, inc=inc, inc_amt=inc_amt,
               fence=fence, pre_drain=pre_drain, skip_op=skip_op):
            if pre_drain:
                e.drain()
            if skip_op:
                e.drain()
                e.drain().then_inc(sems[inc], inc_amt)
                return
            r = getattr(e, meth)(*args, **kw)
            if inc is not None:
                if fence:
                    e.drain().then_inc(sems[inc], inc_amt)
                else:
                    r.then_inc(sems[inc], inc_amt)
        self.emit(eng, fn)

    def fence(self, eng):
        sems = self.sems
        n = self.inc("f" + eng)

        def fn(e, n=n, s="f" + eng):
            e.drain()
            e.drain().then_inc(sems[s], 1)
            e.wait_ge(sems[s], n)
        self.emit(eng, fn)

    def wait(self, eng, sem, v=None):
        v = self.val(sem) if v is None else v
        self.waited.setdefault(sem, set()).add(v)
        sems = self.sems
        self.emit(eng, lambda e, s=sem, v=v: e.wait_ge(sems[s], v))

    def dma(self, eng, out, in_, inc="sD", **kw):
        if self.val(inc) in self.prior_waited.get(inc, ()):  
            self.wait(eng, inc)
        self.op(eng, "dma_start", inc=inc, inc_amt=16, out=out, in_=in_, **kw)

    def barrier(self):
        self.wait("sync", "sD")
        self.wait("sync", "sSp")
        self.wait("gpsimd", "sG")
        for e in self.ENGS:
            self.op(e, "nop", inc="b" + e)
        for e in self.ENGS:
            for o in self.ENGS:
                if o != e:
                    self.wait(e, "b" + o)

    def tap(self, name, src_ap):
        if name not in self.dbg:
            return
        self.barrier()
        self.dma("sync", self.dbg_out[name], src_ap)
        self.barrier()


def _hl(ctx, dst_hi, dst_lo, src_f32, fence=False):
    ctx.op("vector", "tensor_copy", out=dst_hi, in_=src_f32)
    if fence:
        ctx.fence("vector")
    ctx.op("vector", "tensor_tensor", out=dst_lo, in0=src_f32, in1=dst_hi,
           op=A.subtract)


def _mm3_acc(ctx, lhsT_h, lhsT_l, rhs_h, rhs_l, out, first, last):
    ctx.op("tensor", "matmul", out=out, lhsT=lhsT_h, rhs=rhs_h,
           start=first, stop=False)
    ctx.op("tensor", "matmul", out=out, lhsT=lhsT_h, rhs=rhs_l,
           start=False, stop=False)
    ctx.op("tensor", "matmul", out=out, lhsT=lhsT_l, rhs=rhs_h,
           start=False, stop=last)


def build_core(dbg=None, stop=None):
    nc = bass.Bass("TRN2", target_bir_lowering=False)
    ctx = Ctx(nc, dbg)

    x_in = nc.dram_tensor("x", [NSEQ * S, D], F32, kind="ExternalInput").ap()
    out = nc.dram_tensor("out", [NSEQ * S, D], F32, kind="ExternalOutput").ap()

    win = {}
    for nm, shp in [
        ("wqT", [D, D]), ("wkT", [D, D]), ("wvT", [D, D]),
        ("wgT", [D, INNER]), ("wuT", [D, INNER]), ("wdT", [INNER, D]),
        ("wqkvT", [D, 3 * D]), ("woT", [D, D]), ("projT", [D, FD]),
        ("W1", [D, INNER]), ("W2", [INNER, D]), ("wrT", [D, 1]),
    ]:
        win[nm] = nc.dram_tensor(nm, shp, F32, kind="ExternalInput").ap()
    cin = {}
    for nm, shp, dtp in [
        ("identf", [128, 128], F32), ("identb", [128, 128], BF16),
        ("onesf", [1, 128], F32), ("onescol", [128, 1], F32),
        ("masks", [128, 4 * 512], BF16), ("b1fm", [128, IC], F32),
        ("b2fm", [128, DC], F32), ("boutr", [1, D], F32),
        ("bconst", [128, 1], F32),
    ]:
        cin[nm] = nc.dram_tensor(nm, shp, dtp, kind="ExternalInput").ap()

    dt = nc.dram_tensor
    p = {}
    for nm, shp in [
        ("wqT_h", [D, D]), ("wqT_l", [D, D]), ("wkT_h", [D, D]), ("wkT_l", [D, D]),
        ("wvT_h", [D, D]), ("wvT_l", [D, D]),
        ("wqkvT_h", [D, 3 * D]), ("wqkvT_l", [D, 3 * D]),
        ("woT_h", [D, D]), ("woT_l", [D, D]),
        ("projT_h", [D, FD]), ("projT_l", [D, FD]),
        ("W1_h", [D, INNER]), ("W1_l", [D, INNER]),
        ("W2_h", [INNER, D]), ("W2_l", [INNER, D]),
        ("wrT_h", [D, 1]), ("wrT_l", [D, 1]),
        ("wgq", [D, INNER]), ("wuq", [D, INNER]), ("wdq", [INNER, D]),
    ]:
        p[nm] = dt("p_" + nm, shp, BF16).ap()
    sc = {}
    for nm, shp, dtp in [
        ("xnq", [NT, DC, 128, 128], BF16),
        ("q0_h", [D, S], BF16), ("q0_l", [D, S], BF16),
        ("v0_h", [S, D], BF16), ("v0_l", [S, D], BF16),
        ("mid", [S, INNER], F32),
        ("g0", [S, D], F32),
        ("x1", [S, D], F32),
        ("xn1", [S, D], F32),
        ("srow", [NT, 128], F32),
        ("q1_h", [D, S], BF16), ("q1_l", [D, S], BF16),
        ("k1_h", [D, S], BF16), ("k1_l", [D, S], BF16),
        ("g1", [S, D], F32),
        ("absr", [128, NT], F32),
    ]:
        sc[nm] = dt("s_" + nm, shp, dtp).ap()

    for nm in (dbg or ()):
        shp, dtp = {
            "stats": ([128, 64], F32), "xnfm_hi": ([128, RW], BF16),
            "xnfm_lo": ([128, RW], BF16), "xnq": ([NT * DC * 128, 128], BF16),
            "kfm_hi": ([128, RW], BF16), "kfm_lo": ([128, RW], BF16),
            "q0_h": ([D, S], BF16), "q0_l": ([D, S], BF16),
            "v0_l": ([S, D], BF16),
            "v0_h": ([S, D], BF16), "mid": ([S, INNER], F32),
            "g0": ([S, D], F32), "x1": ([S, D], F32), "xn1": ([S, D], F32),
            "srow": ([NT, 128], F32), "wtok": ([128, NT], F32),
            "g1": ([S, D], F32), "attnfm_hi": ([128, RW], BF16),
            "wgq": ([D, INNER], BF16), "wdq": ([INNER, D], BF16),
            "wqT_h": ([D, D], BF16), "wqT_l": ([D, D], BF16),
            "absr": ([128, NT], F32),
            "wstat": ([128, 16], F32),
        }[nm]
        ctx.dbg_out[nm] = dt("dbg_" + nm, shp, dtp, kind="ExternalOutput").ap()

    import contextlib
    es = contextlib.ExitStack()
    with es:
        E = es.enter_context
        arb = E(nc.sbuf_tensor("arb", [128, 5 * RW], BF16))
        arf = E(nc.sbuf_tensor("arf", [128, 9216], F32))
        masks = E(nc.sbuf_tensor("sb_masks", [128, 4 * 512], BF16))
        stg = E(nc.sbuf_tensor("sb_stg", [128, 1024], BF16))
        qi32 = E(nc.sbuf_tensor("sb_qi32", [128, 1024], I32))
        identf = E(nc.sbuf_tensor("sb_identf", [128, 128], F32))
        identb = E(nc.sbuf_tensor("sb_identb", [128, 128], BF16))
        onesf = E(nc.sbuf_tensor("sb_onesf", [1, 128], F32))
        onescol = E(nc.sbuf_tensor("sb_onescol", [128, 1], F32))
        tiny = E(nc.sbuf_tensor("tiny", [128, 112], F32))
        bconst = E(nc.sbuf_tensor("sb_bconst", [128, 1], F32))
        nc.const_aps.register(F32, 1e-5, bconst[:, 0:1])
        ps = E(nc.psum_tensor("ps", [128, 3072], F32))
        psb = E(nc.psum_tensor("psb", [128, 2048], BF16))
        for s_name in ("sD", "sW", "sSp", "sT", "sV", "sC", "sG",
                       "fvector", "fscalar",
                       *("b" + e for e in Ctx.ENGS)):
            ctx.sems[s_name] = E(nc.semaphore(s_name))

        st = dict(stop=stop, x_in=x_in, out=out, win=win, p=p, sc=sc, arb=arb, arf=arf,
                  masks=masks, identf=identf, identb=identb, onesf=onesf,
                  onescol=onescol, brow=arf[0:1, 5120:6144], tiny=tiny,
                  bconst=bconst, stg=stg, qi32=qi32, psb=psb,
                  trow=arf[0:1, 4096:8256], ps=ps,
                  cin=cin)
        bld_all(ctx, st)
        ctx.prior_waited = {k: set(v) for k, v in ctx.waited.items()}
        ctx.prog = {e: [] for e in Ctx.ENGS}
        ctx.cnt = {}
        ctx.waited = {}
        bld_all(ctx, st)

        blk = E(nc.Block())

        def mk(engname):
            def run(e):
                for fn in ctx.prog[engname]:
                    fn(e)
            return run
        blk.sync(mk("sync"))
        blk.gpsimd(mk("gpsimd"))
        blk.vector(mk("vector"))
        blk.scalar(mk("scalar"))
        blk.tensor(mk("tensor"))
    return nc


def bld_all(ctx, st):
    cin = st["cin"]
    ctx.dma("sync", st["masks"][:], cin["masks"])
    ctx.dma("sync", st["identf"][:], cin["identf"])
    ctx.dma("sync", st["identb"][:], cin["identb"])
    ctx.dma("sync", st["onesf"][:], cin["onesf"])
    ctx.dma("sync", st["onescol"][:], cin["onescol"])
    ctx.dma("sync", st["bconst"][:], cin["bconst"])
    ctx.wait("sync", "sD")
    for e in ("vector", "scalar", "tensor"):
        ctx.wait(e, "sD")

    stop = st.get("stop")
    def done():
        ctx.wait("sync", "sSp")
        ctx.wait("sync", "sD")
        ctx.wait("sync", "sG")
    phase_W(ctx, st)
    ctx.barrier()
    if stop == "W":
        return done()
    nseq = 1 if stop else NSEQ
    for s in range(nseq):
        for nm, fn in [("A", phase_A), ("C", phase_C), ("B", phase_B),
                       ("D", phase_D), ("E", phase_E), ("F", phase_F),
                       ("G", phase_G), ("H", phase_H)]:
            fn(ctx, st, s)
            ctx.barrier()
            if stop == nm or (nm == "C" and stop in ("Cq", "Ck")):
                return done()
    done()



def phase_W(ctx, st):
    arb, arf, ps, tiny = st["arb"], st["arf"], st["ps"], st["tiny"]
    onesf, onescol = st["onesf"], st["onescol"]
    win, p = st["win"], st["p"]

    f32buf = arf[:, 4096:8192]
    for nm, w in [("wqT", D), ("wkT", D), ("wvT", D), ("wqkvT", 3 * D),
                  ("woT", D), ("projT", FD), ("W1", INNER), ("W2", D),
                  ("wrT", 1)]:
        src = win[nm]
        nchunk = src.shape[0] // 128
        for c in range(nchunk):
            fb = f32buf[:, 0:w]
            hb = arb[:, R3:R3 + w]
            lb = arb[:, R3 + w:R3 + 2 * w]
            ctx.dma("sync", fb, src[128 * c:128 * (c + 1), :])
            ctx.wait("gpsimd", "sV")
            ctx.wait("gpsimd", "sSp")
            ctx.dma("gpsimd", hb, src[128 * c:128 * (c + 1), :], inc="sG")
            ctx.wait("vector", "sD")
            ctx.wait("vector", "sG")
            ctx.wait("vector", "sSp")
            ctx.op("vector", "tensor_tensor", out=lb, in0=fb, in1=hb,
                   op=A.subtract, inc="sV")
            ctx.wait("sync", "sV")
            ctx.dma("sync", p[nm + "_h"][128 * c:128 * (c + 1), :], hb, inc="sSp")
            ctx.dma("sync", p[nm + "_l"][128 * c:128 * (c + 1), :], lb, inc="sSp")
            ctx.wait("sync", "sSp")
    ctx.tap("wqT_h", p["wqT_h"])
    ctx.tap("wqT_l", p["wqT_l"])

    for nm, w, outnm in [("wgT", INNER, "wgq"), ("wuT", INNER, "wuq"),
                         ("wdT", D, "wdq")]:
        src = win[nm]
        nchunk = src.shape[0] // 128
        nelem = float(src.shape[0] * src.shape[1])
        acc = tiny[:, 64:65]
        trash = arf[:, 0:w]
        for c in range(nchunk):
            fb = f32buf[:, 0:w]
            ctx.wait("sync", "sC")
            ctx.dma("sync", fb, src[128 * c:128 * (c + 1), :])
            ctx.wait("scalar", "sD")
            ctx.op("scalar", "activation", out=trash, in_=fb, func=AF.Abs,
                   accum_out=tiny[:, 65:66], inc="sC")
            ctx.wait("vector", "sC")
            if c == 0:
                ctx.op("vector", "tensor_copy", out=acc, in_=tiny[:, 65:66],
                       inc="sV")
            else:
                ctx.op("vector", "tensor_tensor", out=acc, in0=acc,
                       in1=tiny[:, 65:66], op=A.add, inc="sV")
            ctx.fence("vector")
            ctx.wait("scalar", "sV")
        ctx.wait("tensor", "sV")
        ctx.op("tensor", "matmul", out=ps[0:1, 2048:2049], lhsT=onescol[:],
               rhs=acc, start=True, stop=True, inc="sT")
        ctx.wait("vector", "sT")
        ctx.op("vector", "tensor_scalar", out=tiny[0:1, 66:67],
               in0=ps[0:1, 2048:2049], scalar1=1.0 / nelem, scalar2=1e-5,
               op0=A.mult, op1=A.max)
        ctx.fence("vector")
        ctx.op("vector", "tensor_scalar", out=tiny[0:1, 67:68],
               in0=tiny[0:1, 66:67], scalar1=0.5, scalar2=None, op0=A.mult,
               inc="sV")
        ctx.wait("tensor", "sV")
        ctx.op("tensor", "matmul", out=ps[:, 2052:2053], lhsT=onesf[:],
               rhs=tiny[0:1, 66:67], start=True, stop=True)
        ctx.op("tensor", "matmul", out=ps[:, 2053:2054], lhsT=onesf[:],
               rhs=tiny[0:1, 67:68], start=True, stop=True, inc="sT")
        ctx.wait("vector", "sT")
        mw_slot = {"wgq": 68, "wuq": 69, "wdq": 70}[outnm]
        ctx.op("vector", "tensor_copy", out=tiny[:, mw_slot:mw_slot + 1],
               in_=ps[:, 2052:2053])
        thr = tiny[:, 71:72]
        nthr = tiny[:, 72:73]
        ctx.op("vector", "tensor_copy", out=thr, in_=ps[:, 2053:2054])
        ctx.fence("vector")
        ctx.op("vector", "tensor_scalar", out=nthr, in0=thr, scalar1=-1.0,
               scalar2=None, op0=A.mult, inc="sV")
        ctx.fence("vector")
        for c in range(nchunk):
            fb = f32buf[:, 0:w]
            c1 = arf[:, 0:w]
            qb = arb[:, R3:R3 + w]
            ctx.wait("sync", "sV")
            ctx.wait("sync", "sSp")
            ctx.dma("sync", fb, src[128 * c:128 * (c + 1), :])
            ctx.wait("vector", "sD")
            ctx.op("vector", "tensor_scalar", out=c1, in0=fb, scalar1=thr,
                   scalar2=None, op0=A.is_ge)
            ctx.op("vector", "scalar_tensor_tensor", out=qb, in0=fb, scalar=nthr,
                   in1=c1, op0=A.is_le, op1=A.subtract, inc="sV")
            ctx.wait("sync", "sV")
            ctx.dma("sync", p[outnm][128 * c:128 * (c + 1), :], qb, inc="sSp")
    ctx.tap("wstat", tiny[:, 64:80])
    ctx.tap("wgq", p["wgq"])
    ctx.tap("wdq", p["wdq"])



def _ln_fm(ctx, st, src_dram, base, quant, xn_tok_dram=None, sseq=0):
    arb, arf, tiny, ps, identf = (st["arb"], st["arf"], st["tiny"], st["ps"],
                                  st["identf"])
    xb = [arf[:, 0:1024], arf[:, 1024:2048]]
    trash = arf[:, 2048:3072]
    m_all = tiny[:, 0:16]
    ssq_all = tiny[:, 16:32]
    rstd = tiny[:, 32:48]

    for t in range(NT):
        b = xb[t % 2]
        ctx.wait("sync", "sC")
        ctx.dma("sync", b, src_dram[base + 128 * t:base + 128 * (t + 1), :])
        ctx.wait("scalar", "sD")
        ctx.op("scalar", "activation", out=trash, in_=b, func=AF.Copy,
               accum_out=m_all[:, t:t + 1])
        ctx.op("scalar", "activation", out=trash, in_=b, func=AF.Square,
               accum_out=ssq_all[:, t:t + 1], inc="sC")
    ctx.wait("vector", "sC")
    ctx.op("vector", "tensor_scalar", out=m_all, in0=m_all, scalar1=1.0 / D,
           scalar2=None, op0=A.mult)
    ctx.op("vector", "tensor_scalar", out=ssq_all, in0=ssq_all, scalar1=1.0 / D,
           scalar2=None, op0=A.mult)
    ctx.fence("vector")
    v2 = tiny[:, 48:64]
    ctx.op("vector", "tensor_tensor", out=v2, in0=m_all, in1=m_all, op=A.mult)
    ctx.fence("vector")
    ctx.op("vector", "tensor_tensor", out=v2, in0=ssq_all, in1=v2,
           op=A.subtract, inc="sV")
    ctx.wait("scalar", "sV")
    ctx.op("scalar", "activation", out=v2, in_=v2, func=AF.Sqrt, bias=1e-5,
           scale=1.0, inc="sC")
    ctx.wait("vector", "sC")
    ctx.op("vector", "reciprocal", out=rstd, in_=v2, inc="sV")
    ctx.fence("vector")

    xnt = arf[:, 2048:3072]
    for t in range(NT):
        b = xb[t % 2]
        ctx.wait("sync", "sV")
        ctx.dma("sync", b, src_dram[base + 128 * t:base + 128 * (t + 1), :])
        ctx.wait("vector", "sD")
        ctx.wait("vector", "sSp")
        ctx.op("vector", "tensor_scalar", out=xnt, in0=b,
               scalar1=m_all[:, t:t + 1], scalar2=rstd[:, t:t + 1],
               op0=A.subtract, op1=A.mult, inc="sV")
        if xn_tok_dram is not None:
            ctx.wait("sync", "sV")
            ctx.dma("sync", xn_tok_dram[128 * t:128 * (t + 1), :], xnt,
                    inc="sSp")
        ctx.wait("tensor", "sV")
        for c in range(DC):
            bank = (c // 4) % 2
            off = 512 * bank + 128 * (c % 4)
            ctx.op("tensor", "transpose", out=ps[:, off:off + 128],
                   in_=xnt[:, 128 * c:128 * (c + 1)], identity=identf[:],
                   inc="sT" if c == DC - 1 else None)
        ctx.wait("vector", "sT")
        for c in range(DC):
            bank = (c // 4) % 2
            off = 512 * bank + 128 * (c % 4)
            a0 = R0 + 2048 * c + 128 * t
            ctx.op("vector", "tensor_copy", out=arb[:, a0:a0 + 128],
                   in_=ps[:, off:off + 128])
        for c in range(DC):
            bank = (c // 4) % 2
            off = 512 * bank + 128 * (c % 4)
            a0 = R0 + 2048 * c + 128 * t
            a1 = R1 + 2048 * c + 128 * t
            ctx.op("vector", "tensor_tensor", out=arb[:, a1:a1 + 128],
                   in0=ps[:, off:off + 128], in1=arb[:, a0:a0 + 128],
                   op=A.subtract)
        ctx.op("vector", "nop", inc="sV")
        if quant:
            _quant_tile(ctx, st, xnt, t)
    ctx.op("vector", "nop", inc="sV")
    ctx.op("scalar", "nop", inc="sC")


def _quant_tile(ctx, st, xnt, t):
    arb, arf, tiny, ps, identb, sc = (st["arb"], st["arf"], st["tiny"],
                                      st["ps"], st["identb"], st["sc"])
    am = tiny[:, 80:81]
    sq = tiny[:, 81:82]
    inv = tiny[:, 82:83]
    ctx.op("vector", "reduce_max", out=am, in_=xnt, axis=AX,
           apply_absolute_value=True)
    ctx.fence("vector")
    ctx.op("vector", "tensor_scalar", out=am, in0=am, scalar1=1e-5,
           scalar2=None, op0=A.max)
    ctx.fence("vector")
    ctx.op("vector", "reciprocal", out=sq, in_=am)
    ctx.fence("vector")
    ctx.op("vector", "tensor_scalar", out=sq, in0=sq, scalar1=127.0,
           scalar2=None, op0=A.mult)
    ctx.fence("vector")
    ctx.op("vector", "reciprocal", out=inv, in_=sq)
    ctx.fence("vector")
    ctx.op("vector", "tensor_copy", out=tiny[:, 84 + t:85 + t], in_=inv)
    v = arf[:, 3072:4096]
    qi = st["qi32"][:]
    qm1 = arf[:, 5120:6144]
    dlt = arf[:, 6144:7168]
    ctx.op("vector", "tensor_scalar", out=v, in0=xnt, scalar1=sq, scalar2=None,
           op0=A.mult)
    ctx.op("vector", "tensor_copy", out=qi, in_=v, inc="sV")
    ctx.wait("scalar", "sV")
    ctx.op("scalar", "activation", out=qm1, in_=qi, func=AF.Copy, bias=-1.0,
           scale=1.0, inc="sC")
    ctx.wait("vector", "sC")
    ctx.op("vector", "tensor_tensor", out=dlt, in0=v, in1=qm1, op=A.subtract)
    ctx.op("vector", "scalar_tensor_tensor", out=v, in0=dlt, scalar=1.5,
           in1=qm1, op0=A.is_ge, op1=A.add)
    xq = arb[:, R4 + 2048:R4 + 3072]
    ctx.op("vector", "scalar_tensor_tensor", out=xq, in0=dlt, scalar=0.5,
           in1=v, op0=A.is_ge, op1=A.add, inc="sV")
    psb = st["psb"]
    ctx.wait("tensor", "sV")
    for c in range(DC):
        off = 1024 * ((c // 4) % 2) + 128 * (c % 4)
        ctx.op("tensor", "transpose", out=psb[:, off:off + 128],
               in_=xq[:, 128 * c:128 * (c + 1)], identity=identb[:],
               inc="sT" if c == DC - 1 else None)
    ctx.wait("vector", "sT")
    ctx.wait("vector", "sSp")
    stg = arb[:, R4 + 3072:R4 + 4096]
    for c in range(DC):
        off = 1024 * ((c // 4) % 2) + 128 * (c % 4)
        ctx.op("vector", "tensor_copy", out=stg[:, 128 * c:128 * (c + 1)],
               in_=psb[:, off:off + 128],
               inc="sV" if c == DC - 1 else None)
    ctx.wait("sync", "sV")
    ctx.dma("sync", sc["xnq"][t].rearrange("c p n -> p c n"),
            stg.rearrange("p (c n) -> p c n", c=DC), inc="sSp")


def phase_A(ctx, st, s):
    _ln_fm(ctx, st, st["x_in"], s * S, quant=True)
    ctx.wait("sync", "sV")
    ctx.dma("sync", st["sc"]["absr"], st["tiny"][:, 84:84 + NT], inc="sSp")
    ctx.tap("stats", st["tiny"][:, 0:64])
    ctx.tap("xnfm_hi", st["arb"][:, R0:R0 + RW])
    ctx.tap("xnfm_lo", st["arb"][:, R1:R1 + RW])
    ctx.tap("xnq", st["sc"]["xnq"].rearrange("t c p n -> (t c p) n"))
    ctx.tap("absr", st["sc"]["absr"])



def phase_C(ctx, st, s):
    arb, arf, ps, sc, p = st["arb"], st["arf"], st["ps"], st["sc"], st["p"]
    wbuf_h = arb[:, R4:R4 + 8192]
    wbuf_l = arb[:, R4 + 8192:R4 + 16384]
    ev_h = st["stg"][:, 0:512]
    ev_l = st["stg"][:, 512:1024]

    stop = st.get("stop") or ""
    mats = [("wq", "q"), ("wk", "k"), ("wv", "v")]
    if stop == "Cq":
        mats = mats[:1]
    elif stop == "Ck":
        mats = mats[:2]
    for nm, kind in mats:
        ctx.wait("sync", "sT")
        ctx.dma("sync", wbuf_h[:].rearrange("p (c n) -> p c n", c=DC),
                p[nm + "T_h"].rearrange("(c p) n -> p c n", p=128))
        ctx.dma("sync", wbuf_l[:].rearrange("p (c n) -> p c n", c=DC),
                p[nm + "T_l"].rearrange("(c p) n -> p c n", p=128))
        ctx.wait("tensor", "sD")

        if kind in ("q", "k"):
            for m in range(DC):
                for n in range(4):
                    bank = (m * 4 + n) % 4
                    ob = ps[:, 512 * bank:512 * (bank + 1)]
                    ctx.wait("tensor", "sV")
                    for c in range(DC):
                        _mm3_acc(ctx,
                                 wbuf_h[:, 1024 * c + 128 * m:1024 * c + 128 * m + 128],
                                 wbuf_l[:, 1024 * c + 128 * m:1024 * c + 128 * m + 128],
                                 arb[:, R0 + 2048 * c + 512 * n:R0 + 2048 * c + 512 * (n + 1)],
                                 arb[:, R1 + 2048 * c + 512 * n:R1 + 2048 * c + 512 * (n + 1)],
                                 ob, first=(c == 0), last=(c == DC - 1))
                    ctx.op("tensor", "nop", inc="sT")
                    ctx.wait("vector", "sT")
                    if kind == "k":
                        a2 = R2 + 2048 * m + 512 * n
                        a3 = R3 + 2048 * m + 512 * n
                        _hl(ctx, arb[:, a2:a2 + 512], arb[:, a3:a3 + 512], ob)
                        ctx.op("vector", "nop", inc="sV")
                    else:
                        ctx.wait("vector", "sSp")
                        _hl(ctx, ev_h[:, 0:512], ev_l[:, 0:512], ob)
                        ctx.op("vector", "nop", inc="sV")
                        ctx.wait("sync", "sV")
                        ctx.dma("sync", sc["q0_h"][128 * m:128 * (m + 1), 512 * n:512 * (n + 1)], ev_h[:, 0:512], inc="sSp")
                        ctx.dma("sync", sc["q0_l"][128 * m:128 * (m + 1), 512 * n:512 * (n + 1)], ev_l[:, 0:512], inc="sSp")
        else:
            for t in range(NT):
                for b in range(2):
                    bank = (t * 2 + b) % 4
                    ob = ps[:, 512 * bank:512 * (bank + 1)]
                    ctx.wait("tensor", "sV")
                    for c in range(DC):
                        xh = arb[:, R0 + 2048 * c + 128 * t:R0 + 2048 * c + 128 * t + 128]
                        xl = arb[:, R1 + 2048 * c + 128 * t:R1 + 2048 * c + 128 * t + 128]
                        wh = wbuf_h[:, 1024 * c + 512 * b:1024 * c + 512 * (b + 1)]
                        wl = wbuf_l[:, 1024 * c + 512 * b:1024 * c + 512 * (b + 1)]
                        _mm3_acc(ctx, xh, xl, wh, wl, ob,
                                 first=(c == 0), last=(c == DC - 1))
                    ctx.op("tensor", "nop", inc="sT")
                    ctx.wait("vector", "sT")
                    ctx.wait("vector", "sSp")
                    _hl(ctx, ev_h[:, 0:512], ev_l[:, 0:512], ob)
                    ctx.op("vector", "nop", inc="sV")
                    ctx.wait("sync", "sV")
                    ctx.dma("sync", sc["v0_h"][128 * t:128 * (t + 1), 512 * b:512 * (b + 1)], ev_h[:, 0:512], inc="sSp")
                    ctx.dma("sync", sc["v0_l"][128 * t:128 * (t + 1), 512 * b:512 * (b + 1)], ev_l[:, 0:512], inc="sSp")
    ctx.tap("kfm_hi", arb[:, R2:R2 + RW])
    ctx.tap("kfm_lo", arb[:, R3:R3 + RW])
    ctx.tap("q0_h", sc["q0_h"])
    ctx.tap("q0_l", sc["q0_l"])
    ctx.tap("v0_h", sc["v0_h"])
    ctx.tap("v0_l", sc["v0_l"])



def phase_B(ctx, st, s):
    arb, arf, tiny, ps, sc, p, identb = (st["arb"], st["arf"], st["tiny"],
                                         st["ps"], st["sc"], st["p"],
                                         st["identb"])
    ctx.dma("sync", arb[:, R0:R0 + RW].rearrange("p (t c n) -> p t c n",
                                                 t=NT, c=DC),
            sc["xnq"].rearrange("t c p n -> p t c n"))
    ctx.dma("sync", tiny[:, 84:84 + NT], sc["absr"])
    ctx.wait("vector", "sD")
    for t in range(NT):
        for j, slot in ((0, 68), (1, 69)):
            ctx.op("vector", "scalar_tensor_tensor",
                   out=tiny[:, 16 * j + t:16 * j + t + 1],
                   in0=tiny[:, slot:slot + 1], scalar=-1.0,
                   in1=tiny[:, 84 + t:85 + t], op0=A.mult, op1=A.mult)
    ctx.op("vector", "nop", inc="sV")
    ctx.wait("tensor", "sD")
    ctx.wait("scalar", "sV")

    wg = arb[:, R1:R1 + 4096]
    wu = arb[:, R1 + 4096:R1 + 8192]
    gsil = arf[:, 0:512]
    mid = arf[:, 512:1024]

    for g in range(8):
        ctx.wait("sync", "sT")
        ctx.dma("sync", wg[:].rearrange("p (c n) -> p c n", c=DC),
                p["wgq"].rearrange("(c p) n -> p c n", p=128)[:, :, 512 * g:512 * (g + 1)])
        ctx.dma("sync", wu[:].rearrange("p (c n) -> p c n", c=DC),
                p["wuq"].rearrange("(c p) n -> p c n", p=128)[:, :, 512 * g:512 * (g + 1)])
        ctx.wait("tensor", "sD")
        for t in range(NT):
            bg = (2 * t) % 6
            bu = (2 * t + 1) % 6
            obg = ps[:, 512 * bg:512 * (bg + 1)]
            obu = ps[:, 512 * bu:512 * (bu + 1)]
            ctx.wait("tensor", "sV")
            for c in range(DC):
                xq = arb[:, R0 + 1024 * t + 128 * c:R0 + 1024 * t + 128 * c + 128]
                ctx.op("tensor", "matmul", out=obg, lhsT=xq,
                       rhs=wg[:, 512 * c:512 * (c + 1)],
                       start=(c == 0), stop=(c == DC - 1))
                ctx.op("tensor", "matmul", out=obu, lhsT=xq,
                       rhs=wu[:, 512 * c:512 * (c + 1)],
                       start=(c == 0), stop=(c == DC - 1))
            ctx.op("tensor", "nop", inc="sT")
            ctx.wait("scalar", "sT")
            ctx.wait("scalar", "sV")
            ctx.op("scalar", "activation", out=gsil, in_=obg, func=AF.Silu,
                   bias=0.0, scale=tiny[:, t:t + 1], inc="sC")
            ctx.wait("vector", "sC")
            ctx.wait("vector", "sSp")
            ctx.op("vector", "scalar_tensor_tensor", out=mid, in0=obu,
                   scalar=tiny[:, 16 + t:17 + t], in1=gsil,
                   op0=A.mult, op1=A.mult, inc="sV")
            ctx.wait("sync", "sV")
            ctx.dma("sync", sc["mid"][128 * t:128 * (t + 1), 512 * g:512 * (g + 1)],
                    mid, inc="sSp")
    ctx.tap("mid", sc["mid"])
    ctx.barrier()

    wd_a = arb[:, R1:R1 + RW]
    wd_b = arb[:, R4:R4 + RW]
    ctx.dma("sync", wd_a[:].rearrange("p (c n) -> p c n", c=16),
            p["wdq"].rearrange("(c p) n -> p c n", p=128)[:, 0:16, :])
    ctx.dma("sync", wd_b[:].rearrange("p (c n) -> p c n", c=16),
            p["wdq"].rearrange("(c p) n -> p c n", p=128)[:, 16:32, :])
    ctx.wait("tensor", "sD")

    xq_stg = arb[:, R0:R0 + 4096]
    mfm = arb[:, R0 + 4096:R0 + 8192]
    midb = arf[:, 0:4096]
    vv = arf[:, 4096:8192]
    for t in range(NT):
        ctx.wait("sync", "sV")
        ctx.dma("sync", midb, sc["mid"][128 * t:128 * (t + 1), :])
        ctx.wait("vector", "sD")
        am = tiny[:, 80:81]
        sq = tiny[:, 81:82]
        inv = tiny[:, 82:83]
        combd = tiny[:, 83:84]
        ctx.op("vector", "reduce_max", out=am, in_=midb, axis=AX,
               apply_absolute_value=True)
        ctx.fence("vector")
        ctx.op("vector", "tensor_scalar", out=am, in0=am, scalar1=1e-5,
               scalar2=None, op0=A.max)
        ctx.fence("vector")
        ctx.op("vector", "reciprocal", out=sq, in_=am)
        ctx.fence("vector")
        ctx.op("vector", "tensor_scalar", out=sq, in0=sq, scalar1=127.0,
               scalar2=None, op0=A.mult)
        ctx.fence("vector")
        ctx.op("vector", "reciprocal", out=inv, in_=sq)
        ctx.fence("vector")
        ctx.op("vector", "scalar_tensor_tensor", out=combd, in0=tiny[:, 70:71],
               scalar=-1.0, in1=inv, op0=A.mult, op1=A.mult)
        ctx.op("vector", "tensor_scalar", out=vv, in0=midb, scalar1=sq,
               scalar2=None, op0=A.mult)
        ctx.op("vector", "nop", inc="sV")
        ctx.wait("scalar", "sSp")
        qi = st["qi32"][:]
        for q in range(4):
            vs = vv[:, 1024 * q:1024 * (q + 1)]
            qm1 = midb[:, 1024 * q:1024 * (q + 1)]
            dlt = arf[:, 8192:9216]
            ctx.op("vector", "tensor_copy", out=qi, in_=vs, inc="sV")
            ctx.wait("scalar", "sV")
            ctx.op("scalar", "activation", out=qm1, in_=qi, func=AF.Copy,
                   bias=-1.0, scale=1.0, inc="sC")
            ctx.wait("vector", "sC")
            ctx.op("vector", "tensor_tensor", out=dlt, in0=vs, in1=qm1,
                   op=A.subtract)
            ctx.op("vector", "scalar_tensor_tensor", out=vs, in0=dlt,
                   scalar=1.5, in1=qm1, op0=A.is_ge, op1=A.add)
            ctx.op("vector", "scalar_tensor_tensor",
                   out=xq_stg[:, 1024 * q:1024 * (q + 1)],
                   in0=dlt, scalar=0.5, in1=vs, op0=A.is_ge, op1=A.add,
                   inc="sV")
        ctx.wait("tensor", "sV")
        psb = st["psb"]
        for c in range(IC):
            off = 1024 * ((c % 8) // 4) + 128 * (c % 4)
            ctx.op("tensor", "transpose",
                   out=psb[:, off:off + 128],
                   in_=xq_stg[:, 128 * c:128 * (c + 1)], identity=identb[:],
                   inc="sT" if (c % 8 == 7) else None)
            if c % 8 == 7:
                ctx.wait("vector", "sT")
                for cc in range(c - 7, c + 1):
                    off2 = 1024 * ((cc % 8) // 4) + 128 * (cc % 4)
                    ctx.op("vector", "tensor_copy",
                           out=mfm[:, 128 * cc:128 * (cc + 1)],
                           in_=psb[:, off2:off2 + 128],
                           inc="sV" if cc == c else None)
                ctx.wait("tensor", "sV")
        ob0 = ps[:, 2048:2560]
        ob1 = ps[:, 2560:3072]
        for c in range(IC):
            lt = mfm[:, 128 * c:128 * (c + 1)]
            wslab = wd_a if c < 16 else wd_b
            cc = c % 16
            ctx.op("tensor", "matmul", out=ob0, lhsT=lt,
                   rhs=wslab[:, 1024 * cc:1024 * cc + 512],
                   start=(c == 0), stop=(c == IC - 1))
            ctx.op("tensor", "matmul", out=ob1, lhsT=lt,
                   rhs=wslab[:, 1024 * cc + 512:1024 * cc + 1024],
                   start=(c == 0), stop=(c == IC - 1))
        ctx.op("tensor", "nop", inc="sT")
        ctx.wait("scalar", "sT")
        ctx.wait("scalar", "sSp")
        g0t = arf[:, 8192:9216]
        ctx.op("scalar", "activation", out=g0t[:, 0:512], in_=ob0,
               func=AF.Silu, bias=0.0, scale=combd)
        ctx.op("scalar", "activation", out=g0t[:, 512:1024], in_=ob1,
               func=AF.Silu, bias=0.0, scale=combd, inc="sC")
        ctx.wait("sync", "sC")
        ctx.dma("sync", sc["g0"][128 * t:128 * (t + 1), :], g0t, inc="sSp")
        ctx.op("vector", "nop", inc="sV")
    ctx.tap("g0", sc["g0"])



def phase_D(ctx, st, s):
    arb, arf, tiny, ps, sc, masks, identb = (st["arb"], st["arf"], st["tiny"],
                                             st["ps"], st["sc"], st["masks"],
                                             st["identb"])
    base = s * S
    ctx.dma("sync", arb[:, R0:R0 + RW].rearrange("p (t n) -> p t n", t=NT),
            sc["v0_h"].rearrange("(t p) n -> p t n", p=128))
    ctx.dma("sync", arb[:, R1:R1 + RW].rearrange("p (t n) -> p t n", t=NT),
            sc["v0_l"].rearrange("(t p) n -> p t n", p=128))
    vload = ctx.val("sD")
    qb_h = [arb[:, R4:R4 + 1024], arb[:, R4 + 1024:R4 + 2048]]
    qb_l = [arb[:, R4 + 2048:R4 + 3072], arb[:, R4 + 3072:R4 + 4096]]
    pt_h = arb[:, R4 + 4096:R4 + 6144]
    pt_l = arb[:, R4 + 6144:R4 + 8192]
    ph = arb[:, R4 + 8192:R4 + 10240]
    pl = arb[:, R4 + 10240:R4 + 12288]
    srow = arf[:, 0:2048]
    prow = arf[:, 2048:4096]
    xg = [arf[:, 4096:5120], arf[:, 5120:6144]]
    x1t = arf[:, 6144:7168]

    for t in range(NT):
        qb, ql = qb_h[t % 2], qb_l[t % 2]
        ctx.wait("sync", "sT")
        ctx.dma("sync", qb[:].rearrange("p (c n) -> p c n", c=DC),
                sc["q0_h"].rearrange("(c p) n -> p c n", p=128)[:, :, 128 * t:128 * (t + 1)])
        ctx.dma("sync", ql[:].rearrange("p (c n) -> p c n", c=DC),
                sc["q0_l"].rearrange("(c p) n -> p c n", p=128)[:, :, 128 * t:128 * (t + 1)])
        ctx.wait("sync", "sV")
        ctx.dma("sync", xg[0], st["x_in"][base + 128 * t:base + 128 * (t + 1), :])
        ctx.dma("sync", xg[1], sc["g0"][128 * t:128 * (t + 1), :])
        qv = ctx.val("sD")
        ngrp = (t + 4) // 4
        ctx.wait("tensor", "sD")
        for gi in range(ngrp):
            bank = gi % 2
            ob = ps[:, 512 * bank:512 * (bank + 1)]
            ctx.wait("tensor", "sV")
            for c in range(DC):
                _mm3_acc(ctx,
                         qb[:, 128 * c:128 * (c + 1)], ql[:, 128 * c:128 * (c + 1)],
                         arb[:, R2 + 2048 * c + 512 * gi:R2 + 2048 * c + 512 * (gi + 1)],
                         arb[:, R3 + 2048 * c + 512 * gi:R3 + 2048 * c + 512 * (gi + 1)],
                         ob, first=(c == 0), last=(c == DC - 1))
            ctx.op("tensor", "nop", inc="sT")
            ctx.wait("vector", "sT")
            if gi == ngrp - 1:
                mi = t % 4
                ctx.op("vector", "scalar_tensor_tensor",
                       out=srow[:, 512 * gi:512 * (gi + 1)], in0=ob,
                       scalar=SCALE0, in1=masks[:, 512 * mi:512 * (mi + 1)],
                       op0=A.mult, op1=A.add, inc="sV")
            else:
                ctx.op("vector", "tensor_scalar",
                       out=srow[:, 512 * gi:512 * (gi + 1)], in0=ob,
                       scalar1=SCALE0, scalar2=None, op0=A.mult, inc="sV")
        keylen = 512 * ngrp
        nk = t + 1
        ctx.op("vector", "reduce_max", out=tiny[:, 80:81],
               in_=srow[:, 0:keylen], axis=AX)
        ctx.fence("vector")
        ctx.op("vector", "tensor_scalar", out=tiny[:, 81:82],
               in0=tiny[:, 80:81], scalar1=-1.0, scalar2=None, op0=A.mult,
               inc="sV")
        ctx.wait("scalar", "sV")
        ctx.op("scalar", "activation", out=prow[:, 0:keylen],
               in_=srow[:, 0:keylen], func=AF.Exp, bias=tiny[:, 81:82],
               scale=1.0, accum_out=tiny[:, 82:83], inc="sC")
        ctx.wait("vector", "sC")
        ctx.op("vector", "reciprocal", out=tiny[:, 83:84], in_=tiny[:, 82:83])
        ctx.fence("vector")
        _hl(ctx, ph[:, 0:keylen], pl[:, 0:keylen], prow[:, 0:keylen])
        ctx.op("vector", "nop", inc="sV")
        ctx.wait("tensor", "sV")
        psb = st["psb"]
        for j in range(nk):
            o1 = 1024 * (j % 2)
            ctx.op("tensor", "transpose", out=psb[:, o1:o1 + 128],
                   in_=ph[:, 128 * j:128 * (j + 1)], identity=identb[:])
            ctx.op("tensor", "transpose",
                   out=psb[:, o1 + 128:o1 + 256],
                   in_=pl[:, 128 * j:128 * (j + 1)], identity=identb[:],
                   inc="sT")
            ctx.wait("vector", "sT")
            ctx.op("vector", "tensor_copy", out=pt_h[:, 128 * j:128 * (j + 1)],
                   in_=psb[:, o1:o1 + 128])
            ctx.op("vector", "tensor_copy", out=pt_l[:, 128 * j:128 * (j + 1)],
                   in_=psb[:, o1 + 128:o1 + 256], inc="sV")
            ctx.wait("tensor", "sV")
        ctx.wait("tensor", "sD")
        for b in range(2):
            ob = ps[:, 2048 + 512 * b:2048 + 512 * (b + 1)]
            for j in range(nk):
                _mm3_acc(ctx,
                         pt_h[:, 128 * j:128 * (j + 1)],
                         pt_l[:, 128 * j:128 * (j + 1)],
                         arb[:, R0 + 1024 * j + 512 * b:R0 + 1024 * j + 512 * (b + 1)],
                         arb[:, R1 + 1024 * j + 512 * b:R1 + 1024 * j + 512 * (b + 1)],
                         ob, first=(j == 0), last=(j == nk - 1))
        ctx.op("tensor", "nop", inc="sT")
        ctx.wait("vector", "sT")
        ctx.wait("vector", "sD")
        ctx.wait("vector", "sSp")
        for b in range(2):
            ob = ps[:, 2048 + 512 * b:2048 + 512 * (b + 1)]
            sl = slice(512 * b, 512 * (b + 1))
            ctx.op("vector", "tensor_scalar", out=x1t[:, sl], in0=ob,
                   scalar1=tiny[:, 83:84], scalar2=None, op0=A.mult)
            ctx.op("vector", "tensor_tensor", out=x1t[:, sl], in0=x1t[:, sl],
                   in1=xg[1][:, sl], op=A.mult)
            ctx.op("vector", "tensor_tensor", out=x1t[:, sl], in0=x1t[:, sl],
                   in1=xg[0][:, sl], op=A.add)
        ctx.op("vector", "nop", inc="sV")
        ctx.wait("sync", "sV")
        ctx.dma("sync", sc["x1"][128 * t:128 * (t + 1), :], x1t, inc="sSp")
    ctx.tap("x1", sc["x1"])



def phase_E(ctx, st, s):
    arb, arf, tiny, ps, sc, p, onesf, trow = (st["arb"], st["arf"], st["tiny"],
                                              st["ps"], st["sc"], st["p"],
                                              st["onesf"], st["trow"])
    _ln_fm(ctx, st, sc["x1"], 0, quant=False, xn_tok_dram=sc["xn1"])
    ctx.barrier()

    wrh = arb[:, R4:R4 + 8].rearrange("p (c n) -> p c n", c=DC)
    wrl = arb[:, R4 + 8:R4 + 16].rearrange("p (c n) -> p c n", c=DC)
    ctx.dma("sync", wrh, p["wrT_h"].rearrange("(c p) n -> p c n", p=128),
            allow_slow_non_contiguous=True)
    ctx.dma("sync", wrl, p["wrT_l"].rearrange("(c p) n -> p c n", p=128),
            allow_slow_non_contiguous=True)
    ctx.wait("tensor", "sD")
    for n in range(4):
        ob = ps[0:1, 512 * n:512 * (n + 1)]
        for c in range(DC):
            _mm3_acc(ctx, wrh[:, c, :], wrl[:, c, :],
                     arb[:, R0 + 2048 * c + 512 * n:R0 + 2048 * c + 512 * (n + 1)],
                     arb[:, R1 + 2048 * c + 512 * n:R1 + 2048 * c + 512 * (n + 1)],
                     ob, first=(c == 0), last=(c == DC - 1))
    ctx.op("tensor", "nop", inc="sT")
    ctx.wait("vector", "sT")
    srw = trow[0:1, 0:S]
    ctx.op("vector", "tensor_copy", out=srw, in_=ps[0:1, 0:2048], inc="sV")
    ctx.tap("srow", sc["srow"])

    lo = trow[0:1, S:S + 1]
    hi = trow[0:1, S + 1:S + 2]
    mid = trow[0:1, S + 2:S + 3]
    cnt = trow[0:1, S + 3:S + 4]
    pred = trow[0:1, S + 4:S + 5]
    tmp = trow[0:1, S + 5:S + 6]
    cmpb = trow[0:1, S + 8:S + 8 + S]
    tmp2 = trow[0:1, S + 6:S + 7]
    ctx.op("vector", "reduce_max", out=hi, in_=srw, axis=AX,
           apply_absolute_value=True)
    ctx.fence("vector")
    ctx.op("vector", "tensor_scalar", out=hi, in0=hi, scalar1=1.0,
           scalar2=None, op0=A.add)
    ctx.fence("vector")
    ctx.op("vector", "tensor_scalar", out=lo, in0=hi, scalar1=-1.0,
           scalar2=None, op0=A.mult)
    ctx.fence("vector")
    ctx.op("vector", "tensor_tensor", out=mid, in0=lo, in1=hi, op=A.add)
    ctx.fence("vector")
    ctx.op("vector", "tensor_scalar", out=mid, in0=mid, scalar1=0.5,
           scalar2=None, op0=A.mult)
    ctx.fence("vector")
    K = S // 2
    for _ in range(44):
        ctx.op("vector", "tensor_scalar", out=cmpb, in0=srw, scalar1=mid,
               scalar2=None, op0=A.is_ge)
        ctx.op("vector", "reduce_sum", out=cnt, in_=cmpb, axis=AX)
        ctx.op("vector", "tensor_tensor", out=tmp, in0=mid, in1=lo,
               op=A.subtract)
        ctx.op("vector", "tensor_tensor", out=tmp2, in0=hi, in1=mid,
               op=A.subtract)
        ctx.fence("vector")
        ctx.op("vector", "scalar_tensor_tensor", out=pred, in0=cnt,
               scalar=float(K), in1=tmp, op0=A.is_ge, op1=A.mult)
        ctx.op("vector", "scalar_tensor_tensor", out=tmp2, in0=cnt,
               scalar=float(K), in1=tmp2, op0=A.is_ge, op1=A.mult)
        ctx.fence("vector")
        ctx.op("vector", "tensor_tensor", out=lo, in0=lo, in1=pred, op=A.add)
        ctx.op("vector", "tensor_tensor", out=hi, in0=mid, in1=tmp2, op=A.add)
        ctx.fence("vector")
        ctx.op("vector", "tensor_tensor", out=mid, in0=lo, in1=hi, op=A.add)
        ctx.fence("vector")
        ctx.op("vector", "tensor_scalar", out=mid, in0=mid, scalar1=0.5,
               scalar2=None, op0=A.mult)
        ctx.fence("vector")
    ctx.op("vector", "nop", inc="sV")
    ctx.wait("tensor", "sV")
    ctx.op("tensor", "matmul", out=ps[:, 2560:2561], lhsT=onesf[:], rhs=lo,
           start=True, stop=True, inc="sT")
    ctx.wait("sync", "sV")
    ctx.dma("sync", sc["srow"].rearrange("(o t) p -> o t p", o=1),
            srw.rearrange("o (t p) -> o t p", p=128), inc="sSp")
    ctx.wait("sync", "sSp")
    ctx.dma("sync", arf[0:16, 3072:3200], sc["srow"])
    sv = ctx.val("sD")
    ctx.wait("tensor", "sD")
    ctx.op("tensor", "transpose", out=ps[:, 1536:1552],
           in_=arf[0:16, 3072:3200], identity=st["identf"][0:16, 0:16],
           inc="sT")
    ctx.wait("vector", "sT")
    ctx.op("vector", "tensor_copy", out=tiny[:, 16:16 + NT],
           in_=ps[:, 1536:1552])
    ctx.fence("vector")
    thr = tiny[:, 32:33]
    ctx.op("vector", "tensor_copy", out=thr, in_=ps[:, 2560:2561], inc="sV")
    ctx.fence("vector")
    ctx.wait("scalar", "sV")
    sig = tiny[:, 48:48 + NT]
    ctx.op("scalar", "activation", out=sig, in_=tiny[:, 16:16 + NT],
           func=AF.Sigmoid, inc="sC")
    ctx.wait("vector", "sC")
    msk = tiny[:, 0:NT]
    ctx.op("vector", "tensor_scalar", out=msk, in0=tiny[:, 16:16 + NT],
           scalar1=thr, scalar2=None, op0=A.is_ge)
    ctx.fence("vector")
    ctx.op("vector", "tensor_tensor", out=msk, in0=msk, in1=sig, op=A.mult,
           inc="sV")
    ctx.tap("wtok", msk)



def phase_F(ctx, st, s):
    arb, arf, tiny, ps, sc, p, identf = (st["arb"], st["arf"], st["tiny"],
                                         st["ps"], st["sc"], st["p"],
                                         st["identf"])
    cin = st["cin"]
    gl = arf[:, 0:512]
    upd = arf[:, 512:1024]
    updtok = arf[:, 1024:5120]
    x1g = arf[:, 5120:6144]
    b1fm = arf[:, 6144:6176]
    b2fm = arf[:, 6176:6184]
    g1t = arf[:, 6208:7232]
    g1o = arf[:, 7232:8256]
    ctx.dma("sync", b1fm, cin["b1fm"])
    ctx.dma("sync", b2fm, cin["b2fm"])
    w1h = [arb[:, R4:R4 + 1024], arb[:, R4 + 1024:R4 + 2048]]
    w1l = [arb[:, R4 + 2048:R4 + 3072], arb[:, R4 + 3072:R4 + 4096]]
    w2h = arb[:, R4 + 4096:R4 + 8192]
    w2l = arb[:, R4 + 8192:R4 + 12288]
    hfm_h = arb[:, R2:R2 + RW]
    hfm_l = arb[:, R3:R3 + RW]
    ctx.wait("tensor", "sD")
    ctx.wait("scalar", "sD")

    for grp in range(4):
        for fc in range(IC):
            pb = fc % 2
            ctx.wait("sync", "sT")
            ctx.dma("sync", w1h[pb][:].rearrange("p (c n) -> p c n", c=DC),
                    p["W1_h"].rearrange("(c p) n -> p c n", p=128)[:, :, 128 * fc:128 * (fc + 1)])
            ctx.dma("sync", w1l[pb][:].rearrange("p (c n) -> p c n", c=DC),
                    p["W1_l"].rearrange("(c p) n -> p c n", p=128)[:, :, 128 * fc:128 * (fc + 1)])
            ctx.wait("tensor", "sD")
            bank = fc % 2
            ob = ps[:, 512 * bank:512 * (bank + 1)]
            ctx.wait("tensor", "sV")
            for c in range(DC):
                _mm3_acc(ctx,
                         w1h[pb][:, 128 * c:128 * (c + 1)],
                         w1l[pb][:, 128 * c:128 * (c + 1)],
                         arb[:, R0 + 2048 * c + 512 * grp:R0 + 2048 * c + 512 * (grp + 1)],
                         arb[:, R1 + 2048 * c + 512 * grp:R1 + 2048 * c + 512 * (grp + 1)],
                         ob, first=(c == 0), last=(c == DC - 1))
            ctx.op("tensor", "nop", inc="sT")
            ctx.wait("scalar", "sT")
            ctx.wait("scalar", "sV")
            ctx.op("scalar", "activation", out=gl, in_=ob,
                   func=AF.Gelu_apprx_tanh, bias=b1fm[:, fc:fc + 1],
                   scale=1.0, inc="sC")
            ctx.wait("vector", "sC")
            _hl(ctx, hfm_h[:, 512 * fc:512 * (fc + 1)],
                hfm_l[:, 512 * fc:512 * (fc + 1)], gl)
            ctx.op("vector", "nop", inc="sV")
        for dc in range(DC):
            ctx.wait("sync", "sT")
            ctx.dma("sync", w2h[:].rearrange("p (c n) -> p c n", c=IC),
                    p["W2_h"].rearrange("(c p) n -> p c n", p=128)[:, :, 128 * dc:128 * (dc + 1)])
            ctx.dma("sync", w2l[:].rearrange("p (c n) -> p c n", c=IC),
                    p["W2_l"].rearrange("(c p) n -> p c n", p=128)[:, :, 128 * dc:128 * (dc + 1)])
            ctx.wait("tensor", "sD")
            bank = 2 + dc % 2
            ob = ps[:, 512 * bank:512 * (bank + 1)]
            ctx.wait("tensor", "sC")
            ctx.wait("tensor", "sV")
            for c in range(IC):
                _mm3_acc(ctx,
                         w2h[:, 128 * c:128 * (c + 1)],
                         w2l[:, 128 * c:128 * (c + 1)],
                         hfm_h[:, 512 * c:512 * (c + 1)],
                         hfm_l[:, 512 * c:512 * (c + 1)],
                         ob, first=(c == 0), last=(c == IC - 1))
            ctx.op("tensor", "nop", inc="sT")
            ctx.wait("vector", "sT")
            ctx.op("vector", "tensor_scalar", out=upd, in0=ob,
                   scalar1=b2fm[:, dc:dc + 1], scalar2=None, op0=A.add,
                   inc="sC")
            ctx.wait("tensor", "sC")
            for j in range(4):
                bk = 4 + j % 2
                ctx.op("tensor", "transpose", out=ps[:, 512 * bk:512 * bk + 128],
                       in_=upd[:, 128 * j:128 * (j + 1)], identity=identf[:],
                       inc="sT")
                ctx.wait("vector", "sT")
                ctx.op("vector", "tensor_copy",
                       out=updtok[:, 1024 * j + 128 * dc:1024 * j + 128 * (dc + 1)],
                       in_=ps[:, 512 * bk:512 * bk + 128], inc="sV")
                ctx.wait("tensor", "sV")
        for j in range(4):
            t = 4 * grp + j
            ctx.wait("sync", "sV")
            ctx.dma("sync", x1g, sc["xn1"][128 * t:128 * (t + 1), :])
            ctx.wait("vector", "sD")
            ctx.op("vector", "scalar_tensor_tensor", out=g1t,
                   in0=updtok[:, 1024 * j:1024 * (j + 1)],
                   scalar=tiny[:, t:t + 1], in1=x1g, op0=A.mult, op1=A.add,
                   inc="sV")
            ctx.wait("scalar", "sV")
            ctx.wait("scalar", "sSp")
            ctx.op("scalar", "activation", out=g1o, in_=g1t, func=AF.Silu,
                   inc="sC")
            ctx.wait("sync", "sC")
            ctx.dma("sync", sc["g1"][128 * t:128 * (t + 1), :], g1o, inc="sSp")
    ctx.tap("g1", sc["g1"])



def phase_G(ctx, st, s):
    arb, arf, ps, sc, p = st["arb"], st["arf"], st["ps"], st["sc"], st["p"]
    ev_h = st["stg"][:, 0:512]
    ev_l = st["stg"][:, 512:1024]
    wch = [arb[:, R4:R4 + 1024], arb[:, R4 + 1024:R4 + 2048]]
    wcl = [arb[:, R4 + 2048:R4 + 3072], arb[:, R4 + 3072:R4 + 4096]]

    for m in range(16):
        pb = m % 2
        ctx.wait("sync", "sT")
        ctx.dma("sync", wch[pb][:].rearrange("p (c n) -> p c n", c=DC),
                p["wqkvT_h"].rearrange("(c p) n -> p c n", p=128)[:, :, 128 * m:128 * (m + 1)])
        ctx.dma("sync", wcl[pb][:].rearrange("p (c n) -> p c n", c=DC),
                p["wqkvT_l"].rearrange("(c p) n -> p c n", p=128)[:, :, 128 * m:128 * (m + 1)])
        ctx.wait("tensor", "sD")
        dst_h = sc["q1_h"] if m < 8 else sc["k1_h"]
        dst_l = sc["q1_l"] if m < 8 else sc["k1_l"]
        mm = m % 8
        for n in range(4):
            bank = n % 4
            ob = ps[:, 512 * bank:512 * (bank + 1)]
            ctx.wait("tensor", "sV")
            for c in range(DC):
                _mm3_acc(ctx, wch[pb][:, 128 * c:128 * (c + 1)],
                         wcl[pb][:, 128 * c:128 * (c + 1)],
                         arb[:, R0 + 2048 * c + 512 * n:R0 + 2048 * c + 512 * (n + 1)],
                         arb[:, R1 + 2048 * c + 512 * n:R1 + 2048 * c + 512 * (n + 1)],
                         ob, first=(c == 0), last=(c == DC - 1))
            ctx.op("tensor", "nop", inc="sT")
            ctx.wait("vector", "sT")
            ctx.wait("vector", "sSp")
            _hl(ctx, ev_h[:, 0:512], ev_l[:, 0:512], ob)
            ctx.op("vector", "nop", inc="sV")
            ctx.wait("sync", "sV")
            ctx.dma("sync", dst_h[128 * mm:128 * (mm + 1), 512 * n:512 * (n + 1)],
                    ev_h[:, 0:512], inc="sSp")
            ctx.dma("sync", dst_l[128 * mm:128 * (mm + 1), 512 * n:512 * (n + 1)],
                    ev_l[:, 0:512], inc="sSp")
    vwh = arb[:, R4:R4 + 8192]
    vwl = arb[:, R4 + 8192:R4 + 16384]
    ctx.wait("sync", "sT")
    ctx.dma("sync", vwh[:].rearrange("p (c n) -> p c n", c=DC),
            p["wqkvT_h"].rearrange("(c p) n -> p c n", p=128)[:, :, 2048:3072])
    ctx.dma("sync", vwl[:].rearrange("p (c n) -> p c n", c=DC),
            p["wqkvT_l"].rearrange("(c p) n -> p c n", p=128)[:, :, 2048:3072])
    ctx.wait("tensor", "sD")
    for t in range(NT):
        for b in range(2):
            bank = (t * 2 + b) % 4
            ob = ps[:, 512 * bank:512 * (bank + 1)]
            ctx.wait("tensor", "sV")
            for c in range(DC):
                xh = arb[:, R0 + 2048 * c + 128 * t:R0 + 2048 * c + 128 * t + 128]
                xl = arb[:, R1 + 2048 * c + 128 * t:R1 + 2048 * c + 128 * t + 128]
                _mm3_acc(ctx, xh, xl,
                         vwh[:, 1024 * c + 512 * b:1024 * c + 512 * (b + 1)],
                         vwl[:, 1024 * c + 512 * b:1024 * c + 512 * (b + 1)],
                         ob, first=(c == 0), last=(c == DC - 1))
            ctx.op("tensor", "nop", inc="sT")
            ctx.wait("vector", "sT")
            a2 = R2 + 1024 * t + 512 * b
            a3 = R3 + 1024 * t + 512 * b
            _hl(ctx, arb[:, a2:a2 + 512], arb[:, a3:a3 + 512], ob)
            ctx.op("vector", "nop", inc="sV")



def phase_H(ctx, st, s):
    arb, arf, tiny, ps, sc, p, identb = (st["arb"], st["arf"], st["tiny"],
                                         st["ps"], st["sc"], st["p"],
                                         st["identb"])
    onesf, brow = st["onesf"], st["brow"]
    cin = st["cin"]
    base = s * S
    pj_h = arb[:, R4:R4 + 1024]
    pj_l = arb[:, R4 + 1024:R4 + 2048]
    ctx.dma("sync", pj_h[:].rearrange("p (c n) -> p c n", c=DC),
            p["projT_h"].rearrange("(c p) n -> p c n", p=128))
    ctx.dma("sync", pj_l[:].rearrange("p (c n) -> p c n", c=DC),
            p["projT_l"].rearrange("(c p) n -> p c n", p=128))
    ctx.dma("sync", brow[:], cin["boutr"])
    hr = R4 + 2048
    qh_h = arb[:, hr:hr + 2048]
    qh_l = arb[:, hr + 2048:hr + 4096]
    kh_h = arb[:, hr + 4096:hr + 6144]
    kh_l = arb[:, hr + 6144:hr + 8192]
    qp_h, qp_l, kp_h, kp_l = qh_h, qh_l, kh_h, kh_l
    at_h = arb[:, R4 + 10240:R4 + 10368]
    at_l = arb[:, R4 + 10368:R4 + 10496]
    ks_h = arb[:, R4 + 10496:R4 + 10560]
    ks_l = arb[:, R4 + 10560:R4 + 10624]
    an_h = arb[:, R4 + 10624:R4 + 10752]
    an_l = arb[:, R4 + 10752:R4 + 10880]
    qpf = arf[:, 0:2048]
    kpf = arf[:, 2048:4096]
    ksf = arf[:, 4096:4160]
    zbuf = arf[:, 4160:4176]
    atf = arf[:, 4304:4432]
    att = arf[:, 4432:4560]
    rlu = arf[:, 4560:5072]
    ctx.wait("tensor", "sD")

    for h in range(HEADS):
        ctx.wait("sync", "sT")
        ctx.dma("sync", qh_h, sc["q1_h"][128 * h:128 * (h + 1), :])
        ctx.dma("sync", qh_l, sc["q1_l"][128 * h:128 * (h + 1), :])
        ctx.dma("sync", kh_h, sc["k1_h"][128 * h:128 * (h + 1), :])
        ctx.dma("sync", kh_l, sc["k1_l"][128 * h:128 * (h + 1), :])
        ctx.wait("tensor", "sD")
        for src_h, src_l, dst in ((qh_h, qh_l, qpf), (kh_h, kh_l, kpf)):
            for n in range(4):
                bank = n % 4
                ob = ps[:, 512 * bank:512 * (bank + 1)]
                ctx.wait("tensor", "sV")
                _mm3_acc(ctx, pj_h[:, 128 * h:128 * (h + 1)],
                         pj_l[:, 128 * h:128 * (h + 1)],
                         src_h[:, 512 * n:512 * (n + 1)],
                         src_l[:, 512 * n:512 * (n + 1)],
                         ob, first=True, last=True)
                ctx.op("tensor", "nop", inc="sT")
                ctx.wait("vector", "sT")
                sl = slice(512 * n, 512 * (n + 1))
                ctx.op("vector", "tensor_scalar", out=dst[:, sl], in0=ob,
                       scalar1=0.0, scalar2=None, op0=A.min, inc="sV")
                ctx.wait("scalar", "sV")
                ctx.op("scalar", "activation", out=dst[:, sl], in_=dst[:, sl],
                       func=AF.Exp)
                ctx.op("scalar", "activation", out=rlu, in_=ob, func=AF.Relu,
                       inc="sC")
                ctx.wait("vector", "sC")
                ctx.op("vector", "tensor_tensor", out=dst[:, sl],
                       in0=dst[:, sl], in1=rlu, op=A.add, inc="sV")
        _hl(ctx, qp_h, qp_l, qpf)
        _hl(ctx, kp_h, kp_l, kpf)
        ctx.op("vector", "tensor_reduce", out=ksf,
               in_=kpf.rearrange("p (b s) -> p b s", s=BLK), axis=AX, op=A.add)
        ctx.fence("vector")
        _hl(ctx, ks_h, ks_l, ksf, fence=True)
        ctx.op("vector", "nop", inc="sV")
        ctx.wait("tensor", "sV")
        for gi in range(NT):
            c0 = 128 * gi
            obA = ps[:, 2048:2176]
            ctx.op("tensor", "matmul", out=obA, lhsT=kp_h[:, c0:c0 + 128],
                   rhs=qp_h[:, c0:c0 + 128], start=True, stop=False)
            ctx.op("tensor", "matmul", out=obA, lhsT=kp_h[:, c0:c0 + 128],
                   rhs=qp_l[:, c0:c0 + 128], start=False, stop=False)
            ctx.op("tensor", "matmul", out=obA, lhsT=kp_l[:, c0:c0 + 128],
                   rhs=qp_h[:, c0:c0 + 128], start=False, stop=True)
            for b in range(4):
                zo = ps[32 * b:32 * (b + 1), 2560:2561]
                tp = (0, 32 * b)
                ctx.op("tensor", "matmul", out=zo,
                       lhsT=qp_h[:, c0 + 32 * b:c0 + 32 * (b + 1)],
                       rhs=ks_h[:, 4 * gi + b:4 * gi + b + 1],
                       tile_position=tp, start=True, stop=False)
                ctx.op("tensor", "matmul", out=zo,
                       lhsT=qp_h[:, c0 + 32 * b:c0 + 32 * (b + 1)],
                       rhs=ks_l[:, 4 * gi + b:4 * gi + b + 1],
                       tile_position=tp, start=False, stop=False)
                ctx.op("tensor", "matmul", out=zo,
                       lhsT=qp_l[:, c0 + 32 * b:c0 + 32 * (b + 1)],
                       rhs=ks_h[:, 4 * gi + b:4 * gi + b + 1],
                       tile_position=tp, start=False, stop=True)
            ctx.op("tensor", "nop", inc="sT")
            ctx.wait("vector", "sT")
            ctx.op("vector", "tensor_copy", out=atf, in_=obA)
            ctx.fence("vector")
            _hl(ctx, at_h, at_l, atf, fence=True)
            ctx.op("vector", "tensor_scalar", out=zbuf[:, 0:1],
                   in0=ps[:, 2560:2561], scalar1=1e-8, scalar2=None, op0=A.add)
            ctx.fence("vector")
            ctx.op("vector", "reciprocal", out=zbuf[:, 1:2], in_=zbuf[:, 0:1],
                   inc="sV")
            ctx.fence("vector")
            ctx.wait("tensor", "sV")
            obN = ps[:, 0:128]
            for b in range(4):
                vh = arb[32 * b:32 * (b + 1),
                         R2 + 1024 * gi + 128 * h:R2 + 1024 * gi + 128 * (h + 1)]
                vl = arb[32 * b:32 * (b + 1),
                         R3 + 1024 * gi + 128 * h:R3 + 1024 * gi + 128 * (h + 1)]
                ah = at_h[32 * b:32 * (b + 1), 32 * b:32 * (b + 1)]
                al = at_l[32 * b:32 * (b + 1), 32 * b:32 * (b + 1)]
                on = obN[32 * b:32 * (b + 1), :]
                tp = (32 * b, 32 * b)
                ctx.op("tensor", "matmul", out=on, lhsT=ah, rhs=vh,
                       tile_position=tp, start=True, stop=False)
                ctx.op("tensor", "matmul", out=on, lhsT=ah, rhs=vl,
                       tile_position=tp, start=False, stop=False)
                ctx.op("tensor", "matmul", out=on, lhsT=al, rhs=vh,
                       tile_position=tp, start=False, stop=True)
            ctx.op("tensor", "nop", inc="sT")
            ctx.wait("vector", "sT")
            ctx.op("vector", "tensor_scalar", out=att, in0=obN,
                   scalar1=zbuf[:, 1:2], scalar2=None, op0=A.mult)
            ctx.fence("vector")
            _hl(ctx, an_h, an_l, att, fence=True)
            ctx.op("vector", "nop", inc="sV")
            ctx.wait("tensor", "sV")
            psb = st["psb"]
            ctx.op("tensor", "transpose", out=psb[:, 0:128],
                   in_=an_h, identity=identb[:])
            ctx.op("tensor", "transpose", out=psb[:, 128:256],
                   in_=an_l, identity=identb[:], inc="sT")
            ctx.wait("vector", "sT")
            a0 = R0 + 2048 * h + 128 * gi
            a1 = R1 + 2048 * h + 128 * gi
            ctx.op("vector", "tensor_copy", out=arb[:, a0:a0 + 128],
                   in_=psb[:, 0:128])
            ctx.op("vector", "tensor_copy", out=arb[:, a1:a1 + 128],
                   in_=psb[:, 128:256], inc="sV")
            ctx.wait("tensor", "sV")
    ctx.tap("attnfm_hi", arb[:, R0:R0 + RW])

    woh = arb[:, R4:R4 + 8192]
    wol = arb[:, R4 + 8192:R4 + 16384]
    ctx.wait("sync", "sT")
    ctx.dma("sync", woh[:].rearrange("p (c n) -> p c n", c=DC),
            p["woT_h"].rearrange("(c p) n -> p c n", p=128))
    ctx.dma("sync", wol[:].rearrange("p (c n) -> p c n", c=DC),
            p["woT_l"].rearrange("(c p) n -> p c n", p=128))
    ctx.wait("tensor", "sD")
    x1t = arf[:, 0:1024]
    g1t = arf[:, 1024:2048]
    ot = arf[:, 2048:3072]
    for t in range(NT):
        ctx.wait("sync", "sV")
        ctx.dma("sync", x1t, sc["x1"][128 * t:128 * (t + 1), :])
        ctx.dma("sync", g1t, sc["g1"][128 * t:128 * (t + 1), :])
        dv = ctx.val("sD")
        for b in range(2):
            ob = ps[:, 512 * b:512 * (b + 1)]
            ctx.wait("tensor", "sV")
            ctx.op("tensor", "matmul", out=ob, lhsT=onesf[:],
                   rhs=brow[0:1, 512 * b:512 * (b + 1)], start=True, stop=False)
            for c in range(DC):
                _mm3_acc(ctx,
                         arb[:, R0 + 2048 * c + 128 * t:R0 + 2048 * c + 128 * t + 128],
                         arb[:, R1 + 2048 * c + 128 * t:R1 + 2048 * c + 128 * t + 128],
                         woh[:, 1024 * c + 512 * b:1024 * c + 512 * (b + 1)],
                         wol[:, 1024 * c + 512 * b:1024 * c + 512 * (b + 1)],
                         ob, first=False, last=(c == DC - 1))
            ctx.op("tensor", "nop", inc="sT")
        ctx.wait("vector", "sT")
        ctx.wait("vector", "sD")
        ctx.wait("vector", "sSp")
        for b in range(2):
            ob = ps[:, 512 * b:512 * (b + 1)]
            sl = slice(512 * b, 512 * (b + 1))
            ctx.op("vector", "tensor_tensor", out=ot[:, sl], in0=ob,
                   in1=g1t[:, sl], op=A.mult)
            ctx.op("vector", "tensor_tensor", out=ot[:, sl], in0=ot[:, sl],
                   in1=x1t[:, sl], op=A.add)
        ctx.op("vector", "nop", inc="sV")
        ctx.wait("sync", "sV")
        ctx.dma("sync", st["out"][base + 128 * t:base + 128 * (t + 1), :], ot,
                inc="sSp")



def _mk_consts():
    identf = np.eye(128, dtype=np.float32)
    identb = np.eye(128, dtype=ml_dtypes.bfloat16)
    onesf = np.ones((1, 128), dtype=np.float32)
    onescol = np.ones((128, 1), dtype=np.float32)
    masks = np.zeros((128, 4 * 512), dtype=np.float32)
    for pp in range(4):
        m = np.zeros((128, 512), dtype=np.float32)
        m[:, 128 * (pp + 1):] = NEG
        m[:, 128 * pp:128 * (pp + 1)] = np.triu(
            np.full((128, 128), NEG, dtype=np.float32), 1)
        masks[:, 512 * pp:512 * (pp + 1)] = m
    return identf, identb, onesf, onescol, masks


def kernel(L0_attn_x, L0_attn_WQ, L0_attn_WK, L0_attn_WV, L0_ffn_w_gate,
           L0_ffn_w_up, L0_ffn_w_down, L1_attn_w_qkv, L1_attn_w_out,
           L1_attn_b_out, L1_attn_proj, L1_ffn_w_router, L1_ffn_W1, L1_ffn_b1,
           L1_ffn_W2, L1_ffn_b2, _dbg=None, _cores=None, _stop=None):
    x = np.ascontiguousarray(np.asarray(L0_attn_x, dtype=np.float32))
    identf, identb, onesf, onescol, masks = _mk_consts()
    T = lambda a: np.ascontiguousarray(np.asarray(a, dtype=np.float32).T)
    C = lambda a: np.ascontiguousarray(np.asarray(a, dtype=np.float32))
    w = dict(
        wqT=T(L0_attn_WQ), wkT=T(L0_attn_WK), wvT=T(L0_attn_WV),
        wgT=T(L0_ffn_w_gate), wuT=T(L0_ffn_w_up), wdT=T(L0_ffn_w_down),
        wqkvT=T(L1_attn_w_qkv), woT=T(L1_attn_w_out),
        projT=C(L1_attn_proj).reshape(D, FD),
        W1=C(L1_ffn_W1), W2=C(L1_ffn_W2),
        wrT=C(L1_ffn_w_router).reshape(D, 1),
        identf=identf, identb=identb, onesf=onesf, onescol=onescol,
        masks=masks.astype(ml_dtypes.bfloat16),
        b1fm=np.ascontiguousarray(C(L1_ffn_b1).reshape(IC, 128).T),
        b2fm=np.ascontiguousarray(C(L1_ffn_b2).reshape(DC, 128).T),
        boutr=C(L1_attn_b_out).reshape(1, D),
        bconst=np.full((128, 1), 1e-5, dtype=np.float32),
    )
    nc = build_core(dbg=_dbg, stop=_stop)
    ncores = NCORE if _cores is None else _cores
    in_maps = []
    for c in range(ncores):
        m = dict(w)
        m["x"] = np.ascontiguousarray(x[NSEQ * c:NSEQ * (c + 1)].reshape(NSEQ * S, D))
        in_maps.append(m)
    res = run_bass_kernel_spmd(nc, in_maps, core_ids=list(range(ncores)))
    if _dbg:
        return res
    outs = [np.asarray(r["out"]).reshape(NSEQ, S, D) for r in res.results]
    return np.concatenate(outs, axis=0).astype(np.float32)


# version 05
# speedup vs baseline: 1.1076x; 1.1043x over previous
import numpy as np
import ml_dtypes
import concourse.bass as bass
import concourse.mybir as mybir
from concourse.bass_utils import run_bass_kernel_spmd
from concourse.alu_op_type import AluOpType as A

F32 = mybir.dt.float32
BF16 = mybir.dt.bfloat16
I32 = mybir.dt.int32
AX = mybir.AxisListType.X
AF = mybir.ActivationFunctionType

S = 2048
D = 1024
DC = D // 128
NT = S // 128
INNER = 4096
IC = INNER // 128
HEADS = 8
HD = 128
FD = 128
BLK = 32
NSEQ = 2
NCORE = 8
SCALE0 = 1024 ** -0.5
NEG = -1.0e30

RW = 16384
R0, R1, R2, R3, R4 = 0, RW, 2 * RW, 3 * RW, 4 * RW


class Ctx:
    ENGS = ("sync", "gpsimd", "vector", "scalar", "tensor")

    def __init__(self, nc, dbg):
        self.nc = nc
        self.dbg = set(dbg or ())
        self.prog = {e: [] for e in self.ENGS}
        self.cnt = {}
        self.sems = {}
        self.dbg_out = {}
        self.waited = {}
        self.prior_waited = {}
        self.touch = {"vector": [], "scalar": []}

    def emit(self, eng, fn):
        self.prog[eng].append(fn)

    def inc(self, sem, amt=1):
        self.cnt[sem] = self.cnt.get(sem, 0) + amt
        return self.cnt[sem]

    def val(self, sem):
        return self.cnt.get(sem, 0)

    def op(self, eng, meth, *args, inc=None, inc_amt=1, **kw):
        sems = self.sems
        if inc is not None:
            self.inc(inc, inc_amt)
        if eng == "tensor" and meth == "nop" and inc is not None:
            meth = "drain"
        fence = inc is not None and eng in ("vector", "scalar")
        pre_drain = False
        if eng in ("vector", "scalar") and meth not in ("nop", "drain"):
            rd, wr = _op_aps(kw)
            hist = self.touch[eng]
            if _conflicts(rd, wr, hist):
                pre_drain = True
                hist.clear()
            hist.extend((a, False) for a in rd)
            hist.extend((a, True) for a in wr)
            if fence:
                hist.clear()

        skip_op = eng in ("vector", "scalar") and meth == "nop" and inc is not None

        def fn(e, meth=meth, args=args, kw=kw, inc=inc, inc_amt=inc_amt,
               fence=fence, pre_drain=pre_drain, skip_op=skip_op):
            if pre_drain:
                e.drain()
            if skip_op:
                e.drain()
                e.drain().then_inc(sems[inc], inc_amt)
                return
            r = getattr(e, meth)(*args, **kw)
            if inc is not None:
                if fence:
                    e.drain().then_inc(sems[inc], inc_amt)
                else:
                    r.then_inc(sems[inc], inc_amt)
        self.emit(eng, fn)

    def fence(self, eng):
        self.touch[eng].clear()
        sems = self.sems
        n = self.inc("f" + eng)

        def fn(e, n=n, s="f" + eng):
            e.drain()
            e.drain().then_inc(sems[s], 1)
            e.wait_ge(sems[s], n)
        self.emit(eng, fn)

    def wait(self, eng, sem, v=None):
        v = self.val(sem) if v is None else v
        self.waited.setdefault(sem, set()).add(v)
        sems = self.sems
        self.emit(eng, lambda e, s=sem, v=v: e.wait_ge(sems[s], v))

    def dma(self, eng, out, in_, inc="sD", **kw):
        if self.val(inc) in self.prior_waited.get(inc, ()):  
            self.wait(eng, inc)
        self.op(eng, "dma_start", inc=inc, inc_amt=16, out=out, in_=in_, **kw)

    def barrier(self):
        self.wait("sync", "sD")
        self.wait("sync", "sSp")
        self.wait("gpsimd", "sG")
        for e in self.ENGS:
            self.op(e, "nop", inc="b" + e)
        for e in self.ENGS:
            for o in self.ENGS:
                if o != e:
                    self.wait(e, "b" + o)

    def tap(self, name, src_ap):
        if name not in self.dbg:
            return
        self.barrier()
        self.dma("sync", self.dbg_out[name], src_ap)
        self.barrier()


def _ap_region(x):
    try:
        apl = x.ap
        off = x.offset
        nm = x.name
    except AttributeError:
        return None
    if not apl:
        return (nm, 0, 1 << 60, 0, 128)
    rowstep = apl[0][0]
    if rowstep <= 0:
        return (nm, 0, 1 << 60, 0, 128)
    p0 = off // rowstep
    c0 = off % rowstep
    span = 1
    for step, count in apl[1:]:
        span += step * (count - 1)
    return (nm, c0, c0 + span, p0, p0 + apl[0][1])


def _op_aps(kw):
    rd, wr = [], []
    for k, v in kw.items():
        r = _ap_region(v) if not isinstance(v, (int, float, bool, type(None))) else None
        if r is None:
            continue
        if k in ("out", "accum_out"):
            wr.append(r)
        else:
            rd.append(r)
    return rd, wr


def _overlap(a, b):
    return (a[0] == b[0] and a[1] < b[2] and b[1] < a[2]
            and a[3] < b[4] and b[3] < a[4])


def _conflicts(rd, wr, hist):
    for h, h_is_wr in hist:
        if h_is_wr:
            for a in rd + wr:
                if _overlap(a, h):
                    return True
        else:
            for a in wr:
                if _overlap(a, h):
                    return True
    return False


class Pipe:

    def __init__(self, ctx, sem="sV", lag=4, eng="tensor"):
        self.ctx, self.sem, self.lag, self.eng = ctx, sem, lag, eng
        self.hist = []

    def pre(self):
        if len(self.hist) >= self.lag:
            self.ctx.wait(self.eng, self.sem, self.hist[-self.lag])

    def post(self):
        self.hist.append(self.ctx.val(self.sem))


def _hl(ctx, dst_hi, dst_lo, src_f32, fence=False):
    ctx.op("vector", "tensor_copy", out=dst_hi, in_=src_f32)
    if fence:
        ctx.fence("vector")
    ctx.op("vector", "tensor_tensor", out=dst_lo, in0=src_f32, in1=dst_hi,
           op=A.subtract)


def _mm3_acc(ctx, lhsT_h, lhsT_l, rhs_h, rhs_l, out, first, last):
    ctx.op("tensor", "matmul", out=out, lhsT=lhsT_h, rhs=rhs_h,
           start=first, stop=False)
    ctx.op("tensor", "matmul", out=out, lhsT=lhsT_h, rhs=rhs_l,
           start=False, stop=False)
    ctx.op("tensor", "matmul", out=out, lhsT=lhsT_l, rhs=rhs_h,
           start=False, stop=last)


def build_core(dbg=None, stop=None):
    nc = bass.Bass("TRN2", target_bir_lowering=False)
    ctx = Ctx(nc, dbg)

    x_in = nc.dram_tensor("x", [NSEQ * S, D], F32, kind="ExternalInput").ap()
    out = nc.dram_tensor("out", [NSEQ * S, D], F32, kind="ExternalOutput").ap()

    win = {}
    for nm, shp in [
        ("wqT", [D, D]), ("wkT", [D, D]), ("wvT", [D, D]),
        ("wgT", [D, INNER]), ("wuT", [D, INNER]), ("wdT", [INNER, D]),
        ("wqkvT", [D, 3 * D]), ("woT", [D, D]), ("projT", [D, FD]),
        ("W1", [D, INNER]), ("W2", [INNER, D]), ("wrT", [D, 1]),
    ]:
        win[nm] = nc.dram_tensor(nm, shp, F32, kind="ExternalInput").ap()
    cin = {}
    for nm, shp, dtp in [
        ("identf", [128, 128], F32), ("identb", [128, 128], BF16),
        ("onesf", [1, 128], F32), ("onescol", [128, 1], F32),
        ("masks", [128, 4 * 512], BF16), ("b1fm", [128, IC], F32),
        ("b2fm", [128, DC], F32), ("boutr", [1, D], F32),
        ("bconst", [128, 1], F32),
    ]:
        cin[nm] = nc.dram_tensor(nm, shp, dtp, kind="ExternalInput").ap()

    dt = nc.dram_tensor
    p = {}
    for nm, shp in [
        ("wqT_h", [D, D]), ("wqT_l", [D, D]), ("wkT_h", [D, D]), ("wkT_l", [D, D]),
        ("wvT_h", [D, D]), ("wvT_l", [D, D]),
        ("wqkvT_h", [D, 3 * D]), ("wqkvT_l", [D, 3 * D]),
        ("woT_h", [D, D]), ("woT_l", [D, D]),
        ("projT_h", [D, FD]), ("projT_l", [D, FD]),
        ("W1_h", [D, INNER]), ("W1_l", [D, INNER]),
        ("W2_h", [INNER, D]), ("W2_l", [INNER, D]),
        ("wrT_h", [D, 1]), ("wrT_l", [D, 1]),
        ("wgq", [D, INNER]), ("wuq", [D, INNER]), ("wdq", [INNER, D]),
    ]:
        p[nm] = dt("p_" + nm, shp, BF16).ap()
    sc = {}
    for nm, shp, dtp in [
        ("xnq", [NT, DC, 128, 128], BF16),
        ("q0_h", [D, S], BF16), ("q0_l", [D, S], BF16),
        ("v0_h", [S, D], BF16), ("v0_l", [S, D], BF16),
        ("mid", [S, INNER], F32),
        ("g0", [S, D], F32),
        ("x1", [S, D], F32),
        ("xn1", [S, D], F32),
        ("srow", [NT, 128], F32),
        ("q1_h", [D, S], BF16), ("q1_l", [D, S], BF16),
        ("k1_h", [D, S], BF16), ("k1_l", [D, S], BF16),
        ("g1", [S, D], F32),
        ("absr", [128, NT], F32),
    ]:
        sc[nm] = dt("s_" + nm, shp, dtp).ap()

    for nm in (dbg or ()):
        shp, dtp = {
            "stats": ([128, 64], F32), "xnfm_hi": ([128, RW], BF16),
            "xnfm_lo": ([128, RW], BF16), "xnq": ([NT * DC * 128, 128], BF16),
            "kfm_hi": ([128, RW], BF16), "kfm_lo": ([128, RW], BF16),
            "q0_h": ([D, S], BF16), "q0_l": ([D, S], BF16),
            "v0_l": ([S, D], BF16),
            "v0_h": ([S, D], BF16), "mid": ([S, INNER], F32),
            "g0": ([S, D], F32), "x1": ([S, D], F32), "xn1": ([S, D], F32),
            "srow": ([NT, 128], F32), "wtok": ([128, NT], F32),
            "g1": ([S, D], F32), "attnfm_hi": ([128, RW], BF16),
            "wgq": ([D, INNER], BF16), "wdq": ([INNER, D], BF16),
            "wqT_h": ([D, D], BF16), "wqT_l": ([D, D], BF16),
            "absr": ([128, NT], F32),
            "wstat": ([128, 16], F32),
        }[nm]
        ctx.dbg_out[nm] = dt("dbg_" + nm, shp, dtp, kind="ExternalOutput").ap()

    import contextlib
    es = contextlib.ExitStack()
    with es:
        E = es.enter_context
        arb = E(nc.sbuf_tensor("arb", [128, 5 * RW], BF16))
        arf = E(nc.sbuf_tensor("arf", [128, 9216], F32))
        masks = E(nc.sbuf_tensor("sb_masks", [128, 4 * 512], BF16))
        stg = E(nc.sbuf_tensor("sb_stg", [128, 1024], BF16))
        qi32 = E(nc.sbuf_tensor("sb_qi32", [128, 1024], I32))
        identf = E(nc.sbuf_tensor("sb_identf", [128, 128], F32))
        identb = E(nc.sbuf_tensor("sb_identb", [128, 128], BF16))
        onesf = E(nc.sbuf_tensor("sb_onesf", [1, 128], F32))
        onescol = E(nc.sbuf_tensor("sb_onescol", [128, 1], F32))
        tiny = E(nc.sbuf_tensor("tiny", [128, 112], F32))
        bconst = E(nc.sbuf_tensor("sb_bconst", [128, 1], F32))
        nc.const_aps.register(F32, 1e-5, bconst[:, 0:1])
        ps = E(nc.psum_tensor("ps", [128, 3072], F32))
        psb = E(nc.psum_tensor("psb", [128, 2048], BF16))
        for s_name in ("sD", "sW", "sSp", "sT", "sV", "sC", "sG",
                       "fvector", "fscalar",
                       *("b" + e for e in Ctx.ENGS)):
            ctx.sems[s_name] = E(nc.semaphore(s_name))

        st = dict(stop=stop, x_in=x_in, out=out, win=win, p=p, sc=sc, arb=arb, arf=arf,
                  masks=masks, identf=identf, identb=identb, onesf=onesf,
                  onescol=onescol, brow=arf[0:1, 5120:6144], tiny=tiny,
                  bconst=bconst, stg=stg, qi32=qi32, psb=psb,
                  trow=arf[0:1, 4096:8256], ps=ps,
                  cin=cin)
        bld_all(ctx, st)
        ctx.prior_waited = {k: set(v) for k, v in ctx.waited.items()}
        ctx.prog = {e: [] for e in Ctx.ENGS}
        ctx.cnt = {}
        ctx.waited = {}
        bld_all(ctx, st)

        blk = E(nc.Block())

        def mk(engname):
            def run(e):
                for fn in ctx.prog[engname]:
                    fn(e)
            return run
        blk.sync(mk("sync"))
        blk.gpsimd(mk("gpsimd"))
        blk.vector(mk("vector"))
        blk.scalar(mk("scalar"))
        blk.tensor(mk("tensor"))
    return nc


def bld_all(ctx, st):
    cin = st["cin"]
    ctx.dma("sync", st["masks"][:], cin["masks"])
    ctx.dma("sync", st["identf"][:], cin["identf"])
    ctx.dma("sync", st["identb"][:], cin["identb"])
    ctx.dma("sync", st["onesf"][:], cin["onesf"])
    ctx.dma("sync", st["onescol"][:], cin["onescol"])
    ctx.dma("sync", st["bconst"][:], cin["bconst"])
    ctx.wait("sync", "sD")
    for e in ("vector", "scalar", "tensor"):
        ctx.wait(e, "sD")

    stop = st.get("stop")
    def done():
        ctx.wait("sync", "sSp")
        ctx.wait("sync", "sD")
        ctx.wait("sync", "sG")
    phase_W(ctx, st)
    ctx.barrier()
    if stop == "W":
        return done()
    nseq = 1 if stop else NSEQ
    for s in range(nseq):
        for nm, fn in [("A", phase_A), ("C", phase_C), ("B", phase_B),
                       ("D", phase_D), ("E", phase_E), ("F", phase_F),
                       ("G", phase_G), ("H", phase_H)]:
            fn(ctx, st, s)
            ctx.barrier()
            if stop == nm or (nm == "C" and stop in ("Cq", "Ck")):
                return done()
    done()



def phase_W(ctx, st):
    arb, arf, ps, tiny = st["arb"], st["arf"], st["ps"], st["tiny"]
    onesf, onescol = st["onesf"], st["onescol"]
    win, p = st["win"], st["p"]

    f32buf = arf[:, 4096:8192]
    for nm, w in [("wqT", D), ("wkT", D), ("wvT", D), ("wqkvT", 3 * D),
                  ("woT", D), ("projT", FD), ("W1", INNER), ("W2", D),
                  ("wrT", 1)]:
        src = win[nm]
        nchunk = src.shape[0] // 128
        for c in range(nchunk):
            fb = f32buf[:, 0:w]
            hb = arb[:, R3:R3 + w]
            lb = arb[:, R3 + w:R3 + 2 * w]
            ctx.dma("sync", fb, src[128 * c:128 * (c + 1), :])
            ctx.wait("gpsimd", "sV")
            ctx.wait("gpsimd", "sSp")
            ctx.dma("gpsimd", hb, src[128 * c:128 * (c + 1), :], inc="sG")
            ctx.wait("vector", "sD")
            ctx.wait("vector", "sG")
            ctx.wait("vector", "sSp")
            ctx.op("vector", "tensor_tensor", out=lb, in0=fb, in1=hb,
                   op=A.subtract, inc="sV")
            ctx.wait("sync", "sV")
            ctx.dma("sync", p[nm + "_h"][128 * c:128 * (c + 1), :], hb, inc="sSp")
            ctx.dma("sync", p[nm + "_l"][128 * c:128 * (c + 1), :], lb, inc="sSp")
            ctx.wait("sync", "sSp")
    ctx.tap("wqT_h", p["wqT_h"])
    ctx.tap("wqT_l", p["wqT_l"])

    for nm, w, outnm in [("wgT", INNER, "wgq"), ("wuT", INNER, "wuq"),
                         ("wdT", D, "wdq")]:
        src = win[nm]
        nchunk = src.shape[0] // 128
        nelem = float(src.shape[0] * src.shape[1])
        acc = tiny[:, 64:65]
        trash = arf[:, 0:w]
        for c in range(nchunk):
            fb = f32buf[:, 0:w]
            ctx.wait("sync", "sC")
            ctx.dma("sync", fb, src[128 * c:128 * (c + 1), :])
            ctx.wait("scalar", "sD")
            ctx.op("scalar", "activation", out=trash, in_=fb, func=AF.Abs,
                   accum_out=tiny[:, 65:66], inc="sC")
            ctx.wait("vector", "sC")
            if c == 0:
                ctx.op("vector", "tensor_copy", out=acc, in_=tiny[:, 65:66],
                       inc="sV")
            else:
                ctx.op("vector", "tensor_tensor", out=acc, in0=acc,
                       in1=tiny[:, 65:66], op=A.add, inc="sV")
            ctx.fence("vector")
            ctx.wait("scalar", "sV")
        ctx.wait("tensor", "sV")
        ctx.op("tensor", "matmul", out=ps[0:1, 2048:2049], lhsT=onescol[:],
               rhs=acc, start=True, stop=True, inc="sT")
        ctx.wait("vector", "sT")
        ctx.op("vector", "tensor_scalar", out=tiny[0:1, 66:67],
               in0=ps[0:1, 2048:2049], scalar1=1.0 / nelem, scalar2=1e-5,
               op0=A.mult, op1=A.max)
        ctx.fence("vector")
        ctx.op("vector", "tensor_scalar", out=tiny[0:1, 67:68],
               in0=tiny[0:1, 66:67], scalar1=0.5, scalar2=None, op0=A.mult,
               inc="sV")
        ctx.wait("tensor", "sV")
        ctx.op("tensor", "matmul", out=ps[:, 2052:2053], lhsT=onesf[:],
               rhs=tiny[0:1, 66:67], start=True, stop=True)
        ctx.op("tensor", "matmul", out=ps[:, 2053:2054], lhsT=onesf[:],
               rhs=tiny[0:1, 67:68], start=True, stop=True, inc="sT")
        ctx.wait("vector", "sT")
        mw_slot = {"wgq": 68, "wuq": 69, "wdq": 70}[outnm]
        ctx.op("vector", "tensor_copy", out=tiny[:, mw_slot:mw_slot + 1],
               in_=ps[:, 2052:2053])
        thr = tiny[:, 71:72]
        nthr = tiny[:, 72:73]
        ctx.op("vector", "tensor_copy", out=thr, in_=ps[:, 2053:2054])
        ctx.fence("vector")
        ctx.op("vector", "tensor_scalar", out=nthr, in0=thr, scalar1=-1.0,
               scalar2=None, op0=A.mult, inc="sV")
        ctx.fence("vector")
        for c in range(nchunk):
            fb = f32buf[:, 0:w]
            c1 = arf[:, 0:w]
            qb = arb[:, R3:R3 + w]
            ctx.wait("sync", "sV")
            ctx.wait("sync", "sSp")
            ctx.dma("sync", fb, src[128 * c:128 * (c + 1), :])
            ctx.wait("vector", "sD")
            ctx.op("vector", "tensor_scalar", out=c1, in0=fb, scalar1=thr,
                   scalar2=None, op0=A.is_ge)
            ctx.op("vector", "scalar_tensor_tensor", out=qb, in0=fb, scalar=nthr,
                   in1=c1, op0=A.is_le, op1=A.subtract, inc="sV")
            ctx.wait("sync", "sV")
            ctx.dma("sync", p[outnm][128 * c:128 * (c + 1), :], qb, inc="sSp")
    ctx.tap("wstat", tiny[:, 64:80])
    ctx.tap("wgq", p["wgq"])
    ctx.tap("wdq", p["wdq"])



def _ln_fm(ctx, st, src_dram, base, quant, xn_tok_dram=None, sseq=0):
    arb, arf, tiny, ps, identf = (st["arb"], st["arf"], st["tiny"], st["ps"],
                                  st["identf"])
    xb = [arf[:, 0:1024], arf[:, 1024:2048]]
    trash = arf[:, 2048:3072]
    m_all = tiny[:, 0:16]
    ssq_all = tiny[:, 16:32]
    rstd = tiny[:, 32:48]

    for t in range(NT):
        b = xb[t % 2]
        ctx.wait("sync", "sC")
        ctx.dma("sync", b, src_dram[base + 128 * t:base + 128 * (t + 1), :])
        ctx.wait("scalar", "sD")
        ctx.op("scalar", "activation", out=trash, in_=b, func=AF.Copy,
               accum_out=m_all[:, t:t + 1])
        ctx.op("scalar", "activation", out=trash, in_=b, func=AF.Square,
               accum_out=ssq_all[:, t:t + 1], inc="sC")
    ctx.wait("vector", "sC")
    ctx.op("vector", "tensor_scalar", out=m_all, in0=m_all, scalar1=1.0 / D,
           scalar2=None, op0=A.mult)
    ctx.op("vector", "tensor_scalar", out=ssq_all, in0=ssq_all, scalar1=1.0 / D,
           scalar2=None, op0=A.mult)
    ctx.fence("vector")
    v2 = tiny[:, 48:64]
    ctx.op("vector", "tensor_tensor", out=v2, in0=m_all, in1=m_all, op=A.mult)
    ctx.fence("vector")
    ctx.op("vector", "tensor_tensor", out=v2, in0=ssq_all, in1=v2,
           op=A.subtract, inc="sV")
    ctx.wait("scalar", "sV")
    ctx.op("scalar", "activation", out=v2, in_=v2, func=AF.Sqrt, bias=1e-5,
           scale=1.0, inc="sC")
    ctx.wait("vector", "sC")
    ctx.op("vector", "reciprocal", out=rstd, in_=v2, inc="sV")
    ctx.fence("vector")

    xnt = arf[:, 2048:3072]
    for t in range(NT):
        b = xb[t % 2]
        ctx.wait("sync", "sV")
        ctx.dma("sync", b, src_dram[base + 128 * t:base + 128 * (t + 1), :])
        ctx.wait("vector", "sD")
        ctx.wait("vector", "sSp")
        ctx.op("vector", "tensor_scalar", out=xnt, in0=b,
               scalar1=m_all[:, t:t + 1], scalar2=rstd[:, t:t + 1],
               op0=A.subtract, op1=A.mult, inc="sV")
        if xn_tok_dram is not None:
            ctx.wait("sync", "sV")
            ctx.dma("sync", xn_tok_dram[128 * t:128 * (t + 1), :], xnt,
                    inc="sSp")
        ctx.wait("tensor", "sV")
        for c in range(DC):
            bank = (c // 4) % 2
            off = 512 * bank + 128 * (c % 4)
            ctx.op("tensor", "transpose", out=ps[:, off:off + 128],
                   in_=xnt[:, 128 * c:128 * (c + 1)], identity=identf[:],
                   inc="sT" if c == DC - 1 else None)
        ctx.wait("vector", "sT")
        for c in range(DC):
            bank = (c // 4) % 2
            off = 512 * bank + 128 * (c % 4)
            a0 = R0 + 2048 * c + 128 * t
            ctx.op("vector", "tensor_copy", out=arb[:, a0:a0 + 128],
                   in_=ps[:, off:off + 128])
        for c in range(DC):
            bank = (c // 4) % 2
            off = 512 * bank + 128 * (c % 4)
            a0 = R0 + 2048 * c + 128 * t
            a1 = R1 + 2048 * c + 128 * t
            ctx.op("vector", "tensor_tensor", out=arb[:, a1:a1 + 128],
                   in0=ps[:, off:off + 128], in1=arb[:, a0:a0 + 128],
                   op=A.subtract)
        ctx.op("vector", "nop", inc="sV")
        if quant:
            _quant_tile(ctx, st, xnt, t)
    ctx.op("vector", "nop", inc="sV")
    ctx.op("scalar", "nop", inc="sC")


def _quant_tile(ctx, st, xnt, t):
    arb, arf, tiny, ps, identb, sc = (st["arb"], st["arf"], st["tiny"],
                                      st["ps"], st["identb"], st["sc"])
    am = tiny[:, 80:81]
    sq = tiny[:, 81:82]
    inv = tiny[:, 82:83]
    ctx.op("vector", "reduce_max", out=am, in_=xnt, axis=AX,
           apply_absolute_value=True)
    ctx.fence("vector")
    ctx.op("vector", "tensor_scalar", out=am, in0=am, scalar1=1e-5,
           scalar2=None, op0=A.max)
    ctx.fence("vector")
    ctx.op("vector", "reciprocal", out=sq, in_=am)
    ctx.fence("vector")
    ctx.op("vector", "tensor_scalar", out=sq, in0=sq, scalar1=127.0,
           scalar2=None, op0=A.mult)
    ctx.fence("vector")
    ctx.op("vector", "reciprocal", out=inv, in_=sq)
    ctx.fence("vector")
    ctx.op("vector", "tensor_copy", out=tiny[:, 84 + t:85 + t], in_=inv)
    v = arf[:, 3072:4096]
    qi = st["qi32"][:]
    qm1 = arf[:, 5120:6144]
    dlt = arf[:, 6144:7168]
    ctx.op("vector", "tensor_scalar", out=v, in0=xnt, scalar1=sq, scalar2=None,
           op0=A.mult)
    ctx.op("vector", "tensor_copy", out=qi, in_=v, inc="sV")
    ctx.wait("scalar", "sV")
    ctx.op("scalar", "activation", out=qm1, in_=qi, func=AF.Copy, bias=-1.0,
           scale=1.0, inc="sC")
    ctx.wait("vector", "sC")
    ctx.op("vector", "tensor_tensor", out=dlt, in0=v, in1=qm1, op=A.subtract)
    ctx.op("vector", "scalar_tensor_tensor", out=v, in0=dlt, scalar=1.5,
           in1=qm1, op0=A.is_ge, op1=A.add)
    xq = arb[:, R4 + 2048:R4 + 3072]
    ctx.op("vector", "scalar_tensor_tensor", out=xq, in0=dlt, scalar=0.5,
           in1=v, op0=A.is_ge, op1=A.add, inc="sV")
    psb = st["psb"]
    ctx.wait("tensor", "sV")
    for c in range(DC):
        off = 1024 * ((c // 4) % 2) + 128 * (c % 4)
        ctx.op("tensor", "transpose", out=psb[:, off:off + 128],
               in_=xq[:, 128 * c:128 * (c + 1)], identity=identb[:],
               inc="sT" if c == DC - 1 else None)
    ctx.wait("vector", "sT")
    ctx.wait("vector", "sSp")
    stg = arb[:, R4 + 3072:R4 + 4096]
    for c in range(DC):
        off = 1024 * ((c // 4) % 2) + 128 * (c % 4)
        ctx.op("vector", "tensor_copy", out=stg[:, 128 * c:128 * (c + 1)],
               in_=psb[:, off:off + 128],
               inc="sV" if c == DC - 1 else None)
    ctx.wait("sync", "sV")
    ctx.dma("sync", sc["xnq"][t].rearrange("c p n -> p c n"),
            stg.rearrange("p (c n) -> p c n", c=DC), inc="sSp")


def phase_A(ctx, st, s):
    _ln_fm(ctx, st, st["x_in"], s * S, quant=True)
    ctx.wait("sync", "sV")
    ctx.dma("sync", st["sc"]["absr"], st["tiny"][:, 84:84 + NT], inc="sSp")
    ctx.tap("stats", st["tiny"][:, 0:64])
    ctx.tap("xnfm_hi", st["arb"][:, R0:R0 + RW])
    ctx.tap("xnfm_lo", st["arb"][:, R1:R1 + RW])
    ctx.tap("xnq", st["sc"]["xnq"].rearrange("t c p n -> (t c p) n"))
    ctx.tap("absr", st["sc"]["absr"])



def phase_C(ctx, st, s):
    arb, arf, ps, sc, p = st["arb"], st["arf"], st["ps"], st["sc"], st["p"]
    wbuf_h = arb[:, R4:R4 + 8192]
    wbuf_l = arb[:, R4 + 8192:R4 + 16384]
    ev_h = st["stg"][:, 0:512]
    ev_l = st["stg"][:, 512:1024]

    stop = st.get("stop") or ""
    pipe = Pipe(ctx, "sV", 4)
    mats = [("wq", "q"), ("wk", "k"), ("wv", "v")]
    if stop == "Cq":
        mats = mats[:1]
    elif stop == "Ck":
        mats = mats[:2]
    for nm, kind in mats:
        ctx.wait("sync", "sT")
        ctx.dma("sync", wbuf_h[:].rearrange("p (c n) -> p c n", c=DC),
                p[nm + "T_h"].rearrange("(c p) n -> p c n", p=128))
        ctx.dma("sync", wbuf_l[:].rearrange("p (c n) -> p c n", c=DC),
                p[nm + "T_l"].rearrange("(c p) n -> p c n", p=128))
        ctx.wait("tensor", "sD")

        if kind in ("q", "k"):
            for m in range(DC):
                for n in range(4):
                    bank = (m * 4 + n) % 4
                    ob = ps[:, 512 * bank:512 * (bank + 1)]
                    pipe.pre()
                    for c in range(DC):
                        _mm3_acc(ctx,
                                 wbuf_h[:, 1024 * c + 128 * m:1024 * c + 128 * m + 128],
                                 wbuf_l[:, 1024 * c + 128 * m:1024 * c + 128 * m + 128],
                                 arb[:, R0 + 2048 * c + 512 * n:R0 + 2048 * c + 512 * (n + 1)],
                                 arb[:, R1 + 2048 * c + 512 * n:R1 + 2048 * c + 512 * (n + 1)],
                                 ob, first=(c == 0), last=(c == DC - 1))
                    ctx.op("tensor", "nop", inc="sT")
                    ctx.wait("vector", "sT")
                    if kind == "k":
                        a2 = R2 + 2048 * m + 512 * n
                        a3 = R3 + 2048 * m + 512 * n
                        _hl(ctx, arb[:, a2:a2 + 512], arb[:, a3:a3 + 512], ob)
                        ctx.op("vector", "nop", inc="sV")
                    else:
                        ctx.wait("vector", "sSp")
                        _hl(ctx, ev_h[:, 0:512], ev_l[:, 0:512], ob)
                        ctx.op("vector", "nop", inc="sV")
                        ctx.wait("sync", "sV")
                        ctx.dma("sync", sc["q0_h"][128 * m:128 * (m + 1), 512 * n:512 * (n + 1)], ev_h[:, 0:512], inc="sSp")
                        ctx.dma("sync", sc["q0_l"][128 * m:128 * (m + 1), 512 * n:512 * (n + 1)], ev_l[:, 0:512], inc="sSp")
                    pipe.post()
        else:
            for t in range(NT):
                for b in range(2):
                    bank = (t * 2 + b) % 4
                    ob = ps[:, 512 * bank:512 * (bank + 1)]
                    pipe.pre()
                    for c in range(DC):
                        xh = arb[:, R0 + 2048 * c + 128 * t:R0 + 2048 * c + 128 * t + 128]
                        xl = arb[:, R1 + 2048 * c + 128 * t:R1 + 2048 * c + 128 * t + 128]
                        wh = wbuf_h[:, 1024 * c + 512 * b:1024 * c + 512 * (b + 1)]
                        wl = wbuf_l[:, 1024 * c + 512 * b:1024 * c + 512 * (b + 1)]
                        _mm3_acc(ctx, xh, xl, wh, wl, ob,
                                 first=(c == 0), last=(c == DC - 1))
                    ctx.op("tensor", "nop", inc="sT")
                    ctx.wait("vector", "sT")
                    ctx.wait("vector", "sSp")
                    _hl(ctx, ev_h[:, 0:512], ev_l[:, 0:512], ob)
                    ctx.op("vector", "nop", inc="sV")
                    ctx.wait("sync", "sV")
                    ctx.dma("sync", sc["v0_h"][128 * t:128 * (t + 1), 512 * b:512 * (b + 1)], ev_h[:, 0:512], inc="sSp")
                    ctx.dma("sync", sc["v0_l"][128 * t:128 * (t + 1), 512 * b:512 * (b + 1)], ev_l[:, 0:512], inc="sSp")
                    pipe.post()
    ctx.tap("kfm_hi", arb[:, R2:R2 + RW])
    ctx.tap("kfm_lo", arb[:, R3:R3 + RW])
    ctx.tap("q0_h", sc["q0_h"])
    ctx.tap("q0_l", sc["q0_l"])
    ctx.tap("v0_h", sc["v0_h"])
    ctx.tap("v0_l", sc["v0_l"])



def phase_B(ctx, st, s):
    arb, arf, tiny, ps, sc, p, identb = (st["arb"], st["arf"], st["tiny"],
                                         st["ps"], st["sc"], st["p"],
                                         st["identb"])
    ctx.dma("sync", arb[:, R0:R0 + RW].rearrange("p (t c n) -> p t c n",
                                                 t=NT, c=DC),
            sc["xnq"].rearrange("t c p n -> p t c n"))
    ctx.dma("sync", tiny[:, 84:84 + NT], sc["absr"])
    ctx.wait("vector", "sD")
    for t in range(NT):
        for j, slot in ((0, 68), (1, 69)):
            ctx.op("vector", "scalar_tensor_tensor",
                   out=tiny[:, 16 * j + t:16 * j + t + 1],
                   in0=tiny[:, slot:slot + 1], scalar=-1.0,
                   in1=tiny[:, 84 + t:85 + t], op0=A.mult, op1=A.mult)
    ctx.op("vector", "nop", inc="sV")
    ctx.wait("tensor", "sD")
    ctx.wait("scalar", "sV")

    wg = arb[:, R1:R1 + 4096]
    wu = arb[:, R1 + 4096:R1 + 8192]
    gsil = arf[:, 0:512]
    mid = arf[:, 512:1024]
    pipeB = Pipe(ctx, "sV", 3)

    for g in range(8):
        ctx.wait("sync", "sT")
        ctx.dma("sync", wg[:].rearrange("p (c n) -> p c n", c=DC),
                p["wgq"].rearrange("(c p) n -> p c n", p=128)[:, :, 512 * g:512 * (g + 1)])
        ctx.dma("sync", wu[:].rearrange("p (c n) -> p c n", c=DC),
                p["wuq"].rearrange("(c p) n -> p c n", p=128)[:, :, 512 * g:512 * (g + 1)])
        ctx.wait("tensor", "sD")
        for t in range(NT):
            bg = (2 * t) % 6
            bu = (2 * t + 1) % 6
            obg = ps[:, 512 * bg:512 * (bg + 1)]
            obu = ps[:, 512 * bu:512 * (bu + 1)]
            pipeB.pre()
            for c in range(DC):
                xq = arb[:, R0 + 1024 * t + 128 * c:R0 + 1024 * t + 128 * c + 128]
                ctx.op("tensor", "matmul", out=obg, lhsT=xq,
                       rhs=wg[:, 512 * c:512 * (c + 1)],
                       start=(c == 0), stop=(c == DC - 1))
                ctx.op("tensor", "matmul", out=obu, lhsT=xq,
                       rhs=wu[:, 512 * c:512 * (c + 1)],
                       start=(c == 0), stop=(c == DC - 1))
            ctx.op("tensor", "nop", inc="sT")
            ctx.wait("scalar", "sT")
            ctx.wait("scalar", "sV")
            ctx.op("scalar", "activation", out=gsil, in_=obg, func=AF.Silu,
                   bias=0.0, scale=tiny[:, t:t + 1], inc="sC")
            ctx.wait("vector", "sC")
            ctx.wait("vector", "sSp")
            ctx.op("vector", "scalar_tensor_tensor", out=mid, in0=obu,
                   scalar=tiny[:, 16 + t:17 + t], in1=gsil,
                   op0=A.mult, op1=A.mult, inc="sV")
            ctx.wait("sync", "sV")
            ctx.dma("sync", sc["mid"][128 * t:128 * (t + 1), 512 * g:512 * (g + 1)],
                    mid, inc="sSp")
            pipeB.post()
    ctx.tap("mid", sc["mid"])
    ctx.barrier()

    wd_a = arb[:, R1:R1 + RW]
    wd_b = arb[:, R4:R4 + RW]
    ctx.dma("sync", wd_a[:].rearrange("p (c n) -> p c n", c=16),
            p["wdq"].rearrange("(c p) n -> p c n", p=128)[:, 0:16, :])
    ctx.dma("sync", wd_b[:].rearrange("p (c n) -> p c n", c=16),
            p["wdq"].rearrange("(c p) n -> p c n", p=128)[:, 16:32, :])
    ctx.wait("tensor", "sD")

    xq_stg = arb[:, R0:R0 + 4096]
    mfm = arb[:, R0 + 4096:R0 + 8192]
    midb = arf[:, 0:4096]
    vv = arf[:, 4096:8192]
    for t in range(NT):
        ctx.wait("sync", "sV")
        ctx.dma("sync", midb, sc["mid"][128 * t:128 * (t + 1), :])
        ctx.wait("vector", "sD")
        am = tiny[:, 80:81]
        sq = tiny[:, 81:82]
        inv = tiny[:, 82:83]
        combd = tiny[:, 83:84]
        ctx.op("vector", "reduce_max", out=am, in_=midb, axis=AX,
               apply_absolute_value=True)
        ctx.fence("vector")
        ctx.op("vector", "tensor_scalar", out=am, in0=am, scalar1=1e-5,
               scalar2=None, op0=A.max)
        ctx.fence("vector")
        ctx.op("vector", "reciprocal", out=sq, in_=am)
        ctx.fence("vector")
        ctx.op("vector", "tensor_scalar", out=sq, in0=sq, scalar1=127.0,
               scalar2=None, op0=A.mult)
        ctx.fence("vector")
        ctx.op("vector", "reciprocal", out=inv, in_=sq)
        ctx.fence("vector")
        ctx.op("vector", "scalar_tensor_tensor", out=combd, in0=tiny[:, 70:71],
               scalar=-1.0, in1=inv, op0=A.mult, op1=A.mult)
        ctx.op("vector", "tensor_scalar", out=vv, in0=midb, scalar1=sq,
               scalar2=None, op0=A.mult)
        ctx.op("vector", "nop", inc="sV")
        ctx.wait("scalar", "sSp")
        qi = st["qi32"][:]
        for q in range(4):
            vs = vv[:, 1024 * q:1024 * (q + 1)]
            qm1 = midb[:, 1024 * q:1024 * (q + 1)]
            dlt = arf[:, 8192:9216]
            ctx.op("vector", "tensor_copy", out=qi, in_=vs, inc="sV")
            ctx.wait("scalar", "sV")
            ctx.op("scalar", "activation", out=qm1, in_=qi, func=AF.Copy,
                   bias=-1.0, scale=1.0, inc="sC")
            ctx.wait("vector", "sC")
            ctx.op("vector", "tensor_tensor", out=dlt, in0=vs, in1=qm1,
                   op=A.subtract)
            ctx.op("vector", "scalar_tensor_tensor", out=vs, in0=dlt,
                   scalar=1.5, in1=qm1, op0=A.is_ge, op1=A.add)
            ctx.op("vector", "scalar_tensor_tensor",
                   out=xq_stg[:, 1024 * q:1024 * (q + 1)],
                   in0=dlt, scalar=0.5, in1=vs, op0=A.is_ge, op1=A.add,
                   inc="sV")
        ctx.wait("tensor", "sV")
        psb = st["psb"]
        for c in range(IC):
            off = 1024 * ((c % 8) // 4) + 128 * (c % 4)
            ctx.op("tensor", "transpose",
                   out=psb[:, off:off + 128],
                   in_=xq_stg[:, 128 * c:128 * (c + 1)], identity=identb[:],
                   inc="sT" if (c % 8 == 7) else None)
            if c % 8 == 7:
                ctx.wait("vector", "sT")
                for cc in range(c - 7, c + 1):
                    off2 = 1024 * ((cc % 8) // 4) + 128 * (cc % 4)
                    ctx.op("vector", "tensor_copy",
                           out=mfm[:, 128 * cc:128 * (cc + 1)],
                           in_=psb[:, off2:off2 + 128],
                           inc="sV" if cc == c else None)
                ctx.wait("tensor", "sV")
        ob0 = ps[:, 2048:2560]
        ob1 = ps[:, 2560:3072]
        for c in range(IC):
            lt = mfm[:, 128 * c:128 * (c + 1)]
            wslab = wd_a if c < 16 else wd_b
            cc = c % 16
            ctx.op("tensor", "matmul", out=ob0, lhsT=lt,
                   rhs=wslab[:, 1024 * cc:1024 * cc + 512],
                   start=(c == 0), stop=(c == IC - 1))
            ctx.op("tensor", "matmul", out=ob1, lhsT=lt,
                   rhs=wslab[:, 1024 * cc + 512:1024 * cc + 1024],
                   start=(c == 0), stop=(c == IC - 1))
        ctx.op("tensor", "nop", inc="sT")
        ctx.wait("scalar", "sT")
        ctx.wait("scalar", "sSp")
        g0t = arf[:, 8192:9216]
        ctx.op("scalar", "activation", out=g0t[:, 0:512], in_=ob0,
               func=AF.Silu, bias=0.0, scale=combd)
        ctx.op("scalar", "activation", out=g0t[:, 512:1024], in_=ob1,
               func=AF.Silu, bias=0.0, scale=combd, inc="sC")
        ctx.wait("sync", "sC")
        ctx.dma("sync", sc["g0"][128 * t:128 * (t + 1), :], g0t, inc="sSp")
        ctx.op("vector", "nop", inc="sV")
    ctx.tap("g0", sc["g0"])



def phase_D(ctx, st, s):
    arb, arf, tiny, ps, sc, masks, identb = (st["arb"], st["arf"], st["tiny"],
                                             st["ps"], st["sc"], st["masks"],
                                             st["identb"])
    base = s * S
    ctx.dma("sync", arb[:, R0:R0 + RW].rearrange("p (t n) -> p t n", t=NT),
            sc["v0_h"].rearrange("(t p) n -> p t n", p=128))
    ctx.dma("sync", arb[:, R1:R1 + RW].rearrange("p (t n) -> p t n", t=NT),
            sc["v0_l"].rearrange("(t p) n -> p t n", p=128))
    vload = ctx.val("sD")
    qb_h = [arb[:, R4:R4 + 1024], arb[:, R4 + 1024:R4 + 2048]]
    qb_l = [arb[:, R4 + 2048:R4 + 3072], arb[:, R4 + 3072:R4 + 4096]]
    pt_h = arb[:, R4 + 4096:R4 + 6144]
    pt_l = arb[:, R4 + 6144:R4 + 8192]
    ph = arb[:, R4 + 8192:R4 + 10240]
    pl = arb[:, R4 + 10240:R4 + 12288]
    srow = arf[:, 0:2048]
    prow = arf[:, 2048:4096]
    xg = [arf[:, 4096:5120], arf[:, 5120:6144]]
    x1t = arf[:, 6144:7168]

    for t in range(NT):
        qb, ql = qb_h[t % 2], qb_l[t % 2]
        ctx.wait("sync", "sT")
        ctx.dma("sync", qb[:].rearrange("p (c n) -> p c n", c=DC),
                sc["q0_h"].rearrange("(c p) n -> p c n", p=128)[:, :, 128 * t:128 * (t + 1)])
        ctx.dma("sync", ql[:].rearrange("p (c n) -> p c n", c=DC),
                sc["q0_l"].rearrange("(c p) n -> p c n", p=128)[:, :, 128 * t:128 * (t + 1)])
        ctx.wait("sync", "sV")
        ctx.dma("sync", xg[0], st["x_in"][base + 128 * t:base + 128 * (t + 1), :])
        ctx.dma("sync", xg[1], sc["g0"][128 * t:128 * (t + 1), :])
        qv = ctx.val("sD")
        ngrp = (t + 4) // 4
        ctx.wait("tensor", "sD")
        for gi in range(ngrp):
            bank = gi % 2
            ob = ps[:, 512 * bank:512 * (bank + 1)]
            ctx.wait("tensor", "sV")
            for c in range(DC):
                _mm3_acc(ctx,
                         qb[:, 128 * c:128 * (c + 1)], ql[:, 128 * c:128 * (c + 1)],
                         arb[:, R2 + 2048 * c + 512 * gi:R2 + 2048 * c + 512 * (gi + 1)],
                         arb[:, R3 + 2048 * c + 512 * gi:R3 + 2048 * c + 512 * (gi + 1)],
                         ob, first=(c == 0), last=(c == DC - 1))
            ctx.op("tensor", "nop", inc="sT")
            ctx.wait("vector", "sT")
            if gi == ngrp - 1:
                mi = t % 4
                ctx.op("vector", "scalar_tensor_tensor",
                       out=srow[:, 512 * gi:512 * (gi + 1)], in0=ob,
                       scalar=SCALE0, in1=masks[:, 512 * mi:512 * (mi + 1)],
                       op0=A.mult, op1=A.add, inc="sV")
            else:
                ctx.op("vector", "tensor_scalar",
                       out=srow[:, 512 * gi:512 * (gi + 1)], in0=ob,
                       scalar1=SCALE0, scalar2=None, op0=A.mult, inc="sV")
        keylen = 512 * ngrp
        nk = t + 1
        ctx.op("vector", "reduce_max", out=tiny[:, 80:81],
               in_=srow[:, 0:keylen], axis=AX)
        ctx.fence("vector")
        ctx.op("vector", "tensor_scalar", out=tiny[:, 81:82],
               in0=tiny[:, 80:81], scalar1=-1.0, scalar2=None, op0=A.mult,
               inc="sV")
        ctx.wait("scalar", "sV")
        ctx.op("scalar", "activation", out=prow[:, 0:keylen],
               in_=srow[:, 0:keylen], func=AF.Exp, bias=tiny[:, 81:82],
               scale=1.0, accum_out=tiny[:, 82:83], inc="sC")
        ctx.wait("vector", "sC")
        ctx.op("vector", "reciprocal", out=tiny[:, 83:84], in_=tiny[:, 82:83])
        ctx.fence("vector")
        _hl(ctx, ph[:, 0:keylen], pl[:, 0:keylen], prow[:, 0:keylen])
        ctx.op("vector", "nop", inc="sV")
        ctx.wait("tensor", "sV")
        psb = st["psb"]
        for j in range(nk):
            o1 = 1024 * (j % 2)
            ctx.op("tensor", "transpose", out=psb[:, o1:o1 + 128],
                   in_=ph[:, 128 * j:128 * (j + 1)], identity=identb[:])
            ctx.op("tensor", "transpose",
                   out=psb[:, o1 + 128:o1 + 256],
                   in_=pl[:, 128 * j:128 * (j + 1)], identity=identb[:],
                   inc="sT")
            ctx.wait("vector", "sT")
            ctx.op("vector", "tensor_copy", out=pt_h[:, 128 * j:128 * (j + 1)],
                   in_=psb[:, o1:o1 + 128])
            ctx.op("vector", "tensor_copy", out=pt_l[:, 128 * j:128 * (j + 1)],
                   in_=psb[:, o1 + 128:o1 + 256], inc="sV")
            ctx.wait("tensor", "sV")
        ctx.wait("tensor", "sD")
        for b in range(2):
            ob = ps[:, 2048 + 512 * b:2048 + 512 * (b + 1)]
            for j in range(nk):
                _mm3_acc(ctx,
                         pt_h[:, 128 * j:128 * (j + 1)],
                         pt_l[:, 128 * j:128 * (j + 1)],
                         arb[:, R0 + 1024 * j + 512 * b:R0 + 1024 * j + 512 * (b + 1)],
                         arb[:, R1 + 1024 * j + 512 * b:R1 + 1024 * j + 512 * (b + 1)],
                         ob, first=(j == 0), last=(j == nk - 1))
        ctx.op("tensor", "nop", inc="sT")
        ctx.wait("vector", "sT")
        ctx.wait("vector", "sD")
        ctx.wait("vector", "sSp")
        for b in range(2):
            ob = ps[:, 2048 + 512 * b:2048 + 512 * (b + 1)]
            sl = slice(512 * b, 512 * (b + 1))
            ctx.op("vector", "tensor_scalar", out=x1t[:, sl], in0=ob,
                   scalar1=tiny[:, 83:84], scalar2=None, op0=A.mult)
            ctx.op("vector", "tensor_tensor", out=x1t[:, sl], in0=x1t[:, sl],
                   in1=xg[1][:, sl], op=A.mult)
            ctx.op("vector", "tensor_tensor", out=x1t[:, sl], in0=x1t[:, sl],
                   in1=xg[0][:, sl], op=A.add)
        ctx.op("vector", "nop", inc="sV")
        ctx.wait("sync", "sV")
        ctx.dma("sync", sc["x1"][128 * t:128 * (t + 1), :], x1t, inc="sSp")
    ctx.tap("x1", sc["x1"])



def phase_E(ctx, st, s):
    arb, arf, tiny, ps, sc, p, onesf, trow = (st["arb"], st["arf"], st["tiny"],
                                              st["ps"], st["sc"], st["p"],
                                              st["onesf"], st["trow"])
    _ln_fm(ctx, st, sc["x1"], 0, quant=False, xn_tok_dram=sc["xn1"])
    ctx.barrier()

    wrh = arb[:, R4:R4 + 8].rearrange("p (c n) -> p c n", c=DC)
    wrl = arb[:, R4 + 8:R4 + 16].rearrange("p (c n) -> p c n", c=DC)
    ctx.dma("sync", wrh, p["wrT_h"].rearrange("(c p) n -> p c n", p=128),
            allow_slow_non_contiguous=True)
    ctx.dma("sync", wrl, p["wrT_l"].rearrange("(c p) n -> p c n", p=128),
            allow_slow_non_contiguous=True)
    ctx.wait("tensor", "sD")
    for n in range(4):
        ob = ps[0:1, 512 * n:512 * (n + 1)]
        for c in range(DC):
            _mm3_acc(ctx, wrh[:, c, :], wrl[:, c, :],
                     arb[:, R0 + 2048 * c + 512 * n:R0 + 2048 * c + 512 * (n + 1)],
                     arb[:, R1 + 2048 * c + 512 * n:R1 + 2048 * c + 512 * (n + 1)],
                     ob, first=(c == 0), last=(c == DC - 1))
    ctx.op("tensor", "nop", inc="sT")
    ctx.wait("vector", "sT")
    srw = trow[0:1, 0:S]
    ctx.op("vector", "tensor_copy", out=srw, in_=ps[0:1, 0:2048], inc="sV")
    ctx.tap("srow", sc["srow"])

    lo = trow[0:1, S:S + 1]
    hi = trow[0:1, S + 1:S + 2]
    mid = trow[0:1, S + 2:S + 3]
    cnt = trow[0:1, S + 3:S + 4]
    pred = trow[0:1, S + 4:S + 5]
    tmp = trow[0:1, S + 5:S + 6]
    cmpb = trow[0:1, S + 8:S + 8 + S]
    tmp2 = trow[0:1, S + 6:S + 7]
    ctx.op("vector", "reduce_max", out=hi, in_=srw, axis=AX,
           apply_absolute_value=True)
    ctx.fence("vector")
    ctx.op("vector", "tensor_scalar", out=hi, in0=hi, scalar1=1.0,
           scalar2=None, op0=A.add)
    ctx.fence("vector")
    ctx.op("vector", "tensor_scalar", out=lo, in0=hi, scalar1=-1.0,
           scalar2=None, op0=A.mult)
    ctx.fence("vector")
    ctx.op("vector", "tensor_tensor", out=mid, in0=lo, in1=hi, op=A.add)
    ctx.fence("vector")
    ctx.op("vector", "tensor_scalar", out=mid, in0=mid, scalar1=0.5,
           scalar2=None, op0=A.mult)
    ctx.fence("vector")
    K = S // 2
    for _ in range(44):
        ctx.op("vector", "tensor_scalar", out=cmpb, in0=srw, scalar1=mid,
               scalar2=None, op0=A.is_ge)
        ctx.op("vector", "reduce_sum", out=cnt, in_=cmpb, axis=AX)
        ctx.op("vector", "tensor_tensor", out=tmp, in0=mid, in1=lo,
               op=A.subtract)
        ctx.op("vector", "tensor_tensor", out=tmp2, in0=hi, in1=mid,
               op=A.subtract)
        ctx.fence("vector")
        ctx.op("vector", "scalar_tensor_tensor", out=pred, in0=cnt,
               scalar=float(K), in1=tmp, op0=A.is_ge, op1=A.mult)
        ctx.op("vector", "scalar_tensor_tensor", out=tmp2, in0=cnt,
               scalar=float(K), in1=tmp2, op0=A.is_ge, op1=A.mult)
        ctx.fence("vector")
        ctx.op("vector", "tensor_tensor", out=lo, in0=lo, in1=pred, op=A.add)
        ctx.op("vector", "tensor_tensor", out=hi, in0=mid, in1=tmp2, op=A.add)
        ctx.fence("vector")
        ctx.op("vector", "tensor_tensor", out=mid, in0=lo, in1=hi, op=A.add)
        ctx.fence("vector")
        ctx.op("vector", "tensor_scalar", out=mid, in0=mid, scalar1=0.5,
               scalar2=None, op0=A.mult)
        ctx.fence("vector")
    ctx.op("vector", "nop", inc="sV")
    ctx.wait("tensor", "sV")
    ctx.op("tensor", "matmul", out=ps[:, 2560:2561], lhsT=onesf[:], rhs=lo,
           start=True, stop=True, inc="sT")
    ctx.wait("sync", "sV")
    ctx.dma("sync", sc["srow"].rearrange("(o t) p -> o t p", o=1),
            srw.rearrange("o (t p) -> o t p", p=128), inc="sSp")
    ctx.wait("sync", "sSp")
    ctx.dma("sync", arf[0:16, 3072:3200], sc["srow"])
    sv = ctx.val("sD")
    ctx.wait("tensor", "sD")
    ctx.op("tensor", "transpose", out=ps[:, 1536:1552],
           in_=arf[0:16, 3072:3200], identity=st["identf"][0:16, 0:16],
           inc="sT")
    ctx.wait("vector", "sT")
    ctx.op("vector", "tensor_copy", out=tiny[:, 16:16 + NT],
           in_=ps[:, 1536:1552])
    ctx.fence("vector")
    thr = tiny[:, 32:33]
    ctx.op("vector", "tensor_copy", out=thr, in_=ps[:, 2560:2561], inc="sV")
    ctx.fence("vector")
    ctx.wait("scalar", "sV")
    sig = tiny[:, 48:48 + NT]
    ctx.op("scalar", "activation", out=sig, in_=tiny[:, 16:16 + NT],
           func=AF.Sigmoid, inc="sC")
    ctx.wait("vector", "sC")
    msk = tiny[:, 0:NT]
    ctx.op("vector", "tensor_scalar", out=msk, in0=tiny[:, 16:16 + NT],
           scalar1=thr, scalar2=None, op0=A.is_ge)
    ctx.fence("vector")
    ctx.op("vector", "tensor_tensor", out=msk, in0=msk, in1=sig, op=A.mult,
           inc="sV")
    ctx.tap("wtok", msk)



def phase_F(ctx, st, s):
    arb, arf, tiny, ps, sc, p, identf = (st["arb"], st["arf"], st["tiny"],
                                         st["ps"], st["sc"], st["p"],
                                         st["identf"])
    cin = st["cin"]
    gl = arf[:, 0:512]
    upd = arf[:, 512:1024]
    updtok = arf[:, 1024:5120]
    x1g = arf[:, 5120:6144]
    b1fm = arf[:, 6144:6176]
    b2fm = arf[:, 6176:6184]
    g1t = arf[:, 6208:7232]
    g1o = arf[:, 7232:8256]
    ctx.dma("sync", b1fm, cin["b1fm"])
    ctx.dma("sync", b2fm, cin["b2fm"])
    w1h = [arb[:, R4:R4 + 1024], arb[:, R4 + 1024:R4 + 2048]]
    w1l = [arb[:, R4 + 2048:R4 + 3072], arb[:, R4 + 3072:R4 + 4096]]
    w2h = arb[:, R4 + 4096:R4 + 8192]
    w2l = arb[:, R4 + 8192:R4 + 12288]
    hfm_h = arb[:, R2:R2 + RW]
    hfm_l = arb[:, R3:R3 + RW]
    ctx.wait("tensor", "sD")
    ctx.wait("scalar", "sD")
    pipeF1 = Pipe(ctx, "sV", 2)
    pipeF2 = Pipe(ctx, "sC", 2)

    for grp in range(4):
        for fc in range(IC):
            pb = fc % 2
            ctx.wait("sync", "sT")
            ctx.dma("sync", w1h[pb][:].rearrange("p (c n) -> p c n", c=DC),
                    p["W1_h"].rearrange("(c p) n -> p c n", p=128)[:, :, 128 * fc:128 * (fc + 1)])
            ctx.dma("sync", w1l[pb][:].rearrange("p (c n) -> p c n", c=DC),
                    p["W1_l"].rearrange("(c p) n -> p c n", p=128)[:, :, 128 * fc:128 * (fc + 1)])
            ctx.wait("tensor", "sD")
            bank = fc % 2
            ob = ps[:, 512 * bank:512 * (bank + 1)]
            pipeF1.pre()
            for c in range(DC):
                _mm3_acc(ctx,
                         w1h[pb][:, 128 * c:128 * (c + 1)],
                         w1l[pb][:, 128 * c:128 * (c + 1)],
                         arb[:, R0 + 2048 * c + 512 * grp:R0 + 2048 * c + 512 * (grp + 1)],
                         arb[:, R1 + 2048 * c + 512 * grp:R1 + 2048 * c + 512 * (grp + 1)],
                         ob, first=(c == 0), last=(c == DC - 1))
            ctx.op("tensor", "nop", inc="sT")
            ctx.wait("scalar", "sT")
            ctx.wait("scalar", "sV")
            ctx.op("scalar", "activation", out=gl, in_=ob,
                   func=AF.Gelu_apprx_tanh, bias=b1fm[:, fc:fc + 1],
                   scale=1.0, inc="sC")
            ctx.wait("vector", "sC")
            _hl(ctx, hfm_h[:, 512 * fc:512 * (fc + 1)],
                hfm_l[:, 512 * fc:512 * (fc + 1)], gl)
            ctx.op("vector", "nop", inc="sV")
            pipeF1.post()
        for dc in range(DC):
            ctx.wait("sync", "sT")
            ctx.dma("sync", w2h[:].rearrange("p (c n) -> p c n", c=IC),
                    p["W2_h"].rearrange("(c p) n -> p c n", p=128)[:, :, 128 * dc:128 * (dc + 1)])
            ctx.dma("sync", w2l[:].rearrange("p (c n) -> p c n", c=IC),
                    p["W2_l"].rearrange("(c p) n -> p c n", p=128)[:, :, 128 * dc:128 * (dc + 1)])
            ctx.wait("tensor", "sD")
            bank = 2 + dc % 2
            ob = ps[:, 512 * bank:512 * (bank + 1)]
            pipeF2.pre()
            ctx.wait("tensor", "sV")
            for c in range(IC):
                _mm3_acc(ctx,
                         w2h[:, 128 * c:128 * (c + 1)],
                         w2l[:, 128 * c:128 * (c + 1)],
                         hfm_h[:, 512 * c:512 * (c + 1)],
                         hfm_l[:, 512 * c:512 * (c + 1)],
                         ob, first=(c == 0), last=(c == IC - 1))
            ctx.op("tensor", "nop", inc="sT")
            ctx.wait("vector", "sT")
            ctx.op("vector", "tensor_scalar", out=upd, in0=ob,
                   scalar1=b2fm[:, dc:dc + 1], scalar2=None, op0=A.add,
                   inc="sC")
            pipeF2.post()
            ctx.wait("tensor", "sC")
            for j in range(4):
                bk = 4 + j % 2
                ctx.op("tensor", "transpose", out=ps[:, 512 * bk:512 * bk + 128],
                       in_=upd[:, 128 * j:128 * (j + 1)], identity=identf[:],
                       inc="sT")
                ctx.wait("vector", "sT")
                ctx.op("vector", "tensor_copy",
                       out=updtok[:, 1024 * j + 128 * dc:1024 * j + 128 * (dc + 1)],
                       in_=ps[:, 512 * bk:512 * bk + 128], inc="sV")
                ctx.wait("tensor", "sV")
        for j in range(4):
            t = 4 * grp + j
            ctx.wait("sync", "sV")
            ctx.dma("sync", x1g, sc["xn1"][128 * t:128 * (t + 1), :])
            ctx.wait("vector", "sD")
            ctx.op("vector", "scalar_tensor_tensor", out=g1t,
                   in0=updtok[:, 1024 * j:1024 * (j + 1)],
                   scalar=tiny[:, t:t + 1], in1=x1g, op0=A.mult, op1=A.add,
                   inc="sV")
            ctx.wait("scalar", "sV")
            ctx.wait("scalar", "sSp")
            ctx.op("scalar", "activation", out=g1o, in_=g1t, func=AF.Silu,
                   inc="sC")
            ctx.wait("sync", "sC")
            ctx.dma("sync", sc["g1"][128 * t:128 * (t + 1), :], g1o, inc="sSp")
    ctx.tap("g1", sc["g1"])



def phase_G(ctx, st, s):
    arb, arf, ps, sc, p = st["arb"], st["arf"], st["ps"], st["sc"], st["p"]
    ev_h = st["stg"][:, 0:512]
    ev_l = st["stg"][:, 512:1024]
    pipeG = Pipe(ctx, "sV", 4)
    wch = [arb[:, R4:R4 + 1024], arb[:, R4 + 1024:R4 + 2048]]
    wcl = [arb[:, R4 + 2048:R4 + 3072], arb[:, R4 + 3072:R4 + 4096]]

    for m in range(16):
        pb = m % 2
        ctx.wait("sync", "sT")
        ctx.dma("sync", wch[pb][:].rearrange("p (c n) -> p c n", c=DC),
                p["wqkvT_h"].rearrange("(c p) n -> p c n", p=128)[:, :, 128 * m:128 * (m + 1)])
        ctx.dma("sync", wcl[pb][:].rearrange("p (c n) -> p c n", c=DC),
                p["wqkvT_l"].rearrange("(c p) n -> p c n", p=128)[:, :, 128 * m:128 * (m + 1)])
        ctx.wait("tensor", "sD")
        dst_h = sc["q1_h"] if m < 8 else sc["k1_h"]
        dst_l = sc["q1_l"] if m < 8 else sc["k1_l"]
        mm = m % 8
        for n in range(4):
            bank = n % 4
            ob = ps[:, 512 * bank:512 * (bank + 1)]
            pipeG.pre()
            for c in range(DC):
                _mm3_acc(ctx, wch[pb][:, 128 * c:128 * (c + 1)],
                         wcl[pb][:, 128 * c:128 * (c + 1)],
                         arb[:, R0 + 2048 * c + 512 * n:R0 + 2048 * c + 512 * (n + 1)],
                         arb[:, R1 + 2048 * c + 512 * n:R1 + 2048 * c + 512 * (n + 1)],
                         ob, first=(c == 0), last=(c == DC - 1))
            ctx.op("tensor", "nop", inc="sT")
            ctx.wait("vector", "sT")
            ctx.wait("vector", "sSp")
            _hl(ctx, ev_h[:, 0:512], ev_l[:, 0:512], ob)
            ctx.op("vector", "nop", inc="sV")
            ctx.wait("sync", "sV")
            ctx.dma("sync", dst_h[128 * mm:128 * (mm + 1), 512 * n:512 * (n + 1)],
                    ev_h[:, 0:512], inc="sSp")
            ctx.dma("sync", dst_l[128 * mm:128 * (mm + 1), 512 * n:512 * (n + 1)],
                    ev_l[:, 0:512], inc="sSp")
            pipeG.post()
    vwh = arb[:, R4:R4 + 8192]
    vwl = arb[:, R4 + 8192:R4 + 16384]
    ctx.wait("sync", "sT")
    ctx.dma("sync", vwh[:].rearrange("p (c n) -> p c n", c=DC),
            p["wqkvT_h"].rearrange("(c p) n -> p c n", p=128)[:, :, 2048:3072])
    ctx.dma("sync", vwl[:].rearrange("p (c n) -> p c n", c=DC),
            p["wqkvT_l"].rearrange("(c p) n -> p c n", p=128)[:, :, 2048:3072])
    ctx.wait("tensor", "sD")
    for t in range(NT):
        for b in range(2):
            bank = (t * 2 + b) % 4
            ob = ps[:, 512 * bank:512 * (bank + 1)]
            pipeG.pre()
            for c in range(DC):
                xh = arb[:, R0 + 2048 * c + 128 * t:R0 + 2048 * c + 128 * t + 128]
                xl = arb[:, R1 + 2048 * c + 128 * t:R1 + 2048 * c + 128 * t + 128]
                _mm3_acc(ctx, xh, xl,
                         vwh[:, 1024 * c + 512 * b:1024 * c + 512 * (b + 1)],
                         vwl[:, 1024 * c + 512 * b:1024 * c + 512 * (b + 1)],
                         ob, first=(c == 0), last=(c == DC - 1))
            ctx.op("tensor", "nop", inc="sT")
            ctx.wait("vector", "sT")
            a2 = R2 + 1024 * t + 512 * b
            a3 = R3 + 1024 * t + 512 * b
            _hl(ctx, arb[:, a2:a2 + 512], arb[:, a3:a3 + 512], ob)
            ctx.op("vector", "nop", inc="sV")
            pipeG.post()



def phase_H(ctx, st, s):
    arb, arf, tiny, ps, sc, p, identb = (st["arb"], st["arf"], st["tiny"],
                                         st["ps"], st["sc"], st["p"],
                                         st["identb"])
    onesf, brow = st["onesf"], st["brow"]
    cin = st["cin"]
    base = s * S
    pj_h = arb[:, R4:R4 + 1024]
    pj_l = arb[:, R4 + 1024:R4 + 2048]
    ctx.dma("sync", pj_h[:].rearrange("p (c n) -> p c n", c=DC),
            p["projT_h"].rearrange("(c p) n -> p c n", p=128))
    ctx.dma("sync", pj_l[:].rearrange("p (c n) -> p c n", c=DC),
            p["projT_l"].rearrange("(c p) n -> p c n", p=128))
    ctx.dma("sync", brow[:], cin["boutr"])
    hr = R4 + 2048
    qh_h = arb[:, hr:hr + 2048]
    qh_l = arb[:, hr + 2048:hr + 4096]
    kh_h = arb[:, hr + 4096:hr + 6144]
    kh_l = arb[:, hr + 6144:hr + 8192]
    qp_h, qp_l, kp_h, kp_l = qh_h, qh_l, kh_h, kh_l
    at_h = arb[:, R4 + 10240:R4 + 10368]
    at_l = arb[:, R4 + 10368:R4 + 10496]
    ks_h = arb[:, R4 + 10496:R4 + 10560]
    ks_l = arb[:, R4 + 10560:R4 + 10624]
    an_h = arb[:, R4 + 10624:R4 + 10752]
    an_l = arb[:, R4 + 10752:R4 + 10880]
    qpf = arf[:, 0:2048]
    kpf = arf[:, 2048:4096]
    ksf = arf[:, 4096:4160]
    zbuf = arf[:, 4160:4176]
    atf = arf[:, 4304:4432]
    att = arf[:, 4432:4560]
    rlu = arf[:, 4560:5072]
    ctx.wait("tensor", "sD")

    for h in range(HEADS):
        ctx.wait("sync", "sT")
        ctx.dma("sync", qh_h, sc["q1_h"][128 * h:128 * (h + 1), :])
        ctx.dma("sync", qh_l, sc["q1_l"][128 * h:128 * (h + 1), :])
        ctx.dma("sync", kh_h, sc["k1_h"][128 * h:128 * (h + 1), :])
        ctx.dma("sync", kh_l, sc["k1_l"][128 * h:128 * (h + 1), :])
        ctx.wait("tensor", "sD")
        for src_h, src_l, dst in ((qh_h, qh_l, qpf), (kh_h, kh_l, kpf)):
            for n in range(4):
                bank = n % 4
                ob = ps[:, 512 * bank:512 * (bank + 1)]
                ctx.wait("tensor", "sV")
                _mm3_acc(ctx, pj_h[:, 128 * h:128 * (h + 1)],
                         pj_l[:, 128 * h:128 * (h + 1)],
                         src_h[:, 512 * n:512 * (n + 1)],
                         src_l[:, 512 * n:512 * (n + 1)],
                         ob, first=True, last=True)
                ctx.op("tensor", "nop", inc="sT")
                ctx.wait("vector", "sT")
                sl = slice(512 * n, 512 * (n + 1))
                ctx.op("vector", "tensor_scalar", out=dst[:, sl], in0=ob,
                       scalar1=0.0, scalar2=None, op0=A.min, inc="sV")
                ctx.wait("scalar", "sV")
                ctx.op("scalar", "activation", out=dst[:, sl], in_=dst[:, sl],
                       func=AF.Exp)
                ctx.op("scalar", "activation", out=rlu, in_=ob, func=AF.Relu,
                       inc="sC")
                ctx.wait("vector", "sC")
                ctx.op("vector", "tensor_tensor", out=dst[:, sl],
                       in0=dst[:, sl], in1=rlu, op=A.add, inc="sV")
        _hl(ctx, qp_h, qp_l, qpf)
        _hl(ctx, kp_h, kp_l, kpf)
        ctx.op("vector", "tensor_reduce", out=ksf,
               in_=kpf.rearrange("p (b s) -> p b s", s=BLK), axis=AX, op=A.add)
        ctx.fence("vector")
        _hl(ctx, ks_h, ks_l, ksf, fence=True)
        ctx.op("vector", "nop", inc="sV")
        ctx.wait("tensor", "sV")
        for gi in range(NT):
            c0 = 128 * gi
            obA = ps[:, 2048:2176]
            ctx.op("tensor", "matmul", out=obA, lhsT=kp_h[:, c0:c0 + 128],
                   rhs=qp_h[:, c0:c0 + 128], start=True, stop=False)
            ctx.op("tensor", "matmul", out=obA, lhsT=kp_h[:, c0:c0 + 128],
                   rhs=qp_l[:, c0:c0 + 128], start=False, stop=False)
            ctx.op("tensor", "matmul", out=obA, lhsT=kp_l[:, c0:c0 + 128],
                   rhs=qp_h[:, c0:c0 + 128], start=False, stop=True)
            for b in range(4):
                zo = ps[32 * b:32 * (b + 1), 2560:2561]
                tp = (0, 32 * b)
                ctx.op("tensor", "matmul", out=zo,
                       lhsT=qp_h[:, c0 + 32 * b:c0 + 32 * (b + 1)],
                       rhs=ks_h[:, 4 * gi + b:4 * gi + b + 1],
                       tile_position=tp, start=True, stop=False)
                ctx.op("tensor", "matmul", out=zo,
                       lhsT=qp_h[:, c0 + 32 * b:c0 + 32 * (b + 1)],
                       rhs=ks_l[:, 4 * gi + b:4 * gi + b + 1],
                       tile_position=tp, start=False, stop=False)
                ctx.op("tensor", "matmul", out=zo,
                       lhsT=qp_l[:, c0 + 32 * b:c0 + 32 * (b + 1)],
                       rhs=ks_h[:, 4 * gi + b:4 * gi + b + 1],
                       tile_position=tp, start=False, stop=True)
            ctx.op("tensor", "nop", inc="sT")
            ctx.wait("vector", "sT")
            ctx.op("vector", "tensor_copy", out=atf, in_=obA)
            ctx.fence("vector")
            _hl(ctx, at_h, at_l, atf, fence=True)
            ctx.op("vector", "tensor_scalar", out=zbuf[:, 0:1],
                   in0=ps[:, 2560:2561], scalar1=1e-8, scalar2=None, op0=A.add)
            ctx.fence("vector")
            ctx.op("vector", "reciprocal", out=zbuf[:, 1:2], in_=zbuf[:, 0:1],
                   inc="sV")
            ctx.fence("vector")
            ctx.wait("tensor", "sV")
            obN = ps[:, 0:128]
            for b in range(4):
                vh = arb[32 * b:32 * (b + 1),
                         R2 + 1024 * gi + 128 * h:R2 + 1024 * gi + 128 * (h + 1)]
                vl = arb[32 * b:32 * (b + 1),
                         R3 + 1024 * gi + 128 * h:R3 + 1024 * gi + 128 * (h + 1)]
                ah = at_h[32 * b:32 * (b + 1), 32 * b:32 * (b + 1)]
                al = at_l[32 * b:32 * (b + 1), 32 * b:32 * (b + 1)]
                on = obN[32 * b:32 * (b + 1), :]
                tp = (32 * b, 32 * b)
                ctx.op("tensor", "matmul", out=on, lhsT=ah, rhs=vh,
                       tile_position=tp, start=True, stop=False)
                ctx.op("tensor", "matmul", out=on, lhsT=ah, rhs=vl,
                       tile_position=tp, start=False, stop=False)
                ctx.op("tensor", "matmul", out=on, lhsT=al, rhs=vh,
                       tile_position=tp, start=False, stop=True)
            ctx.op("tensor", "nop", inc="sT")
            ctx.wait("vector", "sT")
            ctx.op("vector", "tensor_scalar", out=att, in0=obN,
                   scalar1=zbuf[:, 1:2], scalar2=None, op0=A.mult)
            ctx.fence("vector")
            _hl(ctx, an_h, an_l, att, fence=True)
            ctx.op("vector", "nop", inc="sV")
            ctx.wait("tensor", "sV")
            psb = st["psb"]
            ctx.op("tensor", "transpose", out=psb[:, 0:128],
                   in_=an_h, identity=identb[:])
            ctx.op("tensor", "transpose", out=psb[:, 128:256],
                   in_=an_l, identity=identb[:], inc="sT")
            ctx.wait("vector", "sT")
            a0 = R0 + 2048 * h + 128 * gi
            a1 = R1 + 2048 * h + 128 * gi
            ctx.op("vector", "tensor_copy", out=arb[:, a0:a0 + 128],
                   in_=psb[:, 0:128])
            ctx.op("vector", "tensor_copy", out=arb[:, a1:a1 + 128],
                   in_=psb[:, 128:256], inc="sV")
            ctx.wait("tensor", "sV")
    ctx.tap("attnfm_hi", arb[:, R0:R0 + RW])

    woh = arb[:, R4:R4 + 8192]
    wol = arb[:, R4 + 8192:R4 + 16384]
    ctx.wait("sync", "sT")
    ctx.dma("sync", woh[:].rearrange("p (c n) -> p c n", c=DC),
            p["woT_h"].rearrange("(c p) n -> p c n", p=128))
    ctx.dma("sync", wol[:].rearrange("p (c n) -> p c n", c=DC),
            p["woT_l"].rearrange("(c p) n -> p c n", p=128))
    ctx.wait("tensor", "sD")
    x1t = arf[:, 0:1024]
    g1t = arf[:, 1024:2048]
    ot = arf[:, 2048:3072]
    for t in range(NT):
        ctx.wait("sync", "sV")
        ctx.dma("sync", x1t, sc["x1"][128 * t:128 * (t + 1), :])
        ctx.dma("sync", g1t, sc["g1"][128 * t:128 * (t + 1), :])
        dv = ctx.val("sD")
        for b in range(2):
            ob = ps[:, 512 * b:512 * (b + 1)]
            ctx.wait("tensor", "sV")
            ctx.op("tensor", "matmul", out=ob, lhsT=onesf[:],
                   rhs=brow[0:1, 512 * b:512 * (b + 1)], start=True, stop=False)
            for c in range(DC):
                _mm3_acc(ctx,
                         arb[:, R0 + 2048 * c + 128 * t:R0 + 2048 * c + 128 * t + 128],
                         arb[:, R1 + 2048 * c + 128 * t:R1 + 2048 * c + 128 * t + 128],
                         woh[:, 1024 * c + 512 * b:1024 * c + 512 * (b + 1)],
                         wol[:, 1024 * c + 512 * b:1024 * c + 512 * (b + 1)],
                         ob, first=False, last=(c == DC - 1))
            ctx.op("tensor", "nop", inc="sT")
        ctx.wait("vector", "sT")
        ctx.wait("vector", "sD")
        ctx.wait("vector", "sSp")
        for b in range(2):
            ob = ps[:, 512 * b:512 * (b + 1)]
            sl = slice(512 * b, 512 * (b + 1))
            ctx.op("vector", "tensor_tensor", out=ot[:, sl], in0=ob,
                   in1=g1t[:, sl], op=A.mult)
            ctx.op("vector", "tensor_tensor", out=ot[:, sl], in0=ot[:, sl],
                   in1=x1t[:, sl], op=A.add)
        ctx.op("vector", "nop", inc="sV")
        ctx.wait("sync", "sV")
        ctx.dma("sync", st["out"][base + 128 * t:base + 128 * (t + 1), :], ot,
                inc="sSp")



def _mk_consts():
    identf = np.eye(128, dtype=np.float32)
    identb = np.eye(128, dtype=ml_dtypes.bfloat16)
    onesf = np.ones((1, 128), dtype=np.float32)
    onescol = np.ones((128, 1), dtype=np.float32)
    masks = np.zeros((128, 4 * 512), dtype=np.float32)
    for pp in range(4):
        m = np.zeros((128, 512), dtype=np.float32)
        m[:, 128 * (pp + 1):] = NEG
        m[:, 128 * pp:128 * (pp + 1)] = np.triu(
            np.full((128, 128), NEG, dtype=np.float32), 1)
        masks[:, 512 * pp:512 * (pp + 1)] = m
    return identf, identb, onesf, onescol, masks


def kernel(L0_attn_x, L0_attn_WQ, L0_attn_WK, L0_attn_WV, L0_ffn_w_gate,
           L0_ffn_w_up, L0_ffn_w_down, L1_attn_w_qkv, L1_attn_w_out,
           L1_attn_b_out, L1_attn_proj, L1_ffn_w_router, L1_ffn_W1, L1_ffn_b1,
           L1_ffn_W2, L1_ffn_b2, _dbg=None, _cores=None, _stop=None):
    x = np.ascontiguousarray(np.asarray(L0_attn_x, dtype=np.float32))
    identf, identb, onesf, onescol, masks = _mk_consts()
    T = lambda a: np.ascontiguousarray(np.asarray(a, dtype=np.float32).T)
    C = lambda a: np.ascontiguousarray(np.asarray(a, dtype=np.float32))
    w = dict(
        wqT=T(L0_attn_WQ), wkT=T(L0_attn_WK), wvT=T(L0_attn_WV),
        wgT=T(L0_ffn_w_gate), wuT=T(L0_ffn_w_up), wdT=T(L0_ffn_w_down),
        wqkvT=T(L1_attn_w_qkv), woT=T(L1_attn_w_out),
        projT=C(L1_attn_proj).reshape(D, FD),
        W1=C(L1_ffn_W1), W2=C(L1_ffn_W2),
        wrT=C(L1_ffn_w_router).reshape(D, 1),
        identf=identf, identb=identb, onesf=onesf, onescol=onescol,
        masks=masks.astype(ml_dtypes.bfloat16),
        b1fm=np.ascontiguousarray(C(L1_ffn_b1).reshape(IC, 128).T),
        b2fm=np.ascontiguousarray(C(L1_ffn_b2).reshape(DC, 128).T),
        boutr=C(L1_attn_b_out).reshape(1, D),
        bconst=np.full((128, 1), 1e-5, dtype=np.float32),
    )
    nc = build_core(dbg=_dbg, stop=_stop)
    ncores = NCORE if _cores is None else _cores
    in_maps = []
    for c in range(ncores):
        m = dict(w)
        m["x"] = np.ascontiguousarray(x[NSEQ * c:NSEQ * (c + 1)].reshape(NSEQ * S, D))
        in_maps.append(m)
    res = run_bass_kernel_spmd(nc, in_maps, core_ids=list(range(ncores)))
    if _dbg:
        return res
    outs = [np.asarray(r["out"]).reshape(NSEQ, S, D) for r in res.results]
    return np.concatenate(outs, axis=0).astype(np.float32)


# version 07 (final)
# speedup vs baseline: 1.1242x; 1.0149x over previous
import numpy as np
import ml_dtypes
import concourse.bass as bass
import concourse.mybir as mybir
from concourse.bass_utils import run_bass_kernel_spmd
from concourse.alu_op_type import AluOpType as A

F32 = mybir.dt.float32
BF16 = mybir.dt.bfloat16
I32 = mybir.dt.int32
AX = mybir.AxisListType.X
AF = mybir.ActivationFunctionType

S = 2048
D = 1024
DC = D // 128
NT = S // 128
INNER = 4096
IC = INNER // 128
HEADS = 8
HD = 128
FD = 128
BLK = 32
NSEQ = 2
NCORE = 8
SCALE0 = 1024 ** -0.5
NEG = -1.0e30

RW = 16384
R0, R1, R2, R3, R4 = 0, RW, 2 * RW, 3 * RW, 4 * RW


class Ctx:
    ENGS = ("sync", "gpsimd", "vector", "scalar", "tensor")

    def __init__(self, nc, dbg):
        self.nc = nc
        self.dbg = set(dbg or ())
        self.prog = {e: [] for e in self.ENGS}
        self.cnt = {}
        self.sems = {}
        self.dbg_out = {}
        self.waited = {}
        self.prior_waited = {}
        self.touch = {"vector": [], "scalar": []}

    def emit(self, eng, fn):
        self.prog[eng].append(fn)

    def inc(self, sem, amt=1):
        self.cnt[sem] = self.cnt.get(sem, 0) + amt
        return self.cnt[sem]

    def val(self, sem):
        return self.cnt.get(sem, 0)

    def op(self, eng, meth, *args, inc=None, inc_amt=1, **kw):
        sems = self.sems
        if inc is not None:
            self.inc(inc, inc_amt)
        if eng == "tensor" and meth == "nop" and inc is not None:
            meth = "drain"
        fence = inc is not None and eng in ("vector", "scalar")
        pre_drain = False
        if eng in ("vector", "scalar") and meth not in ("nop", "drain"):
            rd, wr = _op_aps(kw)
            hist = self.touch[eng]
            if _conflicts(rd, wr, hist):
                pre_drain = True
                hist.clear()
            hist.extend((a, False) for a in rd)
            hist.extend((a, True) for a in wr)
            if fence:
                hist.clear()

        skip_op = eng in ("vector", "scalar") and meth == "nop" and inc is not None

        def fn(e, meth=meth, args=args, kw=kw, inc=inc, inc_amt=inc_amt,
               fence=fence, pre_drain=pre_drain, skip_op=skip_op):
            if pre_drain:
                e.drain()
            if skip_op:
                e.drain()
                e.drain().then_inc(sems[inc], inc_amt)
                return
            r = getattr(e, meth)(*args, **kw)
            if inc is not None:
                if fence:
                    e.drain().then_inc(sems[inc], inc_amt)
                else:
                    r.then_inc(sems[inc], inc_amt)
        self.emit(eng, fn)

    def fence(self, eng):
        self.touch[eng].clear()
        sems = self.sems
        n = self.inc("f" + eng)

        def fn(e, n=n, s="f" + eng):
            e.drain()
            e.drain().then_inc(sems[s], 1)
            e.wait_ge(sems[s], n)
        self.emit(eng, fn)

    def wait(self, eng, sem, v=None):
        v = self.val(sem) if v is None else v
        self.waited.setdefault(sem, set()).add(v)
        sems = self.sems
        self.emit(eng, lambda e, s=sem, v=v: e.wait_ge(sems[s], v))

    def dma(self, eng, out, in_, inc="sD", **kw):
        if self.val(inc) in self.prior_waited.get(inc, ()):  
            self.wait(eng, inc)
        self.op(eng, "dma_start", inc=inc, inc_amt=16, out=out, in_=in_, **kw)

    def barrier(self):
        self.wait("sync", "sD")
        self.wait("sync", "sSp")
        for sm in ("sWa", "sWb", "sPa", "sPb"):
            self.wait("sync", sm)
        self.wait("gpsimd", "sG")
        self.wait("gpsimd", "sGa")
        self.wait("gpsimd", "sGb")
        for e in self.ENGS:
            self.op(e, "nop", inc="b" + e)
        for e in self.ENGS:
            for o in self.ENGS:
                if o != e:
                    self.wait(e, "b" + o)

    def tap(self, name, src_ap):
        if name not in self.dbg:
            return
        self.barrier()
        self.dma("sync", self.dbg_out[name], src_ap)
        self.barrier()


def _ap_region(x):
    try:
        apl = x.ap
        off = x.offset
        nm = x.name
    except AttributeError:
        return None
    if not apl:
        return (nm, 0, 1 << 60, 0, 128)
    rowstep = apl[0][0]
    if rowstep <= 0:
        return (nm, 0, 1 << 60, 0, 128)
    p0 = off // rowstep
    c0 = off % rowstep
    span = 1
    for step, count in apl[1:]:
        span += step * (count - 1)
    return (nm, c0, c0 + span, p0, p0 + apl[0][1])


def _op_aps(kw):
    rd, wr = [], []
    for k, v in kw.items():
        r = _ap_region(v) if not isinstance(v, (int, float, bool, type(None))) else None
        if r is None:
            continue
        if k in ("out", "accum_out"):
            wr.append(r)
        else:
            rd.append(r)
    return rd, wr


def _overlap(a, b):
    return (a[0] == b[0] and a[1] < b[2] and b[1] < a[2]
            and a[3] < b[4] and b[3] < a[4])


def _conflicts(rd, wr, hist):
    for h, h_is_wr in hist:
        if h_is_wr:
            for a in rd + wr:
                if _overlap(a, h):
                    return True
        else:
            for a in wr:
                if _overlap(a, h):
                    return True
    return False


class Pipe:

    def __init__(self, ctx, sem="sV", lag=4, eng="tensor"):
        self.ctx, self.sem, self.lag, self.eng = ctx, sem, lag, eng
        self.hist = []

    def pre(self):
        if len(self.hist) >= self.lag:
            self.ctx.wait(self.eng, self.sem, self.hist[-self.lag])

    def post(self):
        self.hist.append(self.ctx.val(self.sem))


def _hl(ctx, dst_hi, dst_lo, src_f32, fence=False):
    ctx.op("vector", "tensor_copy", out=dst_hi, in_=src_f32)
    if fence:
        ctx.fence("vector")
    ctx.op("vector", "tensor_tensor", out=dst_lo, in0=src_f32, in1=dst_hi,
           op=A.subtract)


def _mm3_acc(ctx, lhsT_h, lhsT_l, rhs_h, rhs_l, out, first, last):
    ctx.op("tensor", "matmul", out=out, lhsT=lhsT_h, rhs=rhs_h,
           start=first, stop=False)
    ctx.op("tensor", "matmul", out=out, lhsT=lhsT_h, rhs=rhs_l,
           start=False, stop=False)
    ctx.op("tensor", "matmul", out=out, lhsT=lhsT_l, rhs=rhs_h,
           start=False, stop=last)


def build_core(dbg=None, stop=None):
    nc = bass.Bass("TRN2", target_bir_lowering=False)
    ctx = Ctx(nc, dbg)

    x_in = nc.dram_tensor("x", [NSEQ * S, D], F32, kind="ExternalInput").ap()
    out = nc.dram_tensor("out", [NSEQ * S, D], F32, kind="ExternalOutput").ap()

    win = {}
    for nm, shp in [
        ("wqT", [D, D]), ("wkT", [D, D]), ("wvT", [D, D]),
        ("wgT", [D, INNER]), ("wuT", [D, INNER]), ("wdT", [INNER, D]),
        ("wqkvT", [D, 3 * D]), ("woT", [D, D]), ("projT", [D, FD]),
        ("W1", [D, INNER]), ("W2", [INNER, D]), ("wrT", [D, 1]),
    ]:
        win[nm] = nc.dram_tensor(nm, shp, F32, kind="ExternalInput").ap()
    cin = {}
    for nm, shp, dtp in [
        ("identf", [128, 128], F32), ("identb", [128, 128], BF16),
        ("onesf", [1, 128], F32), ("onescol", [128, 1], F32),
        ("masks", [128, 4 * 512], BF16), ("b1fm", [128, IC], F32),
        ("b2fm", [128, DC], F32), ("boutr", [1, D], F32),
        ("bconst", [128, 1], F32),
    ]:
        cin[nm] = nc.dram_tensor(nm, shp, dtp, kind="ExternalInput").ap()

    dt = nc.dram_tensor
    p = {}
    for nm, shp in [
        ("wqT_h", [D, D]), ("wqT_l", [D, D]), ("wkT_h", [D, D]), ("wkT_l", [D, D]),
        ("wvT_h", [D, D]), ("wvT_l", [D, D]),
        ("wqkvT_h", [D, 3 * D]), ("wqkvT_l", [D, 3 * D]),
        ("woT_h", [D, D]), ("woT_l", [D, D]),
        ("projT_h", [D, FD]), ("projT_l", [D, FD]),
        ("W1_h", [D, INNER]), ("W1_l", [D, INNER]),
        ("W2_h", [INNER, D]), ("W2_l", [INNER, D]),
        ("wrT_h", [D, 1]), ("wrT_l", [D, 1]),
        ("wgq", [D, INNER]), ("wuq", [D, INNER]), ("wdq", [INNER, D]),
    ]:
        p[nm] = dt("p_" + nm, shp, BF16).ap()
    sc = {}
    for nm, shp, dtp in [
        ("xnq", [NT, DC, 128, 128], BF16),
        ("q0_h", [D, S], BF16), ("q0_l", [D, S], BF16),
        ("v0_h", [S, D], BF16), ("v0_l", [S, D], BF16),
        ("mid", [S, INNER], F32),
        ("g0", [S, D], F32),
        ("x1", [S, D], F32),
        ("xn1", [S, D], F32),
        ("srow", [NT, 128], F32),
        ("q1_h", [D, S], BF16), ("q1_l", [D, S], BF16),
        ("k1_h", [D, S], BF16), ("k1_l", [D, S], BF16),
        ("g1", [S, D], F32),
        ("absr", [128, NT], F32),
    ]:
        sc[nm] = dt("s_" + nm, shp, dtp).ap()

    for nm in (dbg or ()):
        shp, dtp = {
            "stats": ([128, 64], F32), "xnfm_hi": ([128, RW], BF16),
            "xnfm_lo": ([128, RW], BF16), "xnq": ([NT * DC * 128, 128], BF16),
            "kfm_hi": ([128, RW], BF16), "kfm_lo": ([128, RW], BF16),
            "q0_h": ([D, S], BF16), "q0_l": ([D, S], BF16),
            "v0_l": ([S, D], BF16),
            "v0_h": ([S, D], BF16), "mid": ([S, INNER], F32),
            "g0": ([S, D], F32), "x1": ([S, D], F32), "xn1": ([S, D], F32),
            "srow": ([NT, 128], F32), "wtok": ([128, NT], F32),
            "g1": ([S, D], F32), "attnfm_hi": ([128, RW], BF16),
            "wgq": ([D, INNER], BF16), "wdq": ([INNER, D], BF16),
            "wqT_h": ([D, D], BF16), "wqT_l": ([D, D], BF16),
            "absr": ([128, NT], F32),
            "wstat": ([128, 16], F32),
        }[nm]
        ctx.dbg_out[nm] = dt("dbg_" + nm, shp, dtp, kind="ExternalOutput").ap()

    import contextlib
    es = contextlib.ExitStack()
    with es:
        E = es.enter_context
        arb = E(nc.sbuf_tensor("arb", [128, 5 * RW], BF16))
        arf = E(nc.sbuf_tensor("arf", [128, 9216], F32))
        masks = E(nc.sbuf_tensor("sb_masks", [128, 4 * 512], BF16))
        stg = E(nc.sbuf_tensor("sb_stg", [128, 1024], BF16))
        qi32 = E(nc.sbuf_tensor("sb_qi32", [128, 1024], I32))
        identf = E(nc.sbuf_tensor("sb_identf", [128, 128], F32))
        identb = E(nc.sbuf_tensor("sb_identb", [128, 128], BF16))
        onesf = E(nc.sbuf_tensor("sb_onesf", [1, 128], F32))
        onescol = E(nc.sbuf_tensor("sb_onescol", [128, 1], F32))
        tiny = E(nc.sbuf_tensor("tiny", [128, 144], F32))
        bconst = tiny[:, 100:101]
        nc.const_aps.register(F32, 1e-5, bconst)
        ps = E(nc.psum_tensor("ps", [128, 3072], F32))
        psb = E(nc.psum_tensor("psb", [128, 2048], BF16))
        for s_name in ("sD", "sW", "sSp", "sT", "sV", "sC", "sG",
                       "sWa", "sWb", "sGa", "sGb", "sPa", "sPb",
                       "fvector", "fscalar",
                       *("b" + e for e in Ctx.ENGS)):
            ctx.sems[s_name] = E(nc.semaphore(s_name))

        st = dict(stop=stop, x_in=x_in, out=out, win=win, p=p, sc=sc, arb=arb, arf=arf,
                  masks=masks, identf=identf, identb=identb, onesf=onesf,
                  onescol=onescol, brow=arf[0:1, 5120:6144], tiny=tiny,
                  bconst=bconst, stg=stg, qi32=qi32, psb=psb,
                  trow=arf[0:1, 4096:8256], ps=ps,
                  cin=cin)
        bld_all(ctx, st)
        ctx.prior_waited = {k: set(v) for k, v in ctx.waited.items()}
        ctx.prog = {e: [] for e in Ctx.ENGS}
        ctx.cnt = {}
        ctx.waited = {}
        bld_all(ctx, st)

        blk = E(nc.Block())

        def mk(engname):
            def run(e):
                for fn in ctx.prog[engname]:
                    fn(e)
            return run
        blk.sync(mk("sync"))
        blk.gpsimd(mk("gpsimd"))
        blk.vector(mk("vector"))
        blk.scalar(mk("scalar"))
        blk.tensor(mk("tensor"))
    return nc


def bld_all(ctx, st):
    cin = st["cin"]
    ctx.dma("sync", st["masks"][:], cin["masks"])
    ctx.dma("sync", st["identf"][:], cin["identf"])
    ctx.dma("sync", st["identb"][:], cin["identb"])
    ctx.dma("sync", st["onesf"][:], cin["onesf"])
    ctx.dma("sync", st["onescol"][:], cin["onescol"])
    ctx.dma("sync", st["bconst"], cin["bconst"])
    ctx.wait("sync", "sD")
    for e in ("vector", "scalar", "tensor"):
        ctx.wait(e, "sD")

    stop = st.get("stop")
    def done():
        ctx.wait("sync", "sSp")
        ctx.wait("sync", "sD")
        ctx.wait("sync", "sG")
    phase_W(ctx, st)
    ctx.barrier()
    if stop == "W":
        return done()
    nseq = 1 if stop else NSEQ
    for s in range(nseq):
        for nm, fn in [("A", phase_A), ("C", phase_C), ("B", phase_B),
                       ("D", phase_D), ("E", phase_E), ("F", phase_F),
                       ("G", phase_G), ("H", phase_H)]:
            fn(ctx, st, s)
            ctx.barrier()
            if stop == nm or (nm == "C" and stop in ("Cq", "Ck")):
                return done()
    done()



def phase_W(ctx, st):
    arb, arf, ps, tiny = st["arb"], st["arf"], st["ps"], st["tiny"]
    onesf, onescol = st["onesf"], st["onescol"]
    win, p = st["win"], st["p"]

    fb = [arf[:, 0:4096], arf[:, 4096:8192]]
    hbs = [arb[:, R3:R3 + 4096], arb[:, R3 + 4096:R3 + 8192]]
    lbs = [arb[:, R3 + 8192:R3 + 12288], arb[:, R3 + 12288:R3 + 16384]]
    c1s = [arb[:, R2:R2 + 4096], arb[:, R2 + 4096:R2 + 8192]]
    qbs = [arb[:, R2 + 8192:R2 + 12288], arb[:, R2 + 12288:R2 + 16384]]
    sW = ["sWa", "sWb"]
    sG = ["sGa", "sGb"]
    sP = ["sPa", "sPb"]
    gc = [0]

    def hl_item(nm, w):
        src_t = win[nm]
        for c in range(src_t.shape[0] // 128):
            pp = gc[0] % 2
            gc[0] += 1
            f, h, l = fb[pp][:, 0:w], hbs[pp][:, 0:w], lbs[pp][:, 0:w]
            ctx.wait("sync", sP[pp])
            ctx.dma("sync", f, src_t[128 * c:128 * (c + 1), :], inc=sW[pp])
            ctx.wait("gpsimd", "sV")
            ctx.wait("gpsimd", sP[pp])
            ctx.dma("gpsimd", h, src_t[128 * c:128 * (c + 1), :], inc=sG[pp])
            ctx.wait("vector", sW[pp])
            ctx.wait("vector", sG[pp])
            ctx.wait("vector", sP[pp])
            ctx.op("vector", "tensor_tensor", out=l, in0=f, in1=h,
                   op=A.subtract, inc="sV")
            ctx.wait("sync", "sV")
            ctx.dma("sync", p[nm + "_h"][128 * c:128 * (c + 1), :], h, inc=sP[pp])
            ctx.dma("sync", p[nm + "_l"][128 * c:128 * (c + 1), :], l, inc=sP[pp])

    for nm, w in [("wqT", D), ("wkT", D), ("wvT", D), ("wqkvT", 3 * D),
                  ("woT", D), ("projT", FD), ("W1", INNER), ("W2", D),
                  ("wrT", 1)]:
        hl_item(nm, w)
    ctx.tap("wqT_h", p["wqT_h"])
    ctx.tap("wqT_l", p["wqT_l"])

    for nm, w, outnm in [("wgT", INNER, "wgq"), ("wuT", INNER, "wuq"),
                         ("wdT", D, "wdq")]:
        src_t = win[nm]
        nchunk = src_t.shape[0] // 128
        nelem = float(src_t.shape[0] * src_t.shape[1])
        acc = tiny[:, 64:65]
        for c in range(nchunk):
            pp = gc[0] % 2
            gc[0] += 1
            f = fb[pp][:, 0:w]
            ctx.wait("sync", "sC")
            ctx.dma("sync", f, src_t[128 * c:128 * (c + 1), :], inc=sW[pp])
            ctx.wait("scalar", sW[pp])
            ctx.op("scalar", "activation", out=f, in_=f, func=AF.Abs,
                   accum_out=tiny[:, 65:66], inc="sC")
            ctx.wait("vector", "sC")
            if c == 0:
                ctx.op("vector", "tensor_copy", out=acc, in_=tiny[:, 65:66],
                       inc="sV")
            else:
                ctx.op("vector", "tensor_tensor", out=acc, in0=acc,
                       in1=tiny[:, 65:66], op=A.add, inc="sV")
            ctx.fence("vector")
            ctx.wait("scalar", "sV")
        ctx.wait("tensor", "sV")
        ctx.op("tensor", "matmul", out=ps[0:1, 2048:2049], lhsT=onescol[:],
               rhs=acc, start=True, stop=True, inc="sT")
        ctx.wait("vector", "sT")
        ctx.op("vector", "tensor_scalar", out=tiny[0:1, 66:67],
               in0=ps[0:1, 2048:2049], scalar1=1.0 / nelem, scalar2=1e-5,
               op0=A.mult, op1=A.max)
        ctx.fence("vector")
        ctx.op("vector", "tensor_scalar", out=tiny[0:1, 67:68],
               in0=tiny[0:1, 66:67], scalar1=0.5, scalar2=None, op0=A.mult,
               inc="sV")
        ctx.wait("tensor", "sV")
        ctx.op("tensor", "matmul", out=ps[:, 2052:2053], lhsT=onesf[:],
               rhs=tiny[0:1, 66:67], start=True, stop=True)
        ctx.op("tensor", "matmul", out=ps[:, 2053:2054], lhsT=onesf[:],
               rhs=tiny[0:1, 67:68], start=True, stop=True, inc="sT")
        ctx.wait("vector", "sT")
        mw_slot = {"wgq": 68, "wuq": 69, "wdq": 70}[outnm]
        ctx.op("vector", "tensor_copy", out=tiny[:, mw_slot:mw_slot + 1],
               in_=ps[:, 2052:2053])
        thr = tiny[:, 71:72]
        nthr = tiny[:, 72:73]
        ctx.op("vector", "tensor_copy", out=thr, in_=ps[:, 2053:2054])
        ctx.fence("vector")
        ctx.op("vector", "tensor_scalar", out=nthr, in0=thr, scalar1=-1.0,
               scalar2=None, op0=A.mult, inc="sV")
        ctx.fence("vector")
        for c in range(nchunk):
            pp = gc[0] % 2
            gc[0] += 1
            f = fb[pp][:, 0:w]
            c1 = c1s[pp][:, 0:w]
            qb = qbs[pp][:, 0:w]
            ctx.wait("sync", sP[pp])
            ctx.dma("sync", f, src_t[128 * c:128 * (c + 1), :], inc=sW[pp])
            ctx.wait("vector", sW[pp])
            ctx.wait("vector", sP[pp])
            ctx.op("vector", "tensor_scalar", out=c1, in0=f, scalar1=thr,
                   scalar2=None, op0=A.is_ge)
            ctx.op("vector", "scalar_tensor_tensor", out=qb, in0=f, scalar=nthr,
                   in1=c1, op0=A.is_le, op1=A.subtract, inc="sV")
            ctx.wait("sync", "sV")
            ctx.dma("sync", p[outnm][128 * c:128 * (c + 1), :], qb, inc=sP[pp])
    ctx.tap("wstat", tiny[:, 64:80])
    ctx.tap("wgq", p["wgq"])
    ctx.tap("wdq", p["wdq"])



def _ln_fm(ctx, st, src_dram, base, quant, xn_tok_dram=None, sseq=0):
    arb, arf, tiny, ps, identf = (st["arb"], st["arf"], st["tiny"], st["ps"],
                                  st["identf"])
    xb = [arf[:, 0:1024], arf[:, 1024:2048]]
    trash = arf[:, 2048:3072]
    m_all = tiny[:, 0:16]
    ssq_all = tiny[:, 16:32]
    rstd = tiny[:, 32:48]

    for t in range(NT):
        b = xb[t % 2]
        ctx.wait("sync", "sC")
        ctx.dma("sync", b, src_dram[base + 128 * t:base + 128 * (t + 1), :])
        ctx.wait("scalar", "sD")
        ctx.op("scalar", "activation", out=trash, in_=b, func=AF.Copy,
               accum_out=m_all[:, t:t + 1])
        ctx.op("scalar", "activation", out=trash, in_=b, func=AF.Square,
               accum_out=ssq_all[:, t:t + 1], inc="sC")
    ctx.wait("vector", "sC")
    ctx.op("vector", "tensor_scalar", out=m_all, in0=m_all, scalar1=1.0 / D,
           scalar2=None, op0=A.mult)
    ctx.op("vector", "tensor_scalar", out=ssq_all, in0=ssq_all, scalar1=1.0 / D,
           scalar2=None, op0=A.mult)
    ctx.fence("vector")
    v2 = tiny[:, 48:64]
    ctx.op("vector", "tensor_tensor", out=v2, in0=m_all, in1=m_all, op=A.mult)
    ctx.fence("vector")
    ctx.op("vector", "tensor_tensor", out=v2, in0=ssq_all, in1=v2,
           op=A.subtract, inc="sV")
    ctx.wait("scalar", "sV")
    ctx.op("scalar", "activation", out=v2, in_=v2, func=AF.Sqrt, bias=1e-5,
           scale=1.0, inc="sC")
    ctx.wait("vector", "sC")
    ctx.op("vector", "reciprocal", out=rstd, in_=v2, inc="sV")
    ctx.fence("vector")

    xnt = arf[:, 2048:3072]
    for t in range(NT):
        b = xb[t % 2]
        ctx.wait("sync", "sV")
        ctx.dma("sync", b, src_dram[base + 128 * t:base + 128 * (t + 1), :])
        ctx.wait("vector", "sD")
        ctx.wait("vector", "sSp")
        ctx.op("vector", "tensor_scalar", out=xnt, in0=b,
               scalar1=m_all[:, t:t + 1], scalar2=rstd[:, t:t + 1],
               op0=A.subtract, op1=A.mult, inc="sV")
        if xn_tok_dram is not None:
            ctx.wait("sync", "sV")
            ctx.dma("sync", xn_tok_dram[128 * t:128 * (t + 1), :], xnt,
                    inc="sSp")
        ctx.wait("tensor", "sV")
        for c in range(DC):
            bank = (c // 4) % 2
            off = 512 * bank + 128 * (c % 4)
            ctx.op("tensor", "transpose", out=ps[:, off:off + 128],
                   in_=xnt[:, 128 * c:128 * (c + 1)], identity=identf[:],
                   inc="sT" if c == DC - 1 else None)
        ctx.wait("vector", "sT")
        for c in range(DC):
            bank = (c // 4) % 2
            off = 512 * bank + 128 * (c % 4)
            a0 = R0 + 2048 * c + 128 * t
            ctx.op("vector", "tensor_copy", out=arb[:, a0:a0 + 128],
                   in_=ps[:, off:off + 128])
        for c in range(DC):
            bank = (c // 4) % 2
            off = 512 * bank + 128 * (c % 4)
            a0 = R0 + 2048 * c + 128 * t
            a1 = R1 + 2048 * c + 128 * t
            ctx.op("vector", "tensor_tensor", out=arb[:, a1:a1 + 128],
                   in0=ps[:, off:off + 128], in1=arb[:, a0:a0 + 128],
                   op=A.subtract)
        ctx.op("vector", "nop", inc="sV")
        if quant:
            _quant_tile(ctx, st, xnt, t)
    ctx.op("vector", "nop", inc="sV")
    ctx.op("scalar", "nop", inc="sC")


def _quant_tile(ctx, st, xnt, t):
    arb, arf, tiny, ps, identb, sc = (st["arb"], st["arf"], st["tiny"],
                                      st["ps"], st["identb"], st["sc"])
    am = tiny[:, 80:81]
    sq = tiny[:, 81:82]
    inv = tiny[:, 82:83]
    ctx.op("vector", "reduce_max", out=am, in_=xnt, axis=AX,
           apply_absolute_value=True)
    ctx.fence("vector")
    ctx.op("vector", "tensor_scalar", out=am, in0=am, scalar1=1e-5,
           scalar2=None, op0=A.max)
    ctx.fence("vector")
    ctx.op("vector", "reciprocal", out=sq, in_=am)
    ctx.fence("vector")
    ctx.op("vector", "tensor_scalar", out=sq, in0=sq, scalar1=127.0,
           scalar2=None, op0=A.mult)
    ctx.fence("vector")
    ctx.op("vector", "reciprocal", out=inv, in_=sq)
    ctx.fence("vector")
    ctx.op("vector", "tensor_copy", out=tiny[:, 84 + t:85 + t], in_=inv)
    v = arf[:, 3072:4096]
    qi = st["qi32"][:]
    qm1 = arf[:, 5120:6144]
    dlt = arf[:, 6144:7168]
    ctx.op("vector", "tensor_scalar", out=v, in0=xnt, scalar1=sq, scalar2=None,
           op0=A.mult)
    ctx.op("vector", "tensor_copy", out=qi, in_=v, inc="sV")
    ctx.wait("scalar", "sV")
    ctx.op("scalar", "activation", out=qm1, in_=qi, func=AF.Copy, bias=-1.0,
           scale=1.0, inc="sC")
    ctx.wait("vector", "sC")
    ctx.op("vector", "tensor_tensor", out=dlt, in0=v, in1=qm1, op=A.subtract)
    ctx.op("vector", "scalar_tensor_tensor", out=v, in0=dlt, scalar=1.5,
           in1=qm1, op0=A.is_ge, op1=A.add)
    xq = arb[:, R4 + 2048:R4 + 3072]
    ctx.op("vector", "scalar_tensor_tensor", out=xq, in0=dlt, scalar=0.5,
           in1=v, op0=A.is_ge, op1=A.add, inc="sV")
    psb = st["psb"]
    ctx.wait("tensor", "sV")
    for c in range(DC):
        off = 1024 * ((c // 4) % 2) + 128 * (c % 4)
        ctx.op("tensor", "transpose", out=psb[:, off:off + 128],
               in_=xq[:, 128 * c:128 * (c + 1)], identity=identb[:],
               inc="sT" if c == DC - 1 else None)
    ctx.wait("vector", "sT")
    ctx.wait("vector", "sSp")
    stg = arb[:, R4 + 3072:R4 + 4096]
    for c in range(DC):
        off = 1024 * ((c // 4) % 2) + 128 * (c % 4)
        ctx.op("vector", "tensor_copy", out=stg[:, 128 * c:128 * (c + 1)],
               in_=psb[:, off:off + 128],
               inc="sV" if c == DC - 1 else None)
    ctx.wait("sync", "sV")
    ctx.dma("sync", sc["xnq"][t].rearrange("c p n -> p c n"),
            stg.rearrange("p (c n) -> p c n", c=DC), inc="sSp")


def phase_A(ctx, st, s):
    _ln_fm(ctx, st, st["x_in"], s * S, quant=True)
    ctx.wait("sync", "sV")
    ctx.dma("sync", st["sc"]["absr"], st["tiny"][:, 84:84 + NT], inc="sSp")
    ctx.tap("stats", st["tiny"][:, 0:64])
    ctx.tap("xnfm_hi", st["arb"][:, R0:R0 + RW])
    ctx.tap("xnfm_lo", st["arb"][:, R1:R1 + RW])
    ctx.tap("xnq", st["sc"]["xnq"].rearrange("t c p n -> (t c p) n"))
    ctx.tap("absr", st["sc"]["absr"])



def phase_C(ctx, st, s):
    arb, arf, ps, sc, p = st["arb"], st["arf"], st["ps"], st["sc"], st["p"]
    wbuf_h = arb[:, R4:R4 + 8192]
    wbuf_l = arb[:, R4 + 8192:R4 + 16384]
    ev_h = st["stg"][:, 0:512]
    ev_l = st["stg"][:, 512:1024]

    stop = st.get("stop") or ""
    pipe = Pipe(ctx, "sV", 4)
    mats = [("wq", "q"), ("wk", "k"), ("wv", "v")]
    if stop == "Cq":
        mats = mats[:1]
    elif stop == "Ck":
        mats = mats[:2]
    for nm, kind in mats:
        ctx.wait("sync", "sT")
        ctx.dma("sync", wbuf_h[:].rearrange("p (c n) -> p c n", c=DC),
                p[nm + "T_h"].rearrange("(c p) n -> p c n", p=128))
        ctx.dma("sync", wbuf_l[:].rearrange("p (c n) -> p c n", c=DC),
                p[nm + "T_l"].rearrange("(c p) n -> p c n", p=128))
        ctx.wait("tensor", "sD")

        if kind in ("q", "k"):
            for m in range(DC):
                for n in range(4):
                    bank = (m * 4 + n) % 4
                    ob = ps[:, 512 * bank:512 * (bank + 1)]
                    pipe.pre()
                    for c in range(DC):
                        _mm3_acc(ctx,
                                 wbuf_h[:, 1024 * c + 128 * m:1024 * c + 128 * m + 128],
                                 wbuf_l[:, 1024 * c + 128 * m:1024 * c + 128 * m + 128],
                                 arb[:, R0 + 2048 * c + 512 * n:R0 + 2048 * c + 512 * (n + 1)],
                                 arb[:, R1 + 2048 * c + 512 * n:R1 + 2048 * c + 512 * (n + 1)],
                                 ob, first=(c == 0), last=(c == DC - 1))
                    ctx.op("tensor", "nop", inc="sT")
                    ctx.wait("vector", "sT")
                    if kind == "k":
                        a2 = R2 + 2048 * m + 512 * n
                        a3 = R3 + 2048 * m + 512 * n
                        _hl(ctx, arb[:, a2:a2 + 512], arb[:, a3:a3 + 512], ob)
                        ctx.op("vector", "nop", inc="sV")
                    else:
                        ctx.wait("vector", "sSp")
                        _hl(ctx, ev_h[:, 0:512], ev_l[:, 0:512], ob)
                        ctx.op("vector", "nop", inc="sV")
                        ctx.wait("sync", "sV")
                        ctx.dma("sync", sc["q0_h"][128 * m:128 * (m + 1), 512 * n:512 * (n + 1)], ev_h[:, 0:512], inc="sSp")
                        ctx.dma("sync", sc["q0_l"][128 * m:128 * (m + 1), 512 * n:512 * (n + 1)], ev_l[:, 0:512], inc="sSp")
                    pipe.post()
        else:
            for t in range(NT):
                for b in range(2):
                    bank = (t * 2 + b) % 4
                    ob = ps[:, 512 * bank:512 * (bank + 1)]
                    pipe.pre()
                    for c in range(DC):
                        xh = arb[:, R0 + 2048 * c + 128 * t:R0 + 2048 * c + 128 * t + 128]
                        xl = arb[:, R1 + 2048 * c + 128 * t:R1 + 2048 * c + 128 * t + 128]
                        wh = wbuf_h[:, 1024 * c + 512 * b:1024 * c + 512 * (b + 1)]
                        wl = wbuf_l[:, 1024 * c + 512 * b:1024 * c + 512 * (b + 1)]
                        _mm3_acc(ctx, xh, xl, wh, wl, ob,
                                 first=(c == 0), last=(c == DC - 1))
                    ctx.op("tensor", "nop", inc="sT")
                    ctx.wait("vector", "sT")
                    ctx.wait("vector", "sSp")
                    _hl(ctx, ev_h[:, 0:512], ev_l[:, 0:512], ob)
                    ctx.op("vector", "nop", inc="sV")
                    ctx.wait("sync", "sV")
                    ctx.dma("sync", sc["v0_h"][128 * t:128 * (t + 1), 512 * b:512 * (b + 1)], ev_h[:, 0:512], inc="sSp")
                    ctx.dma("sync", sc["v0_l"][128 * t:128 * (t + 1), 512 * b:512 * (b + 1)], ev_l[:, 0:512], inc="sSp")
                    pipe.post()
    ctx.tap("kfm_hi", arb[:, R2:R2 + RW])
    ctx.tap("kfm_lo", arb[:, R3:R3 + RW])
    ctx.tap("q0_h", sc["q0_h"])
    ctx.tap("q0_l", sc["q0_l"])
    ctx.tap("v0_h", sc["v0_h"])
    ctx.tap("v0_l", sc["v0_l"])



def phase_B(ctx, st, s):
    arb, arf, tiny, ps, sc, p, identb = (st["arb"], st["arf"], st["tiny"],
                                         st["ps"], st["sc"], st["p"],
                                         st["identb"])
    ctx.dma("sync", arb[:, R0:R0 + RW].rearrange("p (t c n) -> p t c n",
                                                 t=NT, c=DC),
            sc["xnq"].rearrange("t c p n -> p t c n"))
    ctx.dma("sync", tiny[:, 84:84 + NT], sc["absr"])
    ctx.wait("vector", "sD")
    for t in range(NT):
        for j, slot in ((0, 68), (1, 69)):
            ctx.op("vector", "scalar_tensor_tensor",
                   out=tiny[:, 16 * j + t:16 * j + t + 1],
                   in0=tiny[:, slot:slot + 1], scalar=-1.0,
                   in1=tiny[:, 84 + t:85 + t], op0=A.mult, op1=A.mult)
    ctx.op("vector", "nop", inc="sV")
    ctx.wait("tensor", "sD")
    ctx.wait("scalar", "sV")

    wg = arb[:, R1:R1 + 4096]
    wu = arb[:, R1 + 4096:R1 + 8192]
    gsil = arf[:, 0:512]
    mid = arf[:, 512:1024]
    pipeB = Pipe(ctx, "sV", 3)

    for g in range(8):
        ctx.wait("sync", "sT")
        ctx.dma("sync", wg[:].rearrange("p (c n) -> p c n", c=DC),
                p["wgq"].rearrange("(c p) n -> p c n", p=128)[:, :, 512 * g:512 * (g + 1)])
        ctx.dma("sync", wu[:].rearrange("p (c n) -> p c n", c=DC),
                p["wuq"].rearrange("(c p) n -> p c n", p=128)[:, :, 512 * g:512 * (g + 1)])
        ctx.wait("tensor", "sD")
        for t in range(NT):
            bg = (2 * t) % 6
            bu = (2 * t + 1) % 6
            obg = ps[:, 512 * bg:512 * (bg + 1)]
            obu = ps[:, 512 * bu:512 * (bu + 1)]
            pipeB.pre()
            for c in range(DC):
                xq = arb[:, R0 + 1024 * t + 128 * c:R0 + 1024 * t + 128 * c + 128]
                ctx.op("tensor", "matmul", out=obg, lhsT=xq,
                       rhs=wg[:, 512 * c:512 * (c + 1)],
                       start=(c == 0), stop=(c == DC - 1))
                ctx.op("tensor", "matmul", out=obu, lhsT=xq,
                       rhs=wu[:, 512 * c:512 * (c + 1)],
                       start=(c == 0), stop=(c == DC - 1))
            ctx.op("tensor", "nop", inc="sT")
            ctx.wait("scalar", "sT")
            ctx.wait("scalar", "sV")
            ctx.op("scalar", "activation", out=gsil, in_=obg, func=AF.Silu,
                   bias=0.0, scale=tiny[:, t:t + 1], inc="sC")
            ctx.wait("vector", "sC")
            ctx.wait("vector", "sSp")
            ctx.op("vector", "scalar_tensor_tensor", out=mid, in0=obu,
                   scalar=tiny[:, 16 + t:17 + t], in1=gsil,
                   op0=A.mult, op1=A.mult, inc="sV")
            ctx.wait("sync", "sV")
            ctx.dma("sync", sc["mid"][128 * t:128 * (t + 1), 512 * g:512 * (g + 1)],
                    mid, inc="sSp")
            pipeB.post()
    ctx.tap("mid", sc["mid"])
    ctx.barrier()

    wd_a = arb[:, R1:R1 + RW]
    wd_b = arb[:, R4:R4 + RW]
    ctx.dma("sync", wd_a[:].rearrange("p (c n) -> p c n", c=16),
            p["wdq"].rearrange("(c p) n -> p c n", p=128)[:, 0:16, :])
    ctx.dma("sync", wd_b[:].rearrange("p (c n) -> p c n", c=16),
            p["wdq"].rearrange("(c p) n -> p c n", p=128)[:, 16:32, :])
    ctx.wait("tensor", "sD")

    xq_stg = arb[:, R0:R0 + 4096]
    mfm = arb[:, R0 + 4096:R0 + 8192]
    midb = arf[:, 0:4096]
    vv = arf[:, 4096:8192]
    for t in range(NT):
        ctx.wait("sync", "sV")
        ctx.dma("sync", midb, sc["mid"][128 * t:128 * (t + 1), :])
        ctx.wait("vector", "sD")
        am = tiny[:, 80:81]
        sq = tiny[:, 81:82]
        inv = tiny[:, 82:83]
        combd = tiny[:, 83:84]
        ctx.op("vector", "reduce_max", out=am, in_=midb, axis=AX,
               apply_absolute_value=True)
        ctx.fence("vector")
        ctx.op("vector", "tensor_scalar", out=am, in0=am, scalar1=1e-5,
               scalar2=None, op0=A.max)
        ctx.fence("vector")
        ctx.op("vector", "reciprocal", out=sq, in_=am)
        ctx.fence("vector")
        ctx.op("vector", "tensor_scalar", out=sq, in0=sq, scalar1=127.0,
               scalar2=None, op0=A.mult)
        ctx.fence("vector")
        ctx.op("vector", "reciprocal", out=inv, in_=sq)
        ctx.fence("vector")
        ctx.op("vector", "scalar_tensor_tensor", out=combd, in0=tiny[:, 70:71],
               scalar=-1.0, in1=inv, op0=A.mult, op1=A.mult)
        ctx.op("vector", "tensor_scalar", out=vv, in0=midb, scalar1=sq,
               scalar2=None, op0=A.mult)
        ctx.op("vector", "nop", inc="sV")
        ctx.wait("scalar", "sSp")
        qi = st["qi32"][:]
        psb = st["psb"]
        for q in range(4):
            vs = vv[:, 1024 * q:1024 * (q + 1)]
            qm1 = midb[:, 1024 * q:1024 * (q + 1)]
            dlt = arf[:, 8192:9216]
            ctx.op("vector", "tensor_copy", out=qi, in_=vs, inc="sV")
            ctx.wait("scalar", "sV")
            ctx.op("scalar", "activation", out=qm1, in_=qi, func=AF.Copy,
                   bias=-1.0, scale=1.0, inc="sC")
            ctx.wait("vector", "sC")
            ctx.op("vector", "tensor_tensor", out=dlt, in0=vs, in1=qm1,
                   op=A.subtract)
            ctx.op("vector", "scalar_tensor_tensor", out=vs, in0=dlt,
                   scalar=1.5, in1=qm1, op0=A.is_ge, op1=A.add)
            ctx.op("vector", "scalar_tensor_tensor",
                   out=xq_stg[:, 1024 * q:1024 * (q + 1)],
                   in0=dlt, scalar=0.5, in1=vs, op0=A.is_ge, op1=A.add,
                   inc="sV")
            ctx.wait("tensor", "sV")
            for c in range(8 * q, 8 * q + 8):
                off = 1024 * ((c % 8) // 4) + 128 * (c % 4)
                ctx.op("tensor", "transpose",
                       out=psb[:, off:off + 128],
                       in_=xq_stg[:, 128 * c:128 * (c + 1)], identity=identb[:],
                       inc="sT" if (c % 8 == 7) else None)
            ctx.wait("vector", "sT")
            for cc in range(8 * q, 8 * q + 8):
                off2 = 1024 * ((cc % 8) // 4) + 128 * (cc % 4)
                ctx.op("vector", "tensor_copy",
                       out=mfm[:, 128 * cc:128 * (cc + 1)],
                       in_=psb[:, off2:off2 + 128],
                       inc="sV" if cc == 8 * q + 7 else None)
            ctx.wait("tensor", "sV")
        ob0 = ps[:, 2048:2560]
        ob1 = ps[:, 2560:3072]
        for c in range(IC):
            lt = mfm[:, 128 * c:128 * (c + 1)]
            wslab = wd_a if c < 16 else wd_b
            cc = c % 16
            ctx.op("tensor", "matmul", out=ob0, lhsT=lt,
                   rhs=wslab[:, 1024 * cc:1024 * cc + 512],
                   start=(c == 0), stop=(c == IC - 1))
            ctx.op("tensor", "matmul", out=ob1, lhsT=lt,
                   rhs=wslab[:, 1024 * cc + 512:1024 * cc + 1024],
                   start=(c == 0), stop=(c == IC - 1))
        ctx.op("tensor", "nop", inc="sT")
        ctx.wait("scalar", "sT")
        ctx.wait("scalar", "sSp")
        g0t = arf[:, 8192:9216]
        ctx.op("scalar", "activation", out=g0t[:, 0:512], in_=ob0,
               func=AF.Silu, bias=0.0, scale=combd)
        ctx.op("scalar", "activation", out=g0t[:, 512:1024], in_=ob1,
               func=AF.Silu, bias=0.0, scale=combd, inc="sC")
        ctx.wait("sync", "sC")
        ctx.dma("sync", sc["g0"][128 * t:128 * (t + 1), :], g0t, inc="sSp")
        ctx.op("vector", "nop", inc="sV")
    ctx.tap("g0", sc["g0"])



def phase_D(ctx, st, s):
    arb, arf, tiny, ps, sc, masks, identb = (st["arb"], st["arf"], st["tiny"],
                                             st["ps"], st["sc"], st["masks"],
                                             st["identb"])
    base = s * S
    ctx.dma("sync", arb[:, R0:R0 + RW].rearrange("p (t n) -> p t n", t=NT),
            sc["v0_h"].rearrange("(t p) n -> p t n", p=128))
    ctx.dma("sync", arb[:, R1:R1 + RW].rearrange("p (t n) -> p t n", t=NT),
            sc["v0_l"].rearrange("(t p) n -> p t n", p=128))
    vload = ctx.val("sD")
    qb_h = [arb[:, R4:R4 + 1024], arb[:, R4 + 1024:R4 + 2048]]
    qb_l = [arb[:, R4 + 2048:R4 + 3072], arb[:, R4 + 3072:R4 + 4096]]
    pt_h = arb[:, R4 + 4096:R4 + 6144]
    pt_l = arb[:, R4 + 6144:R4 + 8192]
    ph = arb[:, R4 + 8192:R4 + 10240]
    pl = arb[:, R4 + 10240:R4 + 12288]
    srow = arf[:, 0:2048]
    prow = arf[:, 2048:4096]
    xg = [arf[:, 4096:5120], arf[:, 5120:6144]]
    x1t = arf[:, 6144:7168]

    for t in range(NT):
        qb, ql = qb_h[t % 2], qb_l[t % 2]
        ctx.wait("sync", "sT")
        ctx.dma("sync", qb[:].rearrange("p (c n) -> p c n", c=DC),
                sc["q0_h"].rearrange("(c p) n -> p c n", p=128)[:, :, 128 * t:128 * (t + 1)])
        ctx.dma("sync", ql[:].rearrange("p (c n) -> p c n", c=DC),
                sc["q0_l"].rearrange("(c p) n -> p c n", p=128)[:, :, 128 * t:128 * (t + 1)])
        ctx.wait("sync", "sV")
        ctx.dma("sync", xg[0], st["x_in"][base + 128 * t:base + 128 * (t + 1), :])
        ctx.dma("sync", xg[1], sc["g0"][128 * t:128 * (t + 1), :])
        qv = ctx.val("sD")
        ngrp = (t + 4) // 4
        ctx.wait("tensor", "sD")
        for gi in range(ngrp):
            bank = gi % 2
            ob = ps[:, 512 * bank:512 * (bank + 1)]
            ctx.wait("tensor", "sV")
            for c in range(DC):
                _mm3_acc(ctx,
                         qb[:, 128 * c:128 * (c + 1)], ql[:, 128 * c:128 * (c + 1)],
                         arb[:, R2 + 2048 * c + 512 * gi:R2 + 2048 * c + 512 * (gi + 1)],
                         arb[:, R3 + 2048 * c + 512 * gi:R3 + 2048 * c + 512 * (gi + 1)],
                         ob, first=(c == 0), last=(c == DC - 1))
            ctx.op("tensor", "nop", inc="sT")
            ctx.wait("vector", "sT")
            if gi == ngrp - 1:
                mi = t % 4
                ctx.op("vector", "scalar_tensor_tensor",
                       out=srow[:, 512 * gi:512 * (gi + 1)], in0=ob,
                       scalar=SCALE0, in1=masks[:, 512 * mi:512 * (mi + 1)],
                       op0=A.mult, op1=A.add, inc="sV")
            else:
                ctx.op("vector", "tensor_scalar",
                       out=srow[:, 512 * gi:512 * (gi + 1)], in0=ob,
                       scalar1=SCALE0, scalar2=None, op0=A.mult, inc="sV")
        keylen = 512 * ngrp
        nk = t + 1
        ctx.op("vector", "reduce_max", out=tiny[:, 80:81],
               in_=srow[:, 0:keylen], axis=AX)
        ctx.fence("vector")
        ctx.op("vector", "tensor_scalar", out=tiny[:, 81:82],
               in0=tiny[:, 80:81], scalar1=-1.0, scalar2=None, op0=A.mult,
               inc="sV")
        ctx.wait("scalar", "sV")
        ctx.op("scalar", "activation", out=prow[:, 0:keylen],
               in_=srow[:, 0:keylen], func=AF.Exp, bias=tiny[:, 81:82],
               scale=1.0, accum_out=tiny[:, 82:83], inc="sC")
        ctx.wait("vector", "sC")
        ctx.op("vector", "reciprocal", out=tiny[:, 83:84], in_=tiny[:, 82:83])
        ctx.fence("vector")
        _hl(ctx, ph[:, 0:keylen], pl[:, 0:keylen], prow[:, 0:keylen])
        ctx.op("vector", "nop", inc="sV")
        ctx.wait("tensor", "sV")
        psb = st["psb"]
        for j in range(nk):
            o1 = 1024 * (j % 2)
            ctx.op("tensor", "transpose", out=psb[:, o1:o1 + 128],
                   in_=ph[:, 128 * j:128 * (j + 1)], identity=identb[:])
            ctx.op("tensor", "transpose",
                   out=psb[:, o1 + 128:o1 + 256],
                   in_=pl[:, 128 * j:128 * (j + 1)], identity=identb[:],
                   inc="sT")
            ctx.wait("vector", "sT")
            ctx.op("vector", "tensor_copy", out=pt_h[:, 128 * j:128 * (j + 1)],
                   in_=psb[:, o1:o1 + 128])
            ctx.op("vector", "tensor_copy", out=pt_l[:, 128 * j:128 * (j + 1)],
                   in_=psb[:, o1 + 128:o1 + 256], inc="sV")
            ctx.wait("tensor", "sV")
        ctx.wait("tensor", "sD")
        for b in range(2):
            ob = ps[:, 2048 + 512 * b:2048 + 512 * (b + 1)]
            for j in range(nk):
                _mm3_acc(ctx,
                         pt_h[:, 128 * j:128 * (j + 1)],
                         pt_l[:, 128 * j:128 * (j + 1)],
                         arb[:, R0 + 1024 * j + 512 * b:R0 + 1024 * j + 512 * (b + 1)],
                         arb[:, R1 + 1024 * j + 512 * b:R1 + 1024 * j + 512 * (b + 1)],
                         ob, first=(j == 0), last=(j == nk - 1))
        ctx.op("tensor", "nop", inc="sT")
        ctx.wait("vector", "sT")
        ctx.wait("vector", "sD")
        ctx.wait("vector", "sSp")
        for b in range(2):
            ob = ps[:, 2048 + 512 * b:2048 + 512 * (b + 1)]
            sl = slice(512 * b, 512 * (b + 1))
            ctx.op("vector", "tensor_scalar", out=x1t[:, sl], in0=ob,
                   scalar1=tiny[:, 83:84], scalar2=None, op0=A.mult)
            ctx.op("vector", "tensor_tensor", out=x1t[:, sl], in0=x1t[:, sl],
                   in1=xg[1][:, sl], op=A.mult)
            ctx.op("vector", "tensor_tensor", out=x1t[:, sl], in0=x1t[:, sl],
                   in1=xg[0][:, sl], op=A.add)
        ctx.op("vector", "nop", inc="sV")
        ctx.wait("sync", "sV")
        ctx.dma("sync", sc["x1"][128 * t:128 * (t + 1), :], x1t, inc="sSp")
    ctx.tap("x1", sc["x1"])



def phase_E(ctx, st, s):
    arb, arf, tiny, ps, sc, p, onesf, trow = (st["arb"], st["arf"], st["tiny"],
                                              st["ps"], st["sc"], st["p"],
                                              st["onesf"], st["trow"])
    _ln_fm(ctx, st, sc["x1"], 0, quant=False, xn_tok_dram=sc["xn1"])
    ctx.barrier()

    wrh = arb[:, R4:R4 + 8].rearrange("p (c n) -> p c n", c=DC)
    wrl = arb[:, R4 + 8:R4 + 16].rearrange("p (c n) -> p c n", c=DC)
    ctx.dma("sync", wrh, p["wrT_h"].rearrange("(c p) n -> p c n", p=128),
            allow_slow_non_contiguous=True)
    ctx.dma("sync", wrl, p["wrT_l"].rearrange("(c p) n -> p c n", p=128),
            allow_slow_non_contiguous=True)
    ctx.wait("tensor", "sD")
    for n in range(4):
        ob = ps[0:1, 512 * n:512 * (n + 1)]
        for c in range(DC):
            _mm3_acc(ctx, wrh[:, c, :], wrl[:, c, :],
                     arb[:, R0 + 2048 * c + 512 * n:R0 + 2048 * c + 512 * (n + 1)],
                     arb[:, R1 + 2048 * c + 512 * n:R1 + 2048 * c + 512 * (n + 1)],
                     ob, first=(c == 0), last=(c == DC - 1))
    ctx.op("tensor", "nop", inc="sT")
    ctx.wait("vector", "sT")
    srw = trow[0:1, 0:S]
    ctx.op("vector", "tensor_copy", out=srw, in_=ps[0:1, 0:2048], inc="sV")
    ctx.tap("srow", sc["srow"])

    st["_srw"] = srw




def _bisect_start(ctx, st):
    arf, trow, sc = st["arf"], st["trow"], st["sc"]
    srw = st["_srw"]
    lo = trow[0:1, S:S + 1]
    hi = trow[0:1, S + 1:S + 2]
    mid = trow[0:1, S + 2:S + 3]
    cnt = trow[0:1, S + 3:S + 4]
    pred = trow[0:1, S + 4:S + 5]
    tmp = trow[0:1, S + 5:S + 6]
    tmp2 = trow[0:1, S + 6:S + 7]
    cmpb = trow[0:1, S + 8:S + 8 + S]
    ctx.wait("sync", "sV")
    ctx.dma("sync", sc["srow"].rearrange("(o t) p -> o t p", o=1),
            srw.rearrange("o (t p) -> o t p", p=128), inc="sSp")
    ctx.wait("sync", "sSp")
    ctx.dma("sync", arf[0:16, 8960:9088], sc["srow"])
    ctx.op("vector", "reduce_max", out=hi, in_=srw, axis=AX,
           apply_absolute_value=True)
    ctx.fence("vector")
    ctx.op("vector", "tensor_scalar", out=hi, in0=hi, scalar1=1.0,
           scalar2=None, op0=A.add)
    ctx.fence("vector")
    ctx.op("vector", "tensor_scalar", out=lo, in0=hi, scalar1=-1.0,
           scalar2=None, op0=A.mult)
    ctx.fence("vector")
    ctx.op("vector", "tensor_tensor", out=mid, in0=lo, in1=hi, op=A.add)
    ctx.fence("vector")
    ctx.op("vector", "tensor_scalar", out=mid, in0=mid, scalar1=0.5,
           scalar2=None, op0=A.mult)
    ctx.fence("vector")
    K = S // 2
    for _ in range(44):
        ctx.op("vector", "tensor_scalar", out=cmpb, in0=srw, scalar1=mid,
               scalar2=0.0, op0=A.is_ge, op1=A.add, accum_out=cnt)
        ctx.op("vector", "tensor_tensor", out=tmp, in0=mid, in1=lo,
               op=A.subtract)
        ctx.op("vector", "tensor_tensor", out=tmp2, in0=hi, in1=mid,
               op=A.subtract)
        ctx.fence("vector")
        ctx.op("vector", "scalar_tensor_tensor", out=pred, in0=cnt,
               scalar=float(K), in1=tmp, op0=A.is_ge, op1=A.mult)
        ctx.op("vector", "scalar_tensor_tensor", out=tmp2, in0=cnt,
               scalar=float(K), in1=tmp2, op0=A.is_ge, op1=A.mult)
        ctx.fence("vector")
        ctx.op("vector", "tensor_tensor", out=lo, in0=lo, in1=pred, op=A.add)
        ctx.op("vector", "tensor_tensor", out=hi, in0=mid, in1=tmp2, op=A.add)
        ctx.fence("vector")
        ctx.op("vector", "tensor_tensor", out=mid, in0=lo, in1=hi, op=A.add)
        ctx.fence("vector")
        ctx.op("vector", "tensor_scalar", out=mid, in0=mid, scalar1=0.5,
               scalar2=None, op0=A.mult)
        ctx.fence("vector")
    ctx.op("vector", "nop", inc="sV")


def _bisect_finish(ctx, st):
    arf, tiny, ps, trow, onesf = (st["arf"], st["tiny"], st["ps"], st["trow"],
                                  st["onesf"])
    lo = trow[0:1, S:S + 1]
    ctx.wait("tensor", "sV")
    ctx.op("tensor", "matmul", out=ps[:, 2560:2561], lhsT=onesf[:], rhs=lo,
           start=True, stop=True)
    ctx.wait("tensor", "sD")
    ctx.op("tensor", "transpose", out=ps[:, 1536:1552],
           in_=arf[0:16, 8960:9088], identity=st["identf"][0:16, 0:16],
           inc="sT")
    ctx.wait("vector", "sT")
    ctx.op("vector", "tensor_copy", out=tiny[:, 16:16 + NT],
           in_=ps[:, 1536:1552])
    ctx.fence("vector")
    thr = tiny[:, 32:33]
    ctx.op("vector", "tensor_copy", out=thr, in_=ps[:, 2560:2561], inc="sV")
    ctx.fence("vector")
    ctx.wait("scalar", "sV")
    sig = tiny[:, 48:48 + NT]
    ctx.op("scalar", "activation", out=sig, in_=tiny[:, 16:16 + NT],
           func=AF.Sigmoid, inc="sC")
    ctx.wait("vector", "sC")
    msk = tiny[:, 0:NT]
    ctx.op("vector", "tensor_scalar", out=msk, in0=tiny[:, 16:16 + NT],
           scalar1=thr, scalar2=None, op0=A.is_ge)
    ctx.fence("vector")
    ctx.op("vector", "tensor_tensor", out=msk, in0=msk, in1=sig, op=A.mult,
           inc="sV")
    ctx.tap("wtok", msk)



def phase_F(ctx, st, s):
    arb, arf, tiny, ps, sc, p, identf = (st["arb"], st["arf"], st["tiny"],
                                         st["ps"], st["sc"], st["p"],
                                         st["identf"])
    cin = st["cin"]
    gl = arf[:, 0:512]
    upd = arf[:, 512:1024]
    updtok = arf[:, 1024:5120]
    x1g = arf[:, 5120:6144]
    b1fm = tiny[:, 104:136]
    b2fm = tiny[:, 136:144]
    g1t = arf[:, 6208:7232]
    g1o = arf[:, 7232:8256]
    ctx.dma("sync", b1fm, cin["b1fm"])
    ctx.dma("sync", b2fm, cin["b2fm"])
    w1h = [arb[:, R4:R4 + 1024], arb[:, R4 + 1024:R4 + 2048]]
    w1l = [arb[:, R4 + 2048:R4 + 3072], arb[:, R4 + 3072:R4 + 4096]]
    w2h = arb[:, R4 + 4096:R4 + 8192]
    w2l = arb[:, R4 + 8192:R4 + 12288]
    hfm_h = arb[:, R2:R2 + RW]
    hfm_l = arb[:, R3:R3 + RW]
    ctx.wait("tensor", "sD")
    ctx.wait("scalar", "sD")
    _bisect_start(ctx, st)
    pipeF1 = Pipe(ctx, "sV", 2)
    pipeF2 = Pipe(ctx, "sC", 2)

    for grp in range(4):
        for fc in range(IC):
            pb = fc % 2
            ctx.wait("sync", "sT")
            ctx.dma("sync", w1h[pb][:].rearrange("p (c n) -> p c n", c=DC),
                    p["W1_h"].rearrange("(c p) n -> p c n", p=128)[:, :, 128 * fc:128 * (fc + 1)])
            ctx.dma("sync", w1l[pb][:].rearrange("p (c n) -> p c n", c=DC),
                    p["W1_l"].rearrange("(c p) n -> p c n", p=128)[:, :, 128 * fc:128 * (fc + 1)])
            ctx.wait("tensor", "sD")
            bank = fc % 2
            ob = ps[:, 512 * bank:512 * (bank + 1)]
            pipeF1.pre()
            for c in range(DC):
                _mm3_acc(ctx,
                         w1h[pb][:, 128 * c:128 * (c + 1)],
                         w1l[pb][:, 128 * c:128 * (c + 1)],
                         arb[:, R0 + 2048 * c + 512 * grp:R0 + 2048 * c + 512 * (grp + 1)],
                         arb[:, R1 + 2048 * c + 512 * grp:R1 + 2048 * c + 512 * (grp + 1)],
                         ob, first=(c == 0), last=(c == DC - 1))
            ctx.op("tensor", "nop", inc="sT")
            ctx.wait("scalar", "sT")
            ctx.wait("scalar", "sV")
            ctx.op("scalar", "activation", out=gl, in_=ob,
                   func=AF.Gelu_apprx_tanh, bias=b1fm[:, fc:fc + 1],
                   scale=1.0, inc="sC")
            ctx.wait("vector", "sC")
            _hl(ctx, hfm_h[:, 512 * fc:512 * (fc + 1)],
                hfm_l[:, 512 * fc:512 * (fc + 1)], gl)
            ctx.op("vector", "nop", inc="sV")
            pipeF1.post()
        if grp == 0:
            _bisect_finish(ctx, st)
        for dc in range(DC):
            ctx.wait("sync", "sT")
            ctx.dma("sync", w2h[:].rearrange("p (c n) -> p c n", c=IC),
                    p["W2_h"].rearrange("(c p) n -> p c n", p=128)[:, :, 128 * dc:128 * (dc + 1)])
            ctx.dma("sync", w2l[:].rearrange("p (c n) -> p c n", c=IC),
                    p["W2_l"].rearrange("(c p) n -> p c n", p=128)[:, :, 128 * dc:128 * (dc + 1)])
            ctx.wait("tensor", "sD")
            bank = 2 + dc % 2
            ob = ps[:, 512 * bank:512 * (bank + 1)]
            pipeF2.pre()
            ctx.wait("tensor", "sV")
            for c in range(IC):
                _mm3_acc(ctx,
                         w2h[:, 128 * c:128 * (c + 1)],
                         w2l[:, 128 * c:128 * (c + 1)],
                         hfm_h[:, 512 * c:512 * (c + 1)],
                         hfm_l[:, 512 * c:512 * (c + 1)],
                         ob, first=(c == 0), last=(c == IC - 1))
            ctx.op("tensor", "nop", inc="sT")
            ctx.wait("vector", "sT")
            ctx.op("vector", "tensor_scalar", out=upd, in0=ob,
                   scalar1=b2fm[:, dc:dc + 1], scalar2=None, op0=A.add,
                   inc="sC")
            pipeF2.post()
            ctx.wait("tensor", "sC")
            for j in range(4):
                bk = 4 + j % 2
                ctx.op("tensor", "transpose", out=ps[:, 512 * bk:512 * bk + 128],
                       in_=upd[:, 128 * j:128 * (j + 1)], identity=identf[:],
                       inc="sT")
                ctx.wait("vector", "sT")
                ctx.op("vector", "tensor_copy",
                       out=updtok[:, 1024 * j + 128 * dc:1024 * j + 128 * (dc + 1)],
                       in_=ps[:, 512 * bk:512 * bk + 128], inc="sV")
                ctx.wait("tensor", "sV")
        for j in range(4):
            t = 4 * grp + j
            ctx.wait("sync", "sV")
            ctx.dma("sync", x1g, sc["xn1"][128 * t:128 * (t + 1), :])
            ctx.wait("vector", "sD")
            ctx.op("vector", "scalar_tensor_tensor", out=g1t,
                   in0=updtok[:, 1024 * j:1024 * (j + 1)],
                   scalar=tiny[:, t:t + 1], in1=x1g, op0=A.mult, op1=A.add,
                   inc="sV")
            ctx.wait("scalar", "sV")
            ctx.wait("scalar", "sSp")
            ctx.op("scalar", "activation", out=g1o, in_=g1t, func=AF.Silu,
                   inc="sC")
            ctx.wait("sync", "sC")
            ctx.dma("sync", sc["g1"][128 * t:128 * (t + 1), :], g1o, inc="sSp")
    ctx.tap("g1", sc["g1"])



def phase_G(ctx, st, s):
    arb, arf, ps, sc, p = st["arb"], st["arf"], st["ps"], st["sc"], st["p"]
    ev_h = st["stg"][:, 0:512]
    ev_l = st["stg"][:, 512:1024]
    pipeG = Pipe(ctx, "sV", 4)
    wch = [arb[:, R4:R4 + 1024], arb[:, R4 + 1024:R4 + 2048]]
    wcl = [arb[:, R4 + 2048:R4 + 3072], arb[:, R4 + 3072:R4 + 4096]]

    for m in range(16):
        pb = m % 2
        ctx.wait("sync", "sT")
        ctx.dma("sync", wch[pb][:].rearrange("p (c n) -> p c n", c=DC),
                p["wqkvT_h"].rearrange("(c p) n -> p c n", p=128)[:, :, 128 * m:128 * (m + 1)])
        ctx.dma("sync", wcl[pb][:].rearrange("p (c n) -> p c n", c=DC),
                p["wqkvT_l"].rearrange("(c p) n -> p c n", p=128)[:, :, 128 * m:128 * (m + 1)])
        ctx.wait("tensor", "sD")
        dst_h = sc["q1_h"] if m < 8 else sc["k1_h"]
        dst_l = sc["q1_l"] if m < 8 else sc["k1_l"]
        mm = m % 8
        for n in range(4):
            bank = n % 4
            ob = ps[:, 512 * bank:512 * (bank + 1)]
            pipeG.pre()
            for c in range(DC):
                _mm3_acc(ctx, wch[pb][:, 128 * c:128 * (c + 1)],
                         wcl[pb][:, 128 * c:128 * (c + 1)],
                         arb[:, R0 + 2048 * c + 512 * n:R0 + 2048 * c + 512 * (n + 1)],
                         arb[:, R1 + 2048 * c + 512 * n:R1 + 2048 * c + 512 * (n + 1)],
                         ob, first=(c == 0), last=(c == DC - 1))
            ctx.op("tensor", "nop", inc="sT")
            ctx.wait("vector", "sT")
            ctx.wait("vector", "sSp")
            _hl(ctx, ev_h[:, 0:512], ev_l[:, 0:512], ob)
            ctx.op("vector", "nop", inc="sV")
            ctx.wait("sync", "sV")
            ctx.dma("sync", dst_h[128 * mm:128 * (mm + 1), 512 * n:512 * (n + 1)],
                    ev_h[:, 0:512], inc="sSp")
            ctx.dma("sync", dst_l[128 * mm:128 * (mm + 1), 512 * n:512 * (n + 1)],
                    ev_l[:, 0:512], inc="sSp")
            pipeG.post()
    vwh = arb[:, R4:R4 + 8192]
    vwl = arb[:, R4 + 8192:R4 + 16384]
    ctx.wait("sync", "sT")
    ctx.dma("sync", vwh[:].rearrange("p (c n) -> p c n", c=DC),
            p["wqkvT_h"].rearrange("(c p) n -> p c n", p=128)[:, :, 2048:3072])
    ctx.dma("sync", vwl[:].rearrange("p (c n) -> p c n", c=DC),
            p["wqkvT_l"].rearrange("(c p) n -> p c n", p=128)[:, :, 2048:3072])
    ctx.wait("tensor", "sD")
    for t in range(NT):
        for b in range(2):
            bank = (t * 2 + b) % 4
            ob = ps[:, 512 * bank:512 * (bank + 1)]
            pipeG.pre()
            for c in range(DC):
                xh = arb[:, R0 + 2048 * c + 128 * t:R0 + 2048 * c + 128 * t + 128]
                xl = arb[:, R1 + 2048 * c + 128 * t:R1 + 2048 * c + 128 * t + 128]
                _mm3_acc(ctx, xh, xl,
                         vwh[:, 1024 * c + 512 * b:1024 * c + 512 * (b + 1)],
                         vwl[:, 1024 * c + 512 * b:1024 * c + 512 * (b + 1)],
                         ob, first=(c == 0), last=(c == DC - 1))
            ctx.op("tensor", "nop", inc="sT")
            ctx.wait("vector", "sT")
            a2 = R2 + 1024 * t + 512 * b
            a3 = R3 + 1024 * t + 512 * b
            _hl(ctx, arb[:, a2:a2 + 512], arb[:, a3:a3 + 512], ob)
            ctx.op("vector", "nop", inc="sV")
            pipeG.post()



def phase_H(ctx, st, s):
    arb, arf, tiny, ps, sc, p, identb = (st["arb"], st["arf"], st["tiny"],
                                         st["ps"], st["sc"], st["p"],
                                         st["identb"])
    onesf, brow = st["onesf"], st["brow"]
    cin = st["cin"]
    base = s * S
    pj_h = arb[:, R4:R4 + 1024]
    pj_l = arb[:, R4 + 1024:R4 + 2048]
    ctx.dma("sync", pj_h[:].rearrange("p (c n) -> p c n", c=DC),
            p["projT_h"].rearrange("(c p) n -> p c n", p=128))
    ctx.dma("sync", pj_l[:].rearrange("p (c n) -> p c n", c=DC),
            p["projT_l"].rearrange("(c p) n -> p c n", p=128))
    ctx.dma("sync", brow[:], cin["boutr"])
    hr = R4 + 2048
    qh_h = arb[:, hr:hr + 2048]
    qh_l = arb[:, hr + 2048:hr + 4096]
    kh_h = arb[:, hr + 4096:hr + 6144]
    kh_l = arb[:, hr + 6144:hr + 8192]
    qp_h, qp_l, kp_h, kp_l = qh_h, qh_l, kh_h, kh_l
    at_h = arb[:, R4 + 10240:R4 + 10368]
    at_l = arb[:, R4 + 10368:R4 + 10496]
    ks_h = arb[:, R4 + 10496:R4 + 10560]
    ks_l = arb[:, R4 + 10560:R4 + 10624]
    an_h = arb[:, R4 + 10624:R4 + 10752]
    an_l = arb[:, R4 + 10752:R4 + 10880]
    qpf = arf[:, 0:2048]
    kpf = arf[:, 2048:4096]
    ksf = arf[:, 4096:4160]
    zbuf = arf[:, 4160:4176]
    atf = arf[:, 4304:4432]
    att = arf[:, 4432:4560]
    rlu = arf[:, 4560:5072]
    ctx.wait("tensor", "sD")

    for h in range(HEADS):
        ctx.wait("sync", "sT")
        ctx.dma("sync", qh_h, sc["q1_h"][128 * h:128 * (h + 1), :])
        ctx.dma("sync", qh_l, sc["q1_l"][128 * h:128 * (h + 1), :])
        ctx.dma("sync", kh_h, sc["k1_h"][128 * h:128 * (h + 1), :])
        ctx.dma("sync", kh_l, sc["k1_l"][128 * h:128 * (h + 1), :])
        ctx.wait("tensor", "sD")
        for src_h, src_l, dst in ((qh_h, qh_l, qpf), (kh_h, kh_l, kpf)):
            for n in range(4):
                bank = n % 4
                ob = ps[:, 512 * bank:512 * (bank + 1)]
                ctx.wait("tensor", "sV")
                _mm3_acc(ctx, pj_h[:, 128 * h:128 * (h + 1)],
                         pj_l[:, 128 * h:128 * (h + 1)],
                         src_h[:, 512 * n:512 * (n + 1)],
                         src_l[:, 512 * n:512 * (n + 1)],
                         ob, first=True, last=True)
                ctx.op("tensor", "nop", inc="sT")
                ctx.wait("vector", "sT")
                sl = slice(512 * n, 512 * (n + 1))
                ctx.op("vector", "tensor_scalar", out=dst[:, sl], in0=ob,
                       scalar1=0.0, scalar2=None, op0=A.min, inc="sV")
                ctx.wait("scalar", "sV")
                ctx.op("scalar", "activation", out=dst[:, sl], in_=dst[:, sl],
                       func=AF.Exp)
                ctx.op("scalar", "activation", out=rlu, in_=ob, func=AF.Relu,
                       inc="sC")
                ctx.wait("vector", "sC")
                ctx.op("vector", "tensor_tensor", out=dst[:, sl],
                       in0=dst[:, sl], in1=rlu, op=A.add, inc="sV")
        _hl(ctx, qp_h, qp_l, qpf)
        _hl(ctx, kp_h, kp_l, kpf)
        ctx.op("vector", "tensor_reduce", out=ksf,
               in_=kpf.rearrange("p (b s) -> p b s", s=BLK), axis=AX, op=A.add)
        ctx.fence("vector")
        _hl(ctx, ks_h, ks_l, ksf, fence=True)
        ctx.op("vector", "nop", inc="sV")
        ctx.wait("tensor", "sV")
        for gi in range(NT):
            c0 = 128 * gi
            obA = ps[:, 2048:2176]
            ctx.op("tensor", "matmul", out=obA, lhsT=kp_h[:, c0:c0 + 128],
                   rhs=qp_h[:, c0:c0 + 128], start=True, stop=False)
            ctx.op("tensor", "matmul", out=obA, lhsT=kp_h[:, c0:c0 + 128],
                   rhs=qp_l[:, c0:c0 + 128], start=False, stop=False)
            ctx.op("tensor", "matmul", out=obA, lhsT=kp_l[:, c0:c0 + 128],
                   rhs=qp_h[:, c0:c0 + 128], start=False, stop=True)
            for b in range(4):
                zo = ps[32 * b:32 * (b + 1), 2560:2561]
                tp = (0, 32 * b)
                ctx.op("tensor", "matmul", out=zo,
                       lhsT=qp_h[:, c0 + 32 * b:c0 + 32 * (b + 1)],
                       rhs=ks_h[:, 4 * gi + b:4 * gi + b + 1],
                       tile_position=tp, start=True, stop=False)
                ctx.op("tensor", "matmul", out=zo,
                       lhsT=qp_h[:, c0 + 32 * b:c0 + 32 * (b + 1)],
                       rhs=ks_l[:, 4 * gi + b:4 * gi + b + 1],
                       tile_position=tp, start=False, stop=False)
                ctx.op("tensor", "matmul", out=zo,
                       lhsT=qp_l[:, c0 + 32 * b:c0 + 32 * (b + 1)],
                       rhs=ks_h[:, 4 * gi + b:4 * gi + b + 1],
                       tile_position=tp, start=False, stop=True)
            ctx.op("tensor", "nop", inc="sT")
            ctx.wait("vector", "sT")
            ctx.op("vector", "tensor_copy", out=atf, in_=obA)
            ctx.fence("vector")
            _hl(ctx, at_h, at_l, atf, fence=True)
            ctx.op("vector", "tensor_scalar", out=zbuf[:, 0:1],
                   in0=ps[:, 2560:2561], scalar1=1e-8, scalar2=None, op0=A.add)
            ctx.fence("vector")
            ctx.op("vector", "reciprocal", out=zbuf[:, 1:2], in_=zbuf[:, 0:1],
                   inc="sV")
            ctx.fence("vector")
            ctx.wait("tensor", "sV")
            obN = ps[:, 0:128]
            for b in range(4):
                vh = arb[32 * b:32 * (b + 1),
                         R2 + 1024 * gi + 128 * h:R2 + 1024 * gi + 128 * (h + 1)]
                vl = arb[32 * b:32 * (b + 1),
                         R3 + 1024 * gi + 128 * h:R3 + 1024 * gi + 128 * (h + 1)]
                ah = at_h[32 * b:32 * (b + 1), 32 * b:32 * (b + 1)]
                al = at_l[32 * b:32 * (b + 1), 32 * b:32 * (b + 1)]
                on = obN[32 * b:32 * (b + 1), :]
                tp = (32 * b, 32 * b)
                ctx.op("tensor", "matmul", out=on, lhsT=ah, rhs=vh,
                       tile_position=tp, start=True, stop=False)
                ctx.op("tensor", "matmul", out=on, lhsT=ah, rhs=vl,
                       tile_position=tp, start=False, stop=False)
                ctx.op("tensor", "matmul", out=on, lhsT=al, rhs=vh,
                       tile_position=tp, start=False, stop=True)
            ctx.op("tensor", "nop", inc="sT")
            ctx.wait("vector", "sT")
            ctx.op("vector", "tensor_scalar", out=att, in0=obN,
                   scalar1=zbuf[:, 1:2], scalar2=None, op0=A.mult)
            ctx.fence("vector")
            _hl(ctx, an_h, an_l, att, fence=True)
            ctx.op("vector", "nop", inc="sV")
            ctx.wait("tensor", "sV")
            psb = st["psb"]
            ctx.op("tensor", "transpose", out=psb[:, 0:128],
                   in_=an_h, identity=identb[:])
            ctx.op("tensor", "transpose", out=psb[:, 128:256],
                   in_=an_l, identity=identb[:], inc="sT")
            ctx.wait("vector", "sT")
            a0 = R0 + 2048 * h + 128 * gi
            a1 = R1 + 2048 * h + 128 * gi
            ctx.op("vector", "tensor_copy", out=arb[:, a0:a0 + 128],
                   in_=psb[:, 0:128])
            ctx.op("vector", "tensor_copy", out=arb[:, a1:a1 + 128],
                   in_=psb[:, 128:256], inc="sV")
            ctx.wait("tensor", "sV")
    ctx.tap("attnfm_hi", arb[:, R0:R0 + RW])

    woh = arb[:, R4:R4 + 8192]
    wol = arb[:, R4 + 8192:R4 + 16384]
    ctx.wait("sync", "sT")
    ctx.dma("sync", woh[:].rearrange("p (c n) -> p c n", c=DC),
            p["woT_h"].rearrange("(c p) n -> p c n", p=128))
    ctx.dma("sync", wol[:].rearrange("p (c n) -> p c n", c=DC),
            p["woT_l"].rearrange("(c p) n -> p c n", p=128))
    ctx.wait("tensor", "sD")
    x1t = arf[:, 0:1024]
    g1t = arf[:, 1024:2048]
    ot = arf[:, 2048:3072]
    for t in range(NT):
        ctx.wait("sync", "sV")
        ctx.dma("sync", x1t, sc["x1"][128 * t:128 * (t + 1), :])
        ctx.dma("sync", g1t, sc["g1"][128 * t:128 * (t + 1), :])
        dv = ctx.val("sD")
        for b in range(2):
            ob = ps[:, 512 * b:512 * (b + 1)]
            ctx.wait("tensor", "sV")
            ctx.op("tensor", "matmul", out=ob, lhsT=onesf[:],
                   rhs=brow[0:1, 512 * b:512 * (b + 1)], start=True, stop=False)
            for c in range(DC):
                _mm3_acc(ctx,
                         arb[:, R0 + 2048 * c + 128 * t:R0 + 2048 * c + 128 * t + 128],
                         arb[:, R1 + 2048 * c + 128 * t:R1 + 2048 * c + 128 * t + 128],
                         woh[:, 1024 * c + 512 * b:1024 * c + 512 * (b + 1)],
                         wol[:, 1024 * c + 512 * b:1024 * c + 512 * (b + 1)],
                         ob, first=False, last=(c == DC - 1))
            ctx.op("tensor", "nop", inc="sT")
        ctx.wait("vector", "sT")
        ctx.wait("vector", "sD")
        ctx.wait("vector", "sSp")
        for b in range(2):
            ob = ps[:, 512 * b:512 * (b + 1)]
            sl = slice(512 * b, 512 * (b + 1))
            ctx.op("vector", "tensor_tensor", out=ot[:, sl], in0=ob,
                   in1=g1t[:, sl], op=A.mult)
            ctx.op("vector", "tensor_tensor", out=ot[:, sl], in0=ot[:, sl],
                   in1=x1t[:, sl], op=A.add)
        ctx.op("vector", "nop", inc="sV")
        ctx.wait("sync", "sV")
        ctx.dma("sync", st["out"][base + 128 * t:base + 128 * (t + 1), :], ot,
                inc="sSp")



def _mk_consts():
    identf = np.eye(128, dtype=np.float32)
    identb = np.eye(128, dtype=ml_dtypes.bfloat16)
    onesf = np.ones((1, 128), dtype=np.float32)
    onescol = np.ones((128, 1), dtype=np.float32)
    masks = np.zeros((128, 4 * 512), dtype=np.float32)
    for pp in range(4):
        m = np.zeros((128, 512), dtype=np.float32)
        m[:, 128 * (pp + 1):] = NEG
        m[:, 128 * pp:128 * (pp + 1)] = np.triu(
            np.full((128, 128), NEG, dtype=np.float32), 1)
        masks[:, 512 * pp:512 * (pp + 1)] = m
    return identf, identb, onesf, onescol, masks


def kernel(L0_attn_x, L0_attn_WQ, L0_attn_WK, L0_attn_WV, L0_ffn_w_gate,
           L0_ffn_w_up, L0_ffn_w_down, L1_attn_w_qkv, L1_attn_w_out,
           L1_attn_b_out, L1_attn_proj, L1_ffn_w_router, L1_ffn_W1, L1_ffn_b1,
           L1_ffn_W2, L1_ffn_b2, _dbg=None, _cores=None, _stop=None):
    x = np.ascontiguousarray(np.asarray(L0_attn_x, dtype=np.float32))
    identf, identb, onesf, onescol, masks = _mk_consts()
    T = lambda a: np.ascontiguousarray(np.asarray(a, dtype=np.float32).T)
    C = lambda a: np.ascontiguousarray(np.asarray(a, dtype=np.float32))
    w = dict(
        wqT=T(L0_attn_WQ), wkT=T(L0_attn_WK), wvT=T(L0_attn_WV),
        wgT=T(L0_ffn_w_gate), wuT=T(L0_ffn_w_up), wdT=T(L0_ffn_w_down),
        wqkvT=T(L1_attn_w_qkv), woT=T(L1_attn_w_out),
        projT=C(L1_attn_proj).reshape(D, FD),
        W1=C(L1_ffn_W1), W2=C(L1_ffn_W2),
        wrT=C(L1_ffn_w_router).reshape(D, 1),
        identf=identf, identb=identb, onesf=onesf, onescol=onescol,
        masks=masks.astype(ml_dtypes.bfloat16),
        b1fm=np.ascontiguousarray(C(L1_ffn_b1).reshape(IC, 128).T),
        b2fm=np.ascontiguousarray(C(L1_ffn_b2).reshape(DC, 128).T),
        boutr=C(L1_attn_b_out).reshape(1, D),
        bconst=np.full((128, 1), 1e-5, dtype=np.float32),
    )
    nc = build_core(dbg=_dbg, stop=_stop)
    ncores = NCORE if _cores is None else _cores
    in_maps = []
    for c in range(ncores):
        m = dict(w)
        m["x"] = np.ascontiguousarray(x[NSEQ * c:NSEQ * (c + 1)].reshape(NSEQ * S, D))
        in_maps.append(m)
    res = run_bass_kernel_spmd(nc, in_maps, core_ids=list(range(ncores)))
    if _dbg:
        return res
    outs = [np.asarray(r["out"]).reshape(NSEQ, S, D) for r in res.results]
    return np.concatenate(outs, axis=0).astype(np.float32)


# version 08
# speedup vs baseline: 1.1432x; 1.0170x over previous
import numpy as np
import ml_dtypes
import concourse.bass as bass
import concourse.mybir as mybir
from concourse.bass_utils import run_bass_kernel_spmd
from concourse.alu_op_type import AluOpType as A

F32 = mybir.dt.float32
BF16 = mybir.dt.bfloat16
I32 = mybir.dt.int32
AX = mybir.AxisListType.X
AF = mybir.ActivationFunctionType

S = 2048
D = 1024
DC = D // 128
NT = S // 128
INNER = 4096
IC = INNER // 128
HEADS = 8
HD = 128
FD = 128
BLK = 32
NSEQ = 2
NCORE = 8
SCALE0 = 1024 ** -0.5
NEG = -1.0e30

RW = 16384
R0, R1, R2, R3, R4 = 0, RW, 2 * RW, 3 * RW, 4 * RW


class Ctx:
    ENGS = ("sync", "gpsimd", "vector", "scalar", "tensor")

    def __init__(self, nc, dbg):
        self.nc = nc
        self.dbg = set(dbg or ())
        self.prog = {e: [] for e in self.ENGS}
        self.cnt = {}
        self.sems = {}
        self.dbg_out = {}
        self.waited = {}
        self.prior_waited = {}
        self.touch = {"vector": [], "scalar": []}

    def emit(self, eng, fn):
        self.prog[eng].append(fn)

    def inc(self, sem, amt=1):
        self.cnt[sem] = self.cnt.get(sem, 0) + amt
        return self.cnt[sem]

    def val(self, sem):
        return self.cnt.get(sem, 0)

    def op(self, eng, meth, *args, inc=None, inc_amt=1, **kw):
        sems = self.sems
        if inc is not None:
            self.inc(inc, inc_amt)
        if eng == "tensor" and meth == "nop" and inc is not None:
            meth = "drain"
        fence = inc is not None and eng in ("vector", "scalar")
        pre_drain = False
        if eng in ("vector", "scalar") and meth not in ("nop", "drain"):
            rd, wr = _op_aps(kw)
            hist = self.touch[eng]
            if _conflicts(rd, wr, hist):
                pre_drain = True
                hist.clear()
            hist.extend((a, False) for a in rd)
            hist.extend((a, True) for a in wr)
            if fence:
                hist.clear()

        skip_op = eng in ("vector", "scalar") and meth == "nop" and inc is not None

        def fn(e, meth=meth, args=args, kw=kw, inc=inc, inc_amt=inc_amt,
               fence=fence, pre_drain=pre_drain, skip_op=skip_op):
            if pre_drain:
                e.drain()
            if skip_op:
                e.drain()
                e.drain().then_inc(sems[inc], inc_amt)
                return
            r = getattr(e, meth)(*args, **kw)
            if inc is not None:
                if fence:
                    e.drain().then_inc(sems[inc], inc_amt)
                else:
                    r.then_inc(sems[inc], inc_amt)
        self.emit(eng, fn)

    def fence(self, eng):
        self.touch[eng].clear()
        sems = self.sems
        n = self.inc("f" + eng)

        def fn(e, n=n, s="f" + eng):
            e.drain()
            e.drain().then_inc(sems[s], 1)
            e.wait_ge(sems[s], n)
        self.emit(eng, fn)

    def wait(self, eng, sem, v=None):
        v = self.val(sem) if v is None else v
        self.waited.setdefault(sem, set()).add(v)
        sems = self.sems
        self.emit(eng, lambda e, s=sem, v=v: e.wait_ge(sems[s], v))

    def dma(self, eng, out, in_, inc="sD", **kw):
        if self.val(inc) in self.prior_waited.get(inc, ()):  
            self.wait(eng, inc)
        self.op(eng, "dma_start", inc=inc, inc_amt=16, out=out, in_=in_, **kw)

    def barrier(self):
        self.wait("sync", "sD")
        self.wait("sync", "sSp")
        for sm in ("sWa", "sWb", "sPa", "sPb"):
            self.wait("sync", sm)
        self.wait("gpsimd", "sG")
        self.wait("gpsimd", "sGa")
        self.wait("gpsimd", "sGb")
        for e in self.ENGS:
            self.op(e, "nop", inc="b" + e)
        for e in self.ENGS:
            for o in self.ENGS:
                if o != e:
                    self.wait(e, "b" + o)

    def tap(self, name, src_ap):
        if name not in self.dbg:
            return
        self.barrier()
        self.dma("sync", self.dbg_out[name], src_ap)
        self.barrier()


def _ap_region(x):
    try:
        apl = x.ap
        off = x.offset
        nm = x.name
    except AttributeError:
        return None
    if not apl:
        return (nm, 0, 1 << 60, 0, 128)
    rowstep = apl[0][0]
    if rowstep <= 0:
        return (nm, 0, 1 << 60, 0, 128)
    p0 = off // rowstep
    c0 = off % rowstep
    span = 1
    for step, count in apl[1:]:
        span += step * (count - 1)
    return (nm, c0, c0 + span, p0, p0 + apl[0][1])


def _op_aps(kw):
    rd, wr = [], []
    for k, v in kw.items():
        r = _ap_region(v) if not isinstance(v, (int, float, bool, type(None))) else None
        if r is None:
            continue
        if k in ("out", "accum_out"):
            wr.append(r)
        else:
            rd.append(r)
    return rd, wr


def _overlap(a, b):
    return (a[0] == b[0] and a[1] < b[2] and b[1] < a[2]
            and a[3] < b[4] and b[3] < a[4])


def _conflicts(rd, wr, hist):
    for h, h_is_wr in hist:
        if h_is_wr:
            for a in rd + wr:
                if _overlap(a, h):
                    return True
        else:
            for a in wr:
                if _overlap(a, h):
                    return True
    return False


class Pipe:

    def __init__(self, ctx, sem="sV", lag=4, eng="tensor"):
        self.ctx, self.sem, self.lag, self.eng = ctx, sem, lag, eng
        self.hist = []

    def pre(self):
        if len(self.hist) >= self.lag:
            self.ctx.wait(self.eng, self.sem, self.hist[-self.lag])

    def post(self):
        self.hist.append(self.ctx.val(self.sem))


def _hl(ctx, dst_hi, dst_lo, src_f32, fence=False):
    ctx.op("vector", "tensor_copy", out=dst_hi, in_=src_f32)
    if fence:
        ctx.fence("vector")
    ctx.op("vector", "tensor_tensor", out=dst_lo, in0=src_f32, in1=dst_hi,
           op=A.subtract)


def _mm3_acc(ctx, lhsT_h, lhsT_l, rhs_h, rhs_l, out, first, last):
    ctx.op("tensor", "matmul", out=out, lhsT=lhsT_h, rhs=rhs_h,
           start=first, stop=False)
    ctx.op("tensor", "matmul", out=out, lhsT=lhsT_h, rhs=rhs_l,
           start=False, stop=False)
    ctx.op("tensor", "matmul", out=out, lhsT=lhsT_l, rhs=rhs_h,
           start=False, stop=last)


def build_core(dbg=None, stop=None):
    nc = bass.Bass("TRN2", target_bir_lowering=False)
    ctx = Ctx(nc, dbg)

    x_in = nc.dram_tensor("x", [NSEQ * S, D], F32, kind="ExternalInput").ap()
    out = nc.dram_tensor("out", [NSEQ * S, D], F32, kind="ExternalOutput").ap()

    win = {}
    for nm, shp in [
        ("wqT", [D, D]), ("wkT", [D, D]), ("wvT", [D, D]),
        ("wgT", [D, INNER]), ("wuT", [D, INNER]), ("wdT", [INNER, D]),
        ("wqkvT", [D, 3 * D]), ("woT", [D, D]), ("projT", [D, FD]),
        ("W1", [D, INNER]), ("W2", [INNER, D]), ("wrT", [D, 1]),
    ]:
        win[nm] = nc.dram_tensor(nm, shp, F32, kind="ExternalInput").ap()
    cin = {}
    for nm, shp, dtp in [
        ("identf", [128, 128], F32), ("identb", [128, 128], BF16),
        ("onesf", [1, 128], F32), ("onescol", [128, 1], F32),
        ("masks", [128, 4 * 512], BF16), ("b1fm", [128, IC], F32),
        ("b2fm", [128, DC], F32), ("boutr", [1, D], F32),
        ("bconst", [128, 1], F32),
    ]:
        cin[nm] = nc.dram_tensor(nm, shp, dtp, kind="ExternalInput").ap()

    dt = nc.dram_tensor
    p = {}
    for nm, shp in [
        ("wqT_h", [D, D]), ("wqT_l", [D, D]), ("wkT_h", [D, D]), ("wkT_l", [D, D]),
        ("wvT_h", [D, D]), ("wvT_l", [D, D]),
        ("wqkvT_h", [D, 3 * D]), ("wqkvT_l", [D, 3 * D]),
        ("woT_h", [D, D]), ("woT_l", [D, D]),
        ("projT_h", [D, FD]), ("projT_l", [D, FD]),
        ("W1_h", [D, INNER]), ("W1_l", [D, INNER]),
        ("W2_h", [INNER, D]), ("W2_l", [INNER, D]),
        ("wrT_h", [D, 1]), ("wrT_l", [D, 1]),
        ("wgq", [D, INNER]), ("wuq", [D, INNER]), ("wdq", [INNER, D]),
    ]:
        p[nm] = dt("p_" + nm, shp, BF16).ap()
    sc = {}
    for nm, shp, dtp in [
        ("xnq", [NT, DC, 128, 128], BF16),
        ("q0_h", [D, S], BF16), ("q0_l", [D, S], BF16),
        ("v0_h", [S, D], BF16), ("v0_l", [S, D], BF16),
        ("mid", [S, INNER], F32),
        ("g0", [S, D], F32),
        ("x1", [S, D], F32),
        ("xn1", [S, D], F32),
        ("srow", [NT, 128], F32),
        ("q1_h", [D, S], BF16), ("q1_l", [D, S], BF16),
        ("k1_h", [D, S], BF16), ("k1_l", [D, S], BF16),
        ("g1", [S, D], F32),
        ("absr", [128, NT], F32),
    ]:
        sc[nm] = dt("s_" + nm, shp, dtp).ap()

    for nm in (dbg or ()):
        shp, dtp = {
            "stats": ([128, 64], F32), "xnfm_hi": ([128, RW], BF16),
            "xnfm_lo": ([128, RW], BF16), "xnq": ([NT * DC * 128, 128], BF16),
            "kfm_hi": ([128, RW], BF16), "kfm_lo": ([128, RW], BF16),
            "q0_h": ([D, S], BF16), "q0_l": ([D, S], BF16),
            "v0_l": ([S, D], BF16),
            "v0_h": ([S, D], BF16), "mid": ([S, INNER], F32),
            "g0": ([S, D], F32), "x1": ([S, D], F32), "xn1": ([S, D], F32),
            "srow": ([NT, 128], F32), "wtok": ([128, NT], F32),
            "g1": ([S, D], F32), "attnfm_hi": ([128, RW], BF16),
            "wgq": ([D, INNER], BF16), "wdq": ([INNER, D], BF16),
            "wqT_h": ([D, D], BF16), "wqT_l": ([D, D], BF16),
            "absr": ([128, NT], F32),
            "wstat": ([128, 16], F32),
        }[nm]
        ctx.dbg_out[nm] = dt("dbg_" + nm, shp, dtp, kind="ExternalOutput").ap()

    import contextlib
    es = contextlib.ExitStack()
    with es:
        E = es.enter_context
        arb = E(nc.sbuf_tensor("arb", [128, 5 * RW], BF16))
        arf = E(nc.sbuf_tensor("arf", [128, 9216], F32))
        masks = E(nc.sbuf_tensor("sb_masks", [128, 4 * 512], BF16))
        stg = E(nc.sbuf_tensor("sb_stg", [128, 1024], BF16))
        qi32 = E(nc.sbuf_tensor("sb_qi32", [128, 1024], I32))
        identf = E(nc.sbuf_tensor("sb_identf", [128, 128], F32))
        identb = E(nc.sbuf_tensor("sb_identb", [128, 128], BF16))
        onesf = E(nc.sbuf_tensor("sb_onesf", [1, 128], F32))
        onescol = E(nc.sbuf_tensor("sb_onescol", [128, 1], F32))
        tiny = E(nc.sbuf_tensor("tiny", [128, 144], F32))
        bconst = tiny[:, 100:101]
        nc.const_aps.register(F32, 1e-5, bconst)
        ps = E(nc.psum_tensor("ps", [128, 3072], F32))
        psb = E(nc.psum_tensor("psb", [128, 2048], BF16))
        for s_name in ("sD", "sW", "sSp", "sT", "sV", "sC", "sG",
                       "sWa", "sWb", "sGa", "sGb", "sPa", "sPb",
                       "fvector", "fscalar",
                       *("b" + e for e in Ctx.ENGS)):
            ctx.sems[s_name] = E(nc.semaphore(s_name))

        st = dict(stop=stop, x_in=x_in, out=out, win=win, p=p, sc=sc, arb=arb, arf=arf,
                  masks=masks, identf=identf, identb=identb, onesf=onesf,
                  onescol=onescol, brow=arf[0:1, 5120:6144], tiny=tiny,
                  bconst=bconst, stg=stg, qi32=qi32, psb=psb,
                  trow=arf[0:1, 4096:8256], ps=ps,
                  cin=cin)
        bld_all(ctx, st)
        ctx.prior_waited = {k: set(v) for k, v in ctx.waited.items()}
        ctx.prog = {e: [] for e in Ctx.ENGS}
        ctx.cnt = {}
        ctx.waited = {}
        bld_all(ctx, st)

        blk = E(nc.Block())

        def mk(engname):
            def run(e):
                for fn in ctx.prog[engname]:
                    fn(e)
            return run
        blk.sync(mk("sync"))
        blk.gpsimd(mk("gpsimd"))
        blk.vector(mk("vector"))
        blk.scalar(mk("scalar"))
        blk.tensor(mk("tensor"))
    return nc


def bld_all(ctx, st):
    cin = st["cin"]
    ctx.dma("sync", st["masks"][:], cin["masks"])
    ctx.dma("sync", st["identf"][:], cin["identf"])
    ctx.dma("sync", st["identb"][:], cin["identb"])
    ctx.dma("sync", st["onesf"][:], cin["onesf"])
    ctx.dma("sync", st["onescol"][:], cin["onescol"])
    ctx.dma("sync", st["bconst"], cin["bconst"])
    ctx.wait("sync", "sD")
    for e in ("vector", "scalar", "tensor"):
        ctx.wait(e, "sD")

    stop = st.get("stop")
    def done():
        ctx.wait("sync", "sSp")
        ctx.wait("sync", "sD")
        ctx.wait("sync", "sG")
    phase_W(ctx, st)
    ctx.barrier()
    if stop == "W":
        return done()
    nseq = 1 if stop else NSEQ
    for s in range(nseq):
        for nm, fn in [("A", phase_A), ("C", phase_C), ("B", phase_B),
                       ("D", phase_D), ("E", phase_E), ("F", phase_F),
                       ("G", phase_G), ("H", phase_H)]:
            fn(ctx, st, s)
            ctx.barrier()
            if stop == nm or (nm == "C" and stop in ("Cq", "Ck")):
                return done()
    done()



def phase_W(ctx, st):
    arb, arf, ps, tiny = st["arb"], st["arf"], st["ps"], st["tiny"]
    onesf, onescol = st["onesf"], st["onescol"]
    win, p = st["win"], st["p"]

    fb = [arf[:, 0:4096], arf[:, 4096:8192]]
    hbs = [arb[:, R3:R3 + 4096], arb[:, R3 + 4096:R3 + 8192]]
    lbs = [arb[:, R3 + 8192:R3 + 12288], arb[:, R3 + 12288:R3 + 16384]]
    c1s = [arb[:, R2:R2 + 4096], arb[:, R2 + 4096:R2 + 8192]]
    qbs = [arb[:, R2 + 8192:R2 + 12288], arb[:, R2 + 12288:R2 + 16384]]
    sW = ["sWa", "sWb"]
    sG = ["sGa", "sGb"]
    sP = ["sPa", "sPb"]
    gc = [0]

    def hl_item(nm, w):
        src_t = win[nm]
        for c in range(src_t.shape[0] // 128):
            pp = gc[0] % 2
            gc[0] += 1
            f, h, l = fb[pp][:, 0:w], hbs[pp][:, 0:w], lbs[pp][:, 0:w]
            ctx.wait("sync", sP[pp])
            ctx.dma("sync", f, src_t[128 * c:128 * (c + 1), :], inc=sW[pp])
            ctx.wait("gpsimd", "sV")
            ctx.wait("gpsimd", sP[pp])
            ctx.dma("gpsimd", h, src_t[128 * c:128 * (c + 1), :], inc=sG[pp])
            ctx.wait("vector", sW[pp])
            ctx.wait("vector", sG[pp])
            ctx.wait("vector", sP[pp])
            ctx.op("vector", "tensor_tensor", out=l, in0=f, in1=h,
                   op=A.subtract, inc="sV")
            ctx.wait("sync", "sV")
            ctx.dma("sync", p[nm + "_h"][128 * c:128 * (c + 1), :], h, inc=sP[pp])
            ctx.dma("sync", p[nm + "_l"][128 * c:128 * (c + 1), :], l, inc=sP[pp])

    for nm, w in [("wqT", D), ("wkT", D), ("wvT", D), ("wqkvT", 3 * D),
                  ("woT", D), ("projT", FD), ("W1", INNER), ("W2", D),
                  ("wrT", 1)]:
        hl_item(nm, w)
    ctx.tap("wqT_h", p["wqT_h"])
    ctx.tap("wqT_l", p["wqT_l"])

    for nm, w, outnm in [("wgT", INNER, "wgq"), ("wuT", INNER, "wuq"),
                         ("wdT", D, "wdq")]:
        src_t = win[nm]
        nchunk = src_t.shape[0] // 128
        nelem = float(src_t.shape[0] * src_t.shape[1])
        acc = tiny[:, 64:65]
        for c in range(nchunk):
            pp = gc[0] % 2
            gc[0] += 1
            f = fb[pp][:, 0:w]
            ctx.wait("sync", "sC")
            ctx.dma("sync", f, src_t[128 * c:128 * (c + 1), :], inc=sW[pp])
            ctx.wait("scalar", sW[pp])
            ctx.op("scalar", "activation", out=f, in_=f, func=AF.Abs,
                   accum_out=tiny[:, 65:66], inc="sC")
            ctx.wait("vector", "sC")
            if c == 0:
                ctx.op("vector", "tensor_copy", out=acc, in_=tiny[:, 65:66],
                       inc="sV")
            else:
                ctx.op("vector", "tensor_tensor", out=acc, in0=acc,
                       in1=tiny[:, 65:66], op=A.add, inc="sV")
            ctx.fence("vector")
            ctx.wait("scalar", "sV")
        ctx.wait("tensor", "sV")
        ctx.op("tensor", "matmul", out=ps[0:1, 2048:2049], lhsT=onescol[:],
               rhs=acc, start=True, stop=True, inc="sT")
        ctx.wait("vector", "sT")
        ctx.op("vector", "tensor_scalar", out=tiny[0:1, 66:67],
               in0=ps[0:1, 2048:2049], scalar1=1.0 / nelem, scalar2=1e-5,
               op0=A.mult, op1=A.max)
        ctx.fence("vector")
        ctx.op("vector", "tensor_scalar", out=tiny[0:1, 67:68],
               in0=tiny[0:1, 66:67], scalar1=0.5, scalar2=None, op0=A.mult,
               inc="sV")
        ctx.wait("tensor", "sV")
        ctx.op("tensor", "matmul", out=ps[:, 2052:2053], lhsT=onesf[:],
               rhs=tiny[0:1, 66:67], start=True, stop=True)
        ctx.op("tensor", "matmul", out=ps[:, 2053:2054], lhsT=onesf[:],
               rhs=tiny[0:1, 67:68], start=True, stop=True, inc="sT")
        ctx.wait("vector", "sT")
        mw_slot = {"wgq": 68, "wuq": 69, "wdq": 70}[outnm]
        ctx.op("vector", "tensor_copy", out=tiny[:, mw_slot:mw_slot + 1],
               in_=ps[:, 2052:2053])
        thr = tiny[:, 71:72]
        nthr = tiny[:, 72:73]
        ctx.op("vector", "tensor_copy", out=thr, in_=ps[:, 2053:2054])
        ctx.fence("vector")
        ctx.op("vector", "tensor_scalar", out=nthr, in0=thr, scalar1=-1.0,
               scalar2=None, op0=A.mult, inc="sV")
        ctx.fence("vector")
        for c in range(nchunk):
            pp = gc[0] % 2
            gc[0] += 1
            f = fb[pp][:, 0:w]
            c1 = c1s[pp][:, 0:w]
            qb = qbs[pp][:, 0:w]
            ctx.wait("sync", sP[pp])
            ctx.dma("sync", f, src_t[128 * c:128 * (c + 1), :], inc=sW[pp])
            ctx.wait("vector", sW[pp])
            ctx.wait("vector", sP[pp])
            ctx.op("vector", "tensor_scalar", out=c1, in0=f, scalar1=thr,
                   scalar2=None, op0=A.is_ge)
            ctx.op("vector", "scalar_tensor_tensor", out=qb, in0=f, scalar=nthr,
                   in1=c1, op0=A.is_le, op1=A.subtract, inc="sV")
            ctx.wait("sync", "sV")
            ctx.dma("sync", p[outnm][128 * c:128 * (c + 1), :], qb, inc=sP[pp])
    ctx.tap("wstat", tiny[:, 64:80])
    ctx.tap("wgq", p["wgq"])
    ctx.tap("wdq", p["wdq"])



def _ln_fm(ctx, st, src_dram, base, quant, xn_tok_dram=None, sseq=0):
    arb, arf, tiny, ps, identf = (st["arb"], st["arf"], st["tiny"], st["ps"],
                                  st["identf"])
    xb = [arf[:, 0:1024], arf[:, 1024:2048]]
    trash = arf[:, 2048:3072]
    m_all = tiny[:, 0:16]
    ssq_all = tiny[:, 16:32]
    rstd = tiny[:, 32:48]

    for t in range(NT):
        b = xb[t % 2]
        ctx.wait("sync", "sC")
        ctx.dma("sync", b, src_dram[base + 128 * t:base + 128 * (t + 1), :])
        ctx.wait("scalar", "sD")
        ctx.op("scalar", "activation", out=trash, in_=b, func=AF.Copy,
               accum_out=m_all[:, t:t + 1])
        ctx.op("scalar", "activation", out=trash, in_=b, func=AF.Square,
               accum_out=ssq_all[:, t:t + 1], inc="sC")
    ctx.wait("vector", "sC")
    ctx.op("vector", "tensor_scalar", out=m_all, in0=m_all, scalar1=1.0 / D,
           scalar2=None, op0=A.mult)
    ctx.op("vector", "tensor_scalar", out=ssq_all, in0=ssq_all, scalar1=1.0 / D,
           scalar2=None, op0=A.mult)
    ctx.fence("vector")
    v2 = tiny[:, 48:64]
    ctx.op("vector", "tensor_tensor", out=v2, in0=m_all, in1=m_all, op=A.mult)
    ctx.fence("vector")
    ctx.op("vector", "tensor_tensor", out=v2, in0=ssq_all, in1=v2,
           op=A.subtract, inc="sV")
    ctx.wait("scalar", "sV")
    ctx.op("scalar", "activation", out=v2, in_=v2, func=AF.Sqrt, bias=1e-5,
           scale=1.0, inc="sC")
    ctx.wait("vector", "sC")
    ctx.op("vector", "reciprocal", out=rstd, in_=v2, inc="sV")
    ctx.fence("vector")

    xnt = arf[:, 2048:3072]
    for t in range(NT):
        b = xb[t % 2]
        ctx.wait("sync", "sV")
        ctx.dma("sync", b, src_dram[base + 128 * t:base + 128 * (t + 1), :])
        ctx.wait("vector", "sD")
        ctx.wait("vector", "sSp")
        ctx.op("vector", "tensor_scalar", out=xnt, in0=b,
               scalar1=m_all[:, t:t + 1], scalar2=rstd[:, t:t + 1],
               op0=A.subtract, op1=A.mult, inc="sV")
        if xn_tok_dram is not None:
            ctx.wait("sync", "sV")
            ctx.dma("sync", xn_tok_dram[128 * t:128 * (t + 1), :], xnt,
                    inc="sSp")
        ctx.wait("tensor", "sV")
        for c in range(DC):
            bank = (c // 4) % 2
            off = 512 * bank + 128 * (c % 4)
            ctx.op("tensor", "transpose", out=ps[:, off:off + 128],
                   in_=xnt[:, 128 * c:128 * (c + 1)], identity=identf[:],
                   inc="sT" if c == DC - 1 else None)
        ctx.wait("vector", "sT")
        for c in range(DC):
            bank = (c // 4) % 2
            off = 512 * bank + 128 * (c % 4)
            a0 = R0 + 2048 * c + 128 * t
            ctx.op("vector", "tensor_copy", out=arb[:, a0:a0 + 128],
                   in_=ps[:, off:off + 128])
        for c in range(DC):
            bank = (c // 4) % 2
            off = 512 * bank + 128 * (c % 4)
            a0 = R0 + 2048 * c + 128 * t
            a1 = R1 + 2048 * c + 128 * t
            ctx.op("vector", "tensor_tensor", out=arb[:, a1:a1 + 128],
                   in0=ps[:, off:off + 128], in1=arb[:, a0:a0 + 128],
                   op=A.subtract)
        ctx.op("vector", "nop", inc="sV")
        if quant:
            _quant_tile(ctx, st, xnt, t)
    ctx.op("vector", "nop", inc="sV")
    ctx.op("scalar", "nop", inc="sC")


def _quant_tile(ctx, st, xnt, t):
    arb, arf, tiny, ps, identb, sc = (st["arb"], st["arf"], st["tiny"],
                                      st["ps"], st["identb"], st["sc"])
    am = tiny[:, 80:81]
    sq = tiny[:, 81:82]
    inv = tiny[:, 82:83]
    ctx.op("vector", "reduce_max", out=am, in_=xnt, axis=AX,
           apply_absolute_value=True)
    ctx.fence("vector")
    ctx.op("vector", "tensor_scalar", out=am, in0=am, scalar1=1e-5,
           scalar2=None, op0=A.max)
    ctx.fence("vector")
    ctx.op("vector", "reciprocal", out=sq, in_=am)
    ctx.fence("vector")
    ctx.op("vector", "tensor_scalar", out=sq, in0=sq, scalar1=127.0,
           scalar2=None, op0=A.mult)
    ctx.fence("vector")
    ctx.op("vector", "reciprocal", out=inv, in_=sq)
    ctx.fence("vector")
    ctx.op("vector", "tensor_copy", out=tiny[:, 84 + t:85 + t], in_=inv)
    v = arf[:, 3072:4096]
    qi = st["qi32"][:]
    qm1 = arf[:, 5120:6144]
    dlt = arf[:, 6144:7168]
    ctx.op("vector", "tensor_scalar", out=v, in0=xnt, scalar1=sq, scalar2=None,
           op0=A.mult)
    ctx.op("vector", "tensor_copy", out=qi, in_=v, inc="sV")
    ctx.wait("scalar", "sV")
    ctx.op("scalar", "activation", out=qm1, in_=qi, func=AF.Copy, bias=-1.0,
           scale=1.0, inc="sC")
    ctx.wait("vector", "sC")
    ctx.op("vector", "tensor_tensor", out=dlt, in0=v, in1=qm1, op=A.subtract)
    ctx.op("vector", "scalar_tensor_tensor", out=v, in0=dlt, scalar=1.5,
           in1=qm1, op0=A.is_ge, op1=A.add)
    xq = arb[:, R4 + 2048:R4 + 3072]
    ctx.op("vector", "scalar_tensor_tensor", out=xq, in0=dlt, scalar=0.5,
           in1=v, op0=A.is_ge, op1=A.add, inc="sV")
    psb = st["psb"]
    ctx.wait("tensor", "sV")
    for c in range(DC):
        off = 1024 * ((c // 4) % 2) + 128 * (c % 4)
        ctx.op("tensor", "transpose", out=psb[:, off:off + 128],
               in_=xq[:, 128 * c:128 * (c + 1)], identity=identb[:],
               inc="sT" if c == DC - 1 else None)
    ctx.wait("vector", "sT")
    ctx.wait("vector", "sSp")
    stg = arb[:, R4 + 3072:R4 + 4096]
    for c in range(DC):
        off = 1024 * ((c // 4) % 2) + 128 * (c % 4)
        ctx.op("vector", "tensor_copy", out=stg[:, 128 * c:128 * (c + 1)],
               in_=psb[:, off:off + 128],
               inc="sV" if c == DC - 1 else None)
    ctx.wait("sync", "sV")
    ctx.dma("sync", sc["xnq"][t].rearrange("c p n -> p c n"),
            stg.rearrange("p (c n) -> p c n", c=DC), inc="sSp")


def phase_A(ctx, st, s):
    _ln_fm(ctx, st, st["x_in"], s * S, quant=True)
    ctx.wait("sync", "sV")
    ctx.dma("sync", st["sc"]["absr"], st["tiny"][:, 84:84 + NT], inc="sSp")
    ctx.tap("stats", st["tiny"][:, 0:64])
    ctx.tap("xnfm_hi", st["arb"][:, R0:R0 + RW])
    ctx.tap("xnfm_lo", st["arb"][:, R1:R1 + RW])
    ctx.tap("xnq", st["sc"]["xnq"].rearrange("t c p n -> (t c p) n"))
    ctx.tap("absr", st["sc"]["absr"])



def phase_C(ctx, st, s):
    arb, arf, ps, sc, p = st["arb"], st["arf"], st["ps"], st["sc"], st["p"]
    wbuf_h = arb[:, R4:R4 + 8192]
    wbuf_l = arb[:, R4 + 8192:R4 + 16384]
    ev_h = st["stg"][:, 0:512]
    ev_l = st["stg"][:, 512:1024]

    stop = st.get("stop") or ""
    pipe = Pipe(ctx, "sV", 4)
    mats = [("wq", "q"), ("wk", "k"), ("wv", "v")]
    if stop == "Cq":
        mats = mats[:1]
    elif stop == "Ck":
        mats = mats[:2]
    for nm, kind in mats:
        ctx.wait("sync", "sT")
        ctx.dma("sync", wbuf_h[:].rearrange("p (c n) -> p c n", c=DC),
                p[nm + "T_h"].rearrange("(c p) n -> p c n", p=128))
        ctx.dma("sync", wbuf_l[:].rearrange("p (c n) -> p c n", c=DC),
                p[nm + "T_l"].rearrange("(c p) n -> p c n", p=128))
        ctx.wait("tensor", "sD")

        if kind in ("q", "k"):
            for m in range(DC):
                for n in range(4):
                    bank = (m * 4 + n) % 4
                    ob = ps[:, 512 * bank:512 * (bank + 1)]
                    pipe.pre()
                    for c in range(DC):
                        _mm3_acc(ctx,
                                 wbuf_h[:, 1024 * c + 128 * m:1024 * c + 128 * m + 128],
                                 wbuf_l[:, 1024 * c + 128 * m:1024 * c + 128 * m + 128],
                                 arb[:, R0 + 2048 * c + 512 * n:R0 + 2048 * c + 512 * (n + 1)],
                                 arb[:, R1 + 2048 * c + 512 * n:R1 + 2048 * c + 512 * (n + 1)],
                                 ob, first=(c == 0), last=(c == DC - 1))
                    ctx.op("tensor", "nop", inc="sT")
                    ctx.wait("vector", "sT")
                    if kind == "k":
                        a2 = R2 + 2048 * m + 512 * n
                        a3 = R3 + 2048 * m + 512 * n
                        _hl(ctx, arb[:, a2:a2 + 512], arb[:, a3:a3 + 512], ob)
                        ctx.op("vector", "nop", inc="sV")
                    else:
                        ctx.wait("vector", "sSp")
                        _hl(ctx, ev_h[:, 0:512], ev_l[:, 0:512], ob)
                        ctx.op("vector", "nop", inc="sV")
                        ctx.wait("sync", "sV")
                        ctx.dma("sync", sc["q0_h"][128 * m:128 * (m + 1), 512 * n:512 * (n + 1)], ev_h[:, 0:512], inc="sSp")
                        ctx.dma("sync", sc["q0_l"][128 * m:128 * (m + 1), 512 * n:512 * (n + 1)], ev_l[:, 0:512], inc="sSp")
                    pipe.post()
        else:
            for t in range(NT):
                for b in range(2):
                    bank = (t * 2 + b) % 4
                    ob = ps[:, 512 * bank:512 * (bank + 1)]
                    pipe.pre()
                    for c in range(DC):
                        xh = arb[:, R0 + 2048 * c + 128 * t:R0 + 2048 * c + 128 * t + 128]
                        xl = arb[:, R1 + 2048 * c + 128 * t:R1 + 2048 * c + 128 * t + 128]
                        wh = wbuf_h[:, 1024 * c + 512 * b:1024 * c + 512 * (b + 1)]
                        wl = wbuf_l[:, 1024 * c + 512 * b:1024 * c + 512 * (b + 1)]
                        _mm3_acc(ctx, xh, xl, wh, wl, ob,
                                 first=(c == 0), last=(c == DC - 1))
                    ctx.op("tensor", "nop", inc="sT")
                    ctx.wait("vector", "sT")
                    ctx.wait("vector", "sSp")
                    _hl(ctx, ev_h[:, 0:512], ev_l[:, 0:512], ob)
                    ctx.op("vector", "nop", inc="sV")
                    ctx.wait("sync", "sV")
                    ctx.dma("sync", sc["v0_h"][128 * t:128 * (t + 1), 512 * b:512 * (b + 1)], ev_h[:, 0:512], inc="sSp")
                    ctx.dma("sync", sc["v0_l"][128 * t:128 * (t + 1), 512 * b:512 * (b + 1)], ev_l[:, 0:512], inc="sSp")
                    pipe.post()
    ctx.tap("kfm_hi", arb[:, R2:R2 + RW])
    ctx.tap("kfm_lo", arb[:, R3:R3 + RW])
    ctx.tap("q0_h", sc["q0_h"])
    ctx.tap("q0_l", sc["q0_l"])
    ctx.tap("v0_h", sc["v0_h"])
    ctx.tap("v0_l", sc["v0_l"])



def phase_B(ctx, st, s):
    arb, arf, tiny, ps, sc, p, identb = (st["arb"], st["arf"], st["tiny"],
                                         st["ps"], st["sc"], st["p"],
                                         st["identb"])
    ctx.dma("sync", arb[:, R0:R0 + RW].rearrange("p (t c n) -> p t c n",
                                                 t=NT, c=DC),
            sc["xnq"].rearrange("t c p n -> p t c n"))
    ctx.dma("sync", tiny[:, 84:84 + NT], sc["absr"])
    ctx.wait("vector", "sD")
    for t in range(NT):
        for j, slot in ((0, 68), (1, 69)):
            ctx.op("vector", "scalar_tensor_tensor",
                   out=tiny[:, 16 * j + t:16 * j + t + 1],
                   in0=tiny[:, slot:slot + 1], scalar=-1.0,
                   in1=tiny[:, 84 + t:85 + t], op0=A.mult, op1=A.mult)
    ctx.op("vector", "nop", inc="sV")
    ctx.wait("tensor", "sD")
    ctx.wait("scalar", "sV")

    wg = arb[:, R1:R1 + 4096]
    wu = arb[:, R1 + 4096:R1 + 8192]
    gsil = arf[:, 0:512]
    mid = arf[:, 512:1024]
    pipeB = Pipe(ctx, "sV", 3)

    for g in range(8):
        ctx.wait("sync", "sT")
        ctx.dma("sync", wg[:].rearrange("p (c n) -> p c n", c=DC),
                p["wgq"].rearrange("(c p) n -> p c n", p=128)[:, :, 512 * g:512 * (g + 1)])
        ctx.dma("sync", wu[:].rearrange("p (c n) -> p c n", c=DC),
                p["wuq"].rearrange("(c p) n -> p c n", p=128)[:, :, 512 * g:512 * (g + 1)])
        ctx.wait("tensor", "sD")
        for t in range(NT):
            bg = (2 * t) % 6
            bu = (2 * t + 1) % 6
            obg = ps[:, 512 * bg:512 * (bg + 1)]
            obu = ps[:, 512 * bu:512 * (bu + 1)]
            pipeB.pre()
            for c in range(DC):
                xq = arb[:, R0 + 1024 * t + 128 * c:R0 + 1024 * t + 128 * c + 128]
                ctx.op("tensor", "matmul", out=obg, lhsT=xq,
                       rhs=wg[:, 512 * c:512 * (c + 1)],
                       start=(c == 0), stop=(c == DC - 1))
                ctx.op("tensor", "matmul", out=obu, lhsT=xq,
                       rhs=wu[:, 512 * c:512 * (c + 1)],
                       start=(c == 0), stop=(c == DC - 1))
            ctx.op("tensor", "nop", inc="sT")
            ctx.wait("scalar", "sT")
            ctx.wait("scalar", "sV")
            ctx.op("scalar", "activation", out=gsil, in_=obg, func=AF.Silu,
                   bias=0.0, scale=tiny[:, t:t + 1], inc="sC")
            ctx.wait("vector", "sC")
            ctx.wait("vector", "sSp")
            ctx.op("vector", "scalar_tensor_tensor", out=mid, in0=obu,
                   scalar=tiny[:, 16 + t:17 + t], in1=gsil,
                   op0=A.mult, op1=A.mult, inc="sV")
            ctx.wait("sync", "sV")
            ctx.dma("sync", sc["mid"][128 * t:128 * (t + 1), 512 * g:512 * (g + 1)],
                    mid, inc="sSp")
            pipeB.post()
    ctx.tap("mid", sc["mid"])
    ctx.barrier()

    wd_a = arb[:, R1:R1 + RW]
    wd_b = arb[:, R4:R4 + RW]
    ctx.dma("sync", wd_a[:].rearrange("p (c n) -> p c n", c=16),
            p["wdq"].rearrange("(c p) n -> p c n", p=128)[:, 0:16, :])
    ctx.dma("sync", wd_b[:].rearrange("p (c n) -> p c n", c=16),
            p["wdq"].rearrange("(c p) n -> p c n", p=128)[:, 16:32, :])
    ctx.wait("tensor", "sD")

    xq_stg = arb[:, R0:R0 + 4096]
    mfm = arb[:, R0 + 4096:R0 + 8192]
    midb = arf[:, 0:4096]
    vv = arf[:, 4096:8192]
    for t in range(NT):
        ctx.wait("sync", "sV")
        ctx.dma("sync", midb, sc["mid"][128 * t:128 * (t + 1), :])
        ctx.wait("vector", "sD")
        am = tiny[:, 80:81]
        sq = tiny[:, 81:82]
        inv = tiny[:, 82:83]
        combd = tiny[:, 83:84]
        ctx.op("vector", "reduce_max", out=am, in_=midb, axis=AX,
               apply_absolute_value=True)
        ctx.fence("vector")
        ctx.op("vector", "tensor_scalar", out=am, in0=am, scalar1=1e-5,
               scalar2=None, op0=A.max)
        ctx.fence("vector")
        ctx.op("vector", "reciprocal", out=sq, in_=am)
        ctx.fence("vector")
        ctx.op("vector", "tensor_scalar", out=sq, in0=sq, scalar1=127.0,
               scalar2=None, op0=A.mult)
        ctx.fence("vector")
        ctx.op("vector", "reciprocal", out=inv, in_=sq)
        ctx.fence("vector")
        ctx.op("vector", "scalar_tensor_tensor", out=combd, in0=tiny[:, 70:71],
               scalar=-1.0, in1=inv, op0=A.mult, op1=A.mult)
        ctx.op("vector", "tensor_scalar", out=vv, in0=midb, scalar1=sq,
               scalar2=None, op0=A.mult)
        ctx.op("vector", "nop", inc="sV")
        ctx.wait("scalar", "sSp")
        qi = st["qi32"][:]
        psb = st["psb"]
        for q in range(4):
            vs = vv[:, 1024 * q:1024 * (q + 1)]
            qm1 = midb[:, 1024 * q:1024 * (q + 1)]
            dlt = arf[:, 8192:9216]
            ctx.op("vector", "tensor_copy", out=qi, in_=vs, inc="sV")
            ctx.wait("scalar", "sV")
            ctx.op("scalar", "activation", out=qm1, in_=qi, func=AF.Copy,
                   bias=-1.0, scale=1.0, inc="sC")
            ctx.wait("vector", "sC")
            ctx.op("vector", "tensor_tensor", out=dlt, in0=vs, in1=qm1,
                   op=A.subtract)
            ctx.op("vector", "scalar_tensor_tensor", out=vs, in0=dlt,
                   scalar=1.5, in1=qm1, op0=A.is_ge, op1=A.add)
            ctx.op("vector", "scalar_tensor_tensor",
                   out=xq_stg[:, 1024 * q:1024 * (q + 1)],
                   in0=dlt, scalar=0.5, in1=vs, op0=A.is_ge, op1=A.add,
                   inc="sV")
            ctx.wait("tensor", "sV")
            for c in range(8 * q, 8 * q + 8):
                off = 1024 * ((c % 8) // 4) + 128 * (c % 4)
                ctx.op("tensor", "transpose",
                       out=psb[:, off:off + 128],
                       in_=xq_stg[:, 128 * c:128 * (c + 1)], identity=identb[:],
                       inc="sT" if (c % 8 == 7) else None)
            ctx.wait("vector", "sT")
            for cc in range(8 * q, 8 * q + 8):
                off2 = 1024 * ((cc % 8) // 4) + 128 * (cc % 4)
                ctx.op("vector", "tensor_copy",
                       out=mfm[:, 128 * cc:128 * (cc + 1)],
                       in_=psb[:, off2:off2 + 128],
                       inc="sV" if cc == 8 * q + 7 else None)
            ctx.wait("tensor", "sV")
        ob0 = ps[:, 2048:2560]
        ob1 = ps[:, 2560:3072]
        for c in range(IC):
            lt = mfm[:, 128 * c:128 * (c + 1)]
            wslab = wd_a if c < 16 else wd_b
            cc = c % 16
            ctx.op("tensor", "matmul", out=ob0, lhsT=lt,
                   rhs=wslab[:, 1024 * cc:1024 * cc + 512],
                   start=(c == 0), stop=(c == IC - 1))
            ctx.op("tensor", "matmul", out=ob1, lhsT=lt,
                   rhs=wslab[:, 1024 * cc + 512:1024 * cc + 1024],
                   start=(c == 0), stop=(c == IC - 1))
        ctx.op("tensor", "nop", inc="sT")
        ctx.wait("scalar", "sT")
        ctx.wait("scalar", "sSp")
        g0t = arf[:, 8192:9216]
        ctx.op("scalar", "activation", out=g0t[:, 0:512], in_=ob0,
               func=AF.Silu, bias=0.0, scale=combd)
        ctx.op("scalar", "activation", out=g0t[:, 512:1024], in_=ob1,
               func=AF.Silu, bias=0.0, scale=combd, inc="sC")
        ctx.wait("sync", "sC")
        ctx.dma("sync", sc["g0"][128 * t:128 * (t + 1), :], g0t, inc="sSp")
        ctx.op("vector", "nop", inc="sV")
    ctx.tap("g0", sc["g0"])



def phase_D(ctx, st, s):
    arb, arf, tiny, ps, sc, masks, identb = (st["arb"], st["arf"], st["tiny"],
                                             st["ps"], st["sc"], st["masks"],
                                             st["identb"])
    base = s * S
    ctx.dma("sync", arb[:, R0:R0 + RW].rearrange("p (t n) -> p t n", t=NT),
            sc["v0_h"].rearrange("(t p) n -> p t n", p=128))
    ctx.dma("sync", arb[:, R1:R1 + RW].rearrange("p (t n) -> p t n", t=NT),
            sc["v0_l"].rearrange("(t p) n -> p t n", p=128))
    vload = ctx.val("sD")
    qb_h = [arb[:, R4:R4 + 1024], arb[:, R4 + 1024:R4 + 2048]]
    qb_l = [arb[:, R4 + 2048:R4 + 3072], arb[:, R4 + 3072:R4 + 4096]]
    pt_h = arb[:, R4 + 4096:R4 + 6144]
    pt_l = arb[:, R4 + 6144:R4 + 8192]
    ph = arb[:, R4 + 8192:R4 + 10240]
    pl = arb[:, R4 + 10240:R4 + 12288]
    srow = arf[:, 0:2048]
    prow = arf[:, 2048:4096]
    xg = [arf[:, 4096:5120], arf[:, 5120:6144]]
    x1t = arf[:, 6144:7168]

    for t in range(NT):
        qb, ql = qb_h[t % 2], qb_l[t % 2]
        ctx.wait("sync", "sT")
        ctx.dma("sync", qb[:].rearrange("p (c n) -> p c n", c=DC),
                sc["q0_h"].rearrange("(c p) n -> p c n", p=128)[:, :, 128 * t:128 * (t + 1)])
        ctx.dma("sync", ql[:].rearrange("p (c n) -> p c n", c=DC),
                sc["q0_l"].rearrange("(c p) n -> p c n", p=128)[:, :, 128 * t:128 * (t + 1)])
        ctx.wait("sync", "sV")
        ctx.dma("sync", xg[0], st["x_in"][base + 128 * t:base + 128 * (t + 1), :])
        ctx.dma("sync", xg[1], sc["g0"][128 * t:128 * (t + 1), :])
        qv = ctx.val("sD")
        ngrp = (t + 4) // 4
        ctx.wait("tensor", "sD")
        for gi in range(ngrp):
            bank = gi % 2
            ob = ps[:, 512 * bank:512 * (bank + 1)]
            ctx.wait("tensor", "sV")
            for c in range(DC):
                _mm3_acc(ctx,
                         qb[:, 128 * c:128 * (c + 1)], ql[:, 128 * c:128 * (c + 1)],
                         arb[:, R2 + 2048 * c + 512 * gi:R2 + 2048 * c + 512 * (gi + 1)],
                         arb[:, R3 + 2048 * c + 512 * gi:R3 + 2048 * c + 512 * (gi + 1)],
                         ob, first=(c == 0), last=(c == DC - 1))
            ctx.op("tensor", "nop", inc="sT")
            ctx.wait("vector", "sT")
            if gi == ngrp - 1:
                mi = t % 4
                ctx.op("vector", "scalar_tensor_tensor",
                       out=srow[:, 512 * gi:512 * (gi + 1)], in0=ob,
                       scalar=SCALE0, in1=masks[:, 512 * mi:512 * (mi + 1)],
                       op0=A.mult, op1=A.add, inc="sV")
            else:
                ctx.op("vector", "tensor_scalar",
                       out=srow[:, 512 * gi:512 * (gi + 1)], in0=ob,
                       scalar1=SCALE0, scalar2=None, op0=A.mult, inc="sV")
        keylen = 512 * ngrp
        nk = t + 1
        ctx.op("vector", "reduce_max", out=tiny[:, 80:81],
               in_=srow[:, 0:keylen], axis=AX)
        ctx.fence("vector")
        ctx.op("vector", "tensor_scalar", out=tiny[:, 81:82],
               in0=tiny[:, 80:81], scalar1=-1.0, scalar2=None, op0=A.mult,
               inc="sV")
        ctx.wait("scalar", "sV")
        ctx.op("scalar", "activation", out=prow[:, 0:keylen],
               in_=srow[:, 0:keylen], func=AF.Exp, bias=tiny[:, 81:82],
               scale=1.0, accum_out=tiny[:, 82:83], inc="sC")
        ctx.wait("vector", "sC")
        ctx.op("vector", "reciprocal", out=tiny[:, 83:84], in_=tiny[:, 82:83])
        ctx.fence("vector")
        _hl(ctx, ph[:, 0:keylen], pl[:, 0:keylen], prow[:, 0:keylen])
        ctx.op("vector", "nop", inc="sV")
        ctx.wait("tensor", "sV")
        psb = st["psb"]
        for j in range(nk):
            o1 = 1024 * (j % 2)
            ctx.op("tensor", "transpose", out=psb[:, o1:o1 + 128],
                   in_=ph[:, 128 * j:128 * (j + 1)], identity=identb[:])
            ctx.op("tensor", "transpose",
                   out=psb[:, o1 + 128:o1 + 256],
                   in_=pl[:, 128 * j:128 * (j + 1)], identity=identb[:],
                   inc="sT")
            ctx.wait("vector", "sT")
            ctx.op("vector", "tensor_copy", out=pt_h[:, 128 * j:128 * (j + 1)],
                   in_=psb[:, o1:o1 + 128])
            ctx.op("vector", "tensor_copy", out=pt_l[:, 128 * j:128 * (j + 1)],
                   in_=psb[:, o1 + 128:o1 + 256], inc="sV")
            ctx.wait("tensor", "sV")
        ctx.wait("tensor", "sD")
        for b in range(2):
            ob = ps[:, 2048 + 512 * b:2048 + 512 * (b + 1)]
            for j in range(nk):
                _mm3_acc(ctx,
                         pt_h[:, 128 * j:128 * (j + 1)],
                         pt_l[:, 128 * j:128 * (j + 1)],
                         arb[:, R0 + 1024 * j + 512 * b:R0 + 1024 * j + 512 * (b + 1)],
                         arb[:, R1 + 1024 * j + 512 * b:R1 + 1024 * j + 512 * (b + 1)],
                         ob, first=(j == 0), last=(j == nk - 1))
        ctx.op("tensor", "nop", inc="sT")
        ctx.wait("vector", "sT")
        ctx.wait("vector", "sD")
        ctx.wait("vector", "sSp")
        for b in range(2):
            ob = ps[:, 2048 + 512 * b:2048 + 512 * (b + 1)]
            sl = slice(512 * b, 512 * (b + 1))
            ctx.op("vector", "tensor_scalar", out=x1t[:, sl], in0=ob,
                   scalar1=tiny[:, 83:84], scalar2=None, op0=A.mult)
            ctx.op("vector", "tensor_tensor", out=x1t[:, sl], in0=x1t[:, sl],
                   in1=xg[1][:, sl], op=A.mult)
            ctx.op("vector", "tensor_tensor", out=x1t[:, sl], in0=x1t[:, sl],
                   in1=xg[0][:, sl], op=A.add)
        ctx.op("vector", "nop", inc="sV")
        ctx.wait("sync", "sV")
        ctx.dma("sync", sc["x1"][128 * t:128 * (t + 1), :], x1t, inc="sSp")
    ctx.tap("x1", sc["x1"])



def phase_E(ctx, st, s):
    arb, arf, tiny, ps, sc, p, onesf, trow = (st["arb"], st["arf"], st["tiny"],
                                              st["ps"], st["sc"], st["p"],
                                              st["onesf"], st["trow"])
    _ln_fm(ctx, st, sc["x1"], 0, quant=False, xn_tok_dram=sc["xn1"])
    ctx.barrier()

    wrh = arb[:, R4:R4 + 8].rearrange("p (c n) -> p c n", c=DC)
    wrl = arb[:, R4 + 8:R4 + 16].rearrange("p (c n) -> p c n", c=DC)
    ctx.dma("sync", wrh, p["wrT_h"].rearrange("(c p) n -> p c n", p=128),
            allow_slow_non_contiguous=True)
    ctx.dma("sync", wrl, p["wrT_l"].rearrange("(c p) n -> p c n", p=128),
            allow_slow_non_contiguous=True)
    ctx.wait("tensor", "sD")
    for n in range(4):
        ob = ps[0:1, 512 * n:512 * (n + 1)]
        for c in range(DC):
            _mm3_acc(ctx, wrh[:, c, :], wrl[:, c, :],
                     arb[:, R0 + 2048 * c + 512 * n:R0 + 2048 * c + 512 * (n + 1)],
                     arb[:, R1 + 2048 * c + 512 * n:R1 + 2048 * c + 512 * (n + 1)],
                     ob, first=(c == 0), last=(c == DC - 1))
    ctx.op("tensor", "nop", inc="sT")
    ctx.wait("vector", "sT")
    srw = trow[0:1, 0:S]
    ctx.op("vector", "tensor_copy", out=srw, in_=ps[0:1, 0:2048], inc="sV")
    ctx.tap("srow", sc["srow"])

    st["_srw"] = srw




def _bisect_start(ctx, st):
    arf, trow, sc = st["arf"], st["trow"], st["sc"]
    srw = st["_srw"]
    lo = trow[0:1, S:S + 1]
    hi = trow[0:1, S + 1:S + 2]
    mid = trow[0:1, S + 2:S + 3]
    cnt = trow[0:1, S + 3:S + 4]
    pred = trow[0:1, S + 4:S + 5]
    tmp = trow[0:1, S + 5:S + 6]
    tmp2 = trow[0:1, S + 6:S + 7]
    cmpb = trow[0:1, S + 8:S + 8 + S]
    ctx.wait("sync", "sV")
    ctx.dma("sync", sc["srow"].rearrange("(o t) p -> o t p", o=1),
            srw.rearrange("o (t p) -> o t p", p=128), inc="sSp")
    ctx.wait("sync", "sSp")
    ctx.dma("sync", arf[0:16, 8960:9088], sc["srow"])
    ctx.op("vector", "reduce_max", out=hi, in_=srw, axis=AX,
           apply_absolute_value=True)
    ctx.fence("vector")
    ctx.op("vector", "tensor_scalar", out=hi, in0=hi, scalar1=1.0,
           scalar2=None, op0=A.add)
    ctx.fence("vector")
    ctx.op("vector", "tensor_scalar", out=lo, in0=hi, scalar1=-1.0,
           scalar2=None, op0=A.mult)
    ctx.fence("vector")
    ctx.op("vector", "tensor_tensor", out=mid, in0=lo, in1=hi, op=A.add)
    ctx.fence("vector")
    ctx.op("vector", "tensor_scalar", out=mid, in0=mid, scalar1=0.5,
           scalar2=None, op0=A.mult)
    ctx.fence("vector")
    K = S // 2
    for _ in range(44):
        ctx.op("vector", "tensor_scalar", out=cmpb, in0=srw, scalar1=mid,
               scalar2=0.0, op0=A.is_ge, op1=A.add, accum_out=cnt)
        ctx.op("vector", "tensor_tensor", out=tmp, in0=mid, in1=lo,
               op=A.subtract)
        ctx.op("vector", "tensor_tensor", out=tmp2, in0=hi, in1=mid,
               op=A.subtract)
        ctx.fence("vector")
        ctx.op("vector", "scalar_tensor_tensor", out=pred, in0=cnt,
               scalar=float(K), in1=tmp, op0=A.is_ge, op1=A.mult)
        ctx.op("vector", "scalar_tensor_tensor", out=tmp2, in0=cnt,
               scalar=float(K), in1=tmp2, op0=A.is_ge, op1=A.mult)
        ctx.fence("vector")
        ctx.op("vector", "tensor_tensor", out=lo, in0=lo, in1=pred, op=A.add)
        ctx.op("vector", "tensor_tensor", out=hi, in0=mid, in1=tmp2, op=A.add)
        ctx.fence("vector")
        ctx.op("vector", "tensor_tensor", out=mid, in0=lo, in1=hi, op=A.add)
        ctx.fence("vector")
        ctx.op("vector", "tensor_scalar", out=mid, in0=mid, scalar1=0.5,
               scalar2=None, op0=A.mult)
        ctx.fence("vector")
    ctx.op("vector", "nop", inc="sV")


def _bisect_finish(ctx, st):
    arf, tiny, ps, trow, onesf = (st["arf"], st["tiny"], st["ps"], st["trow"],
                                  st["onesf"])
    lo = trow[0:1, S:S + 1]
    ctx.wait("tensor", "sV")
    ctx.op("tensor", "matmul", out=ps[:, 2560:2561], lhsT=onesf[:], rhs=lo,
           start=True, stop=True)
    ctx.wait("tensor", "sD")
    ctx.op("tensor", "transpose", out=ps[:, 1536:1552],
           in_=arf[0:16, 8960:9088], identity=st["identf"][0:16, 0:16],
           inc="sT")
    ctx.wait("vector", "sT")
    ctx.op("vector", "tensor_copy", out=tiny[:, 16:16 + NT],
           in_=ps[:, 1536:1552])
    ctx.fence("vector")
    thr = tiny[:, 32:33]
    ctx.op("vector", "tensor_copy", out=thr, in_=ps[:, 2560:2561], inc="sV")
    ctx.fence("vector")
    ctx.wait("scalar", "sV")
    sig = tiny[:, 48:48 + NT]
    ctx.op("scalar", "activation", out=sig, in_=tiny[:, 16:16 + NT],
           func=AF.Sigmoid, inc="sC")
    ctx.wait("vector", "sC")
    msk = tiny[:, 0:NT]
    ctx.op("vector", "tensor_scalar", out=msk, in0=tiny[:, 16:16 + NT],
           scalar1=thr, scalar2=None, op0=A.is_ge)
    ctx.fence("vector")
    ctx.op("vector", "tensor_tensor", out=msk, in0=msk, in1=sig, op=A.mult,
           inc="sV")
    ctx.tap("wtok", msk)



def phase_F(ctx, st, s):
    arb, arf, tiny, ps, sc, p, identf = (st["arb"], st["arf"], st["tiny"],
                                         st["ps"], st["sc"], st["p"],
                                         st["identf"])
    cin = st["cin"]
    gl = arf[:, 0:512]
    upd = arf[:, 512:1024]
    updtok = arf[:, 1024:5120]
    x1g = arf[:, 5120:6144]
    b1fm = tiny[:, 104:136]
    b2fm = tiny[:, 136:144]
    g1t = arf[:, 6208:7232]
    g1o = arf[:, 7232:8256]
    ctx.dma("sync", b1fm, cin["b1fm"])
    ctx.dma("sync", b2fm, cin["b2fm"])
    w1h = [arb[:, R4:R4 + 1024], arb[:, R4 + 1024:R4 + 2048]]
    w1l = [arb[:, R4 + 2048:R4 + 3072], arb[:, R4 + 3072:R4 + 4096]]
    w2h = arb[:, R4 + 4096:R4 + 8192]
    w2l = arb[:, R4 + 8192:R4 + 12288]
    hfm_h = arb[:, R2:R2 + RW]
    hfm_l = arb[:, R3:R3 + RW]
    ctx.wait("tensor", "sD")
    ctx.wait("scalar", "sD")
    _bisect_start(ctx, st)
    pipeF1 = Pipe(ctx, "sV", 2)
    pipeF2 = Pipe(ctx, "sC", 2)

    for grp in range(4):
        for fc in range(IC):
            pb = fc % 2
            ctx.wait("sync", "sT")
            ctx.dma("sync", w1h[pb][:].rearrange("p (c n) -> p c n", c=DC),
                    p["W1_h"].rearrange("(c p) n -> p c n", p=128)[:, :, 128 * fc:128 * (fc + 1)])
            ctx.dma("sync", w1l[pb][:].rearrange("p (c n) -> p c n", c=DC),
                    p["W1_l"].rearrange("(c p) n -> p c n", p=128)[:, :, 128 * fc:128 * (fc + 1)])
            ctx.wait("tensor", "sD")
            bank = fc % 2
            ob = ps[:, 512 * bank:512 * (bank + 1)]
            pipeF1.pre()
            for c in range(DC):
                _mm3_acc(ctx,
                         w1h[pb][:, 128 * c:128 * (c + 1)],
                         w1l[pb][:, 128 * c:128 * (c + 1)],
                         arb[:, R0 + 2048 * c + 512 * grp:R0 + 2048 * c + 512 * (grp + 1)],
                         arb[:, R1 + 2048 * c + 512 * grp:R1 + 2048 * c + 512 * (grp + 1)],
                         ob, first=(c == 0), last=(c == DC - 1))
            ctx.op("tensor", "nop", inc="sT")
            ctx.wait("scalar", "sT")
            ctx.wait("scalar", "sV")
            ctx.op("scalar", "activation", out=gl, in_=ob,
                   func=AF.Gelu_apprx_tanh, bias=b1fm[:, fc:fc + 1],
                   scale=1.0, inc="sC")
            ctx.wait("vector", "sC")
            _hl(ctx, hfm_h[:, 512 * fc:512 * (fc + 1)],
                hfm_l[:, 512 * fc:512 * (fc + 1)], gl)
            ctx.op("vector", "nop", inc="sV")
            pipeF1.post()
        if grp == 0:
            _bisect_finish(ctx, st)
        for dc in range(DC):
            ctx.wait("sync", "sT")
            ctx.dma("sync", w2h[:].rearrange("p (c n) -> p c n", c=IC),
                    p["W2_h"].rearrange("(c p) n -> p c n", p=128)[:, :, 128 * dc:128 * (dc + 1)])
            ctx.dma("sync", w2l[:].rearrange("p (c n) -> p c n", c=IC),
                    p["W2_l"].rearrange("(c p) n -> p c n", p=128)[:, :, 128 * dc:128 * (dc + 1)])
            ctx.wait("tensor", "sD")
            bank = 2 + dc % 2
            ob = ps[:, 512 * bank:512 * (bank + 1)]
            pipeF2.pre()
            ctx.wait("tensor", "sV")
            for c in range(IC):
                _mm3_acc(ctx,
                         w2h[:, 128 * c:128 * (c + 1)],
                         w2l[:, 128 * c:128 * (c + 1)],
                         hfm_h[:, 512 * c:512 * (c + 1)],
                         hfm_l[:, 512 * c:512 * (c + 1)],
                         ob, first=(c == 0), last=(c == IC - 1))
            ctx.op("tensor", "nop", inc="sT")
            ctx.wait("vector", "sT")
            ctx.op("vector", "tensor_scalar", out=upd, in0=ob,
                   scalar1=b2fm[:, dc:dc + 1], scalar2=None, op0=A.add,
                   inc="sC")
            pipeF2.post()
            ctx.wait("tensor", "sC")
            for j in range(4):
                bk = 4 + j % 2
                ctx.op("tensor", "transpose", out=ps[:, 512 * bk:512 * bk + 128],
                       in_=upd[:, 128 * j:128 * (j + 1)], identity=identf[:],
                       inc="sT")
                ctx.wait("vector", "sT")
                ctx.op("vector", "tensor_copy",
                       out=updtok[:, 1024 * j + 128 * dc:1024 * j + 128 * (dc + 1)],
                       in_=ps[:, 512 * bk:512 * bk + 128], inc="sV")
                ctx.wait("tensor", "sV")
        for j in range(4):
            t = 4 * grp + j
            ctx.wait("sync", "sV")
            ctx.dma("sync", x1g, sc["xn1"][128 * t:128 * (t + 1), :])
            ctx.wait("vector", "sD")
            ctx.op("vector", "scalar_tensor_tensor", out=g1t,
                   in0=updtok[:, 1024 * j:1024 * (j + 1)],
                   scalar=tiny[:, t:t + 1], in1=x1g, op0=A.mult, op1=A.add,
                   inc="sV")
            ctx.wait("scalar", "sV")
            ctx.wait("scalar", "sSp")
            ctx.op("scalar", "activation", out=g1o, in_=g1t, func=AF.Silu,
                   inc="sC")
            ctx.wait("sync", "sC")
            ctx.dma("sync", sc["g1"][128 * t:128 * (t + 1), :], g1o, inc="sSp")
    ctx.tap("g1", sc["g1"])



def phase_G(ctx, st, s):
    arb, arf, ps, sc, p = st["arb"], st["arf"], st["ps"], st["sc"], st["p"]
    ev_h = st["stg"][:, 0:512]
    ev_l = st["stg"][:, 512:1024]
    pipeG = Pipe(ctx, "sV", 4)
    wch = [arb[:, R4:R4 + 1024], arb[:, R4 + 1024:R4 + 2048]]
    wcl = [arb[:, R4 + 2048:R4 + 3072], arb[:, R4 + 3072:R4 + 4096]]

    for m in range(16):
        pb = m % 2
        ctx.wait("sync", "sT")
        ctx.dma("sync", wch[pb][:].rearrange("p (c n) -> p c n", c=DC),
                p["wqkvT_h"].rearrange("(c p) n -> p c n", p=128)[:, :, 128 * m:128 * (m + 1)])
        ctx.dma("sync", wcl[pb][:].rearrange("p (c n) -> p c n", c=DC),
                p["wqkvT_l"].rearrange("(c p) n -> p c n", p=128)[:, :, 128 * m:128 * (m + 1)])
        ctx.wait("tensor", "sD")
        dst_h = sc["q1_h"] if m < 8 else sc["k1_h"]
        dst_l = sc["q1_l"] if m < 8 else sc["k1_l"]
        mm = m % 8
        for n in range(4):
            bank = n % 4
            ob = ps[:, 512 * bank:512 * (bank + 1)]
            pipeG.pre()
            for c in range(DC):
                _mm3_acc(ctx, wch[pb][:, 128 * c:128 * (c + 1)],
                         wcl[pb][:, 128 * c:128 * (c + 1)],
                         arb[:, R0 + 2048 * c + 512 * n:R0 + 2048 * c + 512 * (n + 1)],
                         arb[:, R1 + 2048 * c + 512 * n:R1 + 2048 * c + 512 * (n + 1)],
                         ob, first=(c == 0), last=(c == DC - 1))
            ctx.op("tensor", "nop", inc="sT")
            ctx.wait("vector", "sT")
            ctx.wait("vector", "sSp")
            _hl(ctx, ev_h[:, 0:512], ev_l[:, 0:512], ob)
            ctx.op("vector", "nop", inc="sV")
            ctx.wait("sync", "sV")
            ctx.dma("sync", dst_h[128 * mm:128 * (mm + 1), 512 * n:512 * (n + 1)],
                    ev_h[:, 0:512], inc="sSp")
            ctx.dma("sync", dst_l[128 * mm:128 * (mm + 1), 512 * n:512 * (n + 1)],
                    ev_l[:, 0:512], inc="sSp")
            pipeG.post()
    vwh = arb[:, R4:R4 + 8192]
    vwl = arb[:, R4 + 8192:R4 + 16384]
    ctx.wait("sync", "sT")
    ctx.dma("sync", vwh[:].rearrange("p (c n) -> p c n", c=DC),
            p["wqkvT_h"].rearrange("(c p) n -> p c n", p=128)[:, :, 2048:3072])
    ctx.dma("sync", vwl[:].rearrange("p (c n) -> p c n", c=DC),
            p["wqkvT_l"].rearrange("(c p) n -> p c n", p=128)[:, :, 2048:3072])
    ctx.wait("tensor", "sD")
    for t in range(NT):
        for b in range(2):
            bank = (t * 2 + b) % 4
            ob = ps[:, 512 * bank:512 * (bank + 1)]
            pipeG.pre()
            for c in range(DC):
                xh = arb[:, R0 + 2048 * c + 128 * t:R0 + 2048 * c + 128 * t + 128]
                xl = arb[:, R1 + 2048 * c + 128 * t:R1 + 2048 * c + 128 * t + 128]
                _mm3_acc(ctx, xh, xl,
                         vwh[:, 1024 * c + 512 * b:1024 * c + 512 * (b + 1)],
                         vwl[:, 1024 * c + 512 * b:1024 * c + 512 * (b + 1)],
                         ob, first=(c == 0), last=(c == DC - 1))
            ctx.op("tensor", "nop", inc="sT")
            ctx.wait("vector", "sT")
            a2 = R2 + 1024 * t + 512 * b
            a3 = R3 + 1024 * t + 512 * b
            _hl(ctx, arb[:, a2:a2 + 512], arb[:, a3:a3 + 512], ob)
            ctx.op("vector", "nop", inc="sV")
            pipeG.post()



def phase_H(ctx, st, s):
    arb, arf, tiny, ps, sc, p, identb = (st["arb"], st["arf"], st["tiny"],
                                         st["ps"], st["sc"], st["p"],
                                         st["identb"])
    onesf, brow = st["onesf"], st["brow"]
    cin = st["cin"]
    base = s * S
    pj_h = arb[:, R4:R4 + 1024]
    pj_l = arb[:, R4 + 1024:R4 + 2048]
    ctx.dma("sync", pj_h[:].rearrange("p (c n) -> p c n", c=DC),
            p["projT_h"].rearrange("(c p) n -> p c n", p=128))
    ctx.dma("sync", pj_l[:].rearrange("p (c n) -> p c n", c=DC),
            p["projT_l"].rearrange("(c p) n -> p c n", p=128))
    ctx.dma("sync", brow[:], cin["boutr"])
    hr = R4 + 2048
    qh_h = arb[:, hr:hr + 2048]
    qh_l = arb[:, hr + 2048:hr + 4096]
    kh_h = arb[:, hr + 4096:hr + 6144]
    kh_l = arb[:, hr + 6144:hr + 8192]
    qp_h, qp_l, kp_h, kp_l = qh_h, qh_l, kh_h, kh_l
    at_h = arb[:, R4 + 10240:R4 + 10368]
    at_l = arb[:, R4 + 10368:R4 + 10496]
    ks_h = arb[:, R4 + 10496:R4 + 10560]
    ks_l = arb[:, R4 + 10560:R4 + 10624]
    an_h = arb[:, R4 + 10624:R4 + 10752]
    an_l = arb[:, R4 + 10752:R4 + 10880]
    qpf = arf[:, 0:2048]
    kpf = arf[:, 2048:4096]
    ksf = arf[:, 4096:4160]
    zbuf = arf[:, 4160:4176]
    atf = arf[:, 4304:4432]
    att = arf[:, 4432:4560]
    rlu = arf[:, 4560:5072]
    ctx.wait("tensor", "sD")
    pipeHV = Pipe(ctx, "sV", 3)
    pipeHC = Pipe(ctx, "sC", 3)

    for h in range(HEADS):
        ctx.wait("sync", "sT")
        ctx.dma("sync", qh_h, sc["q1_h"][128 * h:128 * (h + 1), :])
        ctx.dma("sync", qh_l, sc["q1_l"][128 * h:128 * (h + 1), :])
        ctx.dma("sync", kh_h, sc["k1_h"][128 * h:128 * (h + 1), :])
        ctx.dma("sync", kh_l, sc["k1_l"][128 * h:128 * (h + 1), :])
        ctx.wait("tensor", "sD")
        for src_h, src_l, dst in ((qh_h, qh_l, qpf), (kh_h, kh_l, kpf)):
            for n in range(4):
                bank = n % 4
                ob = ps[:, 512 * bank:512 * (bank + 1)]
                pipeHV.pre()
                pipeHC.pre()
                _mm3_acc(ctx, pj_h[:, 128 * h:128 * (h + 1)],
                         pj_l[:, 128 * h:128 * (h + 1)],
                         src_h[:, 512 * n:512 * (n + 1)],
                         src_l[:, 512 * n:512 * (n + 1)],
                         ob, first=True, last=True)
                ctx.op("tensor", "nop", inc="sT")
                ctx.wait("vector", "sT")
                sl = slice(512 * n, 512 * (n + 1))
                ctx.op("vector", "tensor_scalar", out=dst[:, sl], in0=ob,
                       scalar1=0.0, scalar2=None, op0=A.min, inc="sV")
                ctx.wait("scalar", "sV")
                ctx.op("scalar", "activation", out=dst[:, sl], in_=dst[:, sl],
                       func=AF.Exp)
                ctx.op("scalar", "activation", out=rlu, in_=ob, func=AF.Relu,
                       inc="sC")
                ctx.wait("vector", "sC")
                ctx.op("vector", "tensor_tensor", out=dst[:, sl],
                       in0=dst[:, sl], in1=rlu, op=A.add, inc="sV")
                pipeHV.post()
                pipeHC.post()
        _hl(ctx, qp_h, qp_l, qpf)
        _hl(ctx, kp_h, kp_l, kpf)
        ctx.op("vector", "tensor_reduce", out=ksf,
               in_=kpf.rearrange("p (b s) -> p b s", s=BLK), axis=AX, op=A.add)
        ctx.fence("vector")
        _hl(ctx, ks_h, ks_l, ksf, fence=True)
        ctx.op("vector", "nop", inc="sV")
        ctx.wait("tensor", "sV")
        for gi in range(NT):
            c0 = 128 * gi
            obA = ps[:, 2048:2176]
            ctx.op("tensor", "matmul", out=obA, lhsT=kp_h[:, c0:c0 + 128],
                   rhs=qp_h[:, c0:c0 + 128], start=True, stop=False)
            ctx.op("tensor", "matmul", out=obA, lhsT=kp_h[:, c0:c0 + 128],
                   rhs=qp_l[:, c0:c0 + 128], start=False, stop=False)
            ctx.op("tensor", "matmul", out=obA, lhsT=kp_l[:, c0:c0 + 128],
                   rhs=qp_h[:, c0:c0 + 128], start=False, stop=True)
            for b in range(4):
                zo = ps[32 * b:32 * (b + 1), 2560:2561]
                tp = (0, 32 * b)
                ctx.op("tensor", "matmul", out=zo,
                       lhsT=qp_h[:, c0 + 32 * b:c0 + 32 * (b + 1)],
                       rhs=ks_h[:, 4 * gi + b:4 * gi + b + 1],
                       tile_position=tp, start=True, stop=False)
                ctx.op("tensor", "matmul", out=zo,
                       lhsT=qp_h[:, c0 + 32 * b:c0 + 32 * (b + 1)],
                       rhs=ks_l[:, 4 * gi + b:4 * gi + b + 1],
                       tile_position=tp, start=False, stop=False)
                ctx.op("tensor", "matmul", out=zo,
                       lhsT=qp_l[:, c0 + 32 * b:c0 + 32 * (b + 1)],
                       rhs=ks_h[:, 4 * gi + b:4 * gi + b + 1],
                       tile_position=tp, start=False, stop=True)
            ctx.op("tensor", "nop", inc="sT")
            ctx.wait("vector", "sT")
            ctx.op("vector", "tensor_copy", out=atf, in_=obA)
            ctx.fence("vector")
            _hl(ctx, at_h, at_l, atf, fence=True)
            ctx.op("vector", "tensor_scalar", out=zbuf[:, 0:1],
                   in0=ps[:, 2560:2561], scalar1=1e-8, scalar2=None, op0=A.add)
            ctx.fence("vector")
            ctx.op("vector", "reciprocal", out=zbuf[:, 1:2], in_=zbuf[:, 0:1],
                   inc="sV")
            ctx.fence("vector")
            ctx.wait("tensor", "sV")
            obN = ps[:, 0:128]
            for b in range(4):
                vh = arb[32 * b:32 * (b + 1),
                         R2 + 1024 * gi + 128 * h:R2 + 1024 * gi + 128 * (h + 1)]
                vl = arb[32 * b:32 * (b + 1),
                         R3 + 1024 * gi + 128 * h:R3 + 1024 * gi + 128 * (h + 1)]
                ah = at_h[32 * b:32 * (b + 1), 32 * b:32 * (b + 1)]
                al = at_l[32 * b:32 * (b + 1), 32 * b:32 * (b + 1)]
                on = obN[32 * b:32 * (b + 1), :]
                tp = (32 * b, 32 * b)
                ctx.op("tensor", "matmul", out=on, lhsT=ah, rhs=vh,
                       tile_position=tp, start=True, stop=False)
                ctx.op("tensor", "matmul", out=on, lhsT=ah, rhs=vl,
                       tile_position=tp, start=False, stop=False)
                ctx.op("tensor", "matmul", out=on, lhsT=al, rhs=vh,
                       tile_position=tp, start=False, stop=True)
            ctx.op("tensor", "nop", inc="sT")
            ctx.wait("vector", "sT")
            ctx.op("vector", "tensor_scalar", out=att, in0=obN,
                   scalar1=zbuf[:, 1:2], scalar2=None, op0=A.mult)
            ctx.fence("vector")
            _hl(ctx, an_h, an_l, att, fence=True)
            ctx.op("vector", "nop", inc="sV")
            ctx.wait("tensor", "sV")
            psb = st["psb"]
            ctx.op("tensor", "transpose", out=psb[:, 0:128],
                   in_=an_h, identity=identb[:])
            ctx.op("tensor", "transpose", out=psb[:, 128:256],
                   in_=an_l, identity=identb[:], inc="sT")
            ctx.wait("vector", "sT")
            a0 = R0 + 2048 * h + 128 * gi
            a1 = R1 + 2048 * h + 128 * gi
            ctx.op("vector", "tensor_copy", out=arb[:, a0:a0 + 128],
                   in_=psb[:, 0:128])
            ctx.op("vector", "tensor_copy", out=arb[:, a1:a1 + 128],
                   in_=psb[:, 128:256], inc="sV")
            ctx.wait("tensor", "sV")
    ctx.tap("attnfm_hi", arb[:, R0:R0 + RW])

    woh = arb[:, R4:R4 + 8192]
    wol = arb[:, R4 + 8192:R4 + 16384]
    ctx.wait("sync", "sT")
    ctx.dma("sync", woh[:].rearrange("p (c n) -> p c n", c=DC),
            p["woT_h"].rearrange("(c p) n -> p c n", p=128))
    ctx.dma("sync", wol[:].rearrange("p (c n) -> p c n", c=DC),
            p["woT_l"].rearrange("(c p) n -> p c n", p=128))
    ctx.wait("tensor", "sD")
    x1t = arf[:, 0:1024]
    g1t = arf[:, 1024:2048]
    ot = arf[:, 2048:3072]
    for t in range(NT):
        ctx.wait("sync", "sV")
        ctx.dma("sync", x1t, sc["x1"][128 * t:128 * (t + 1), :])
        ctx.dma("sync", g1t, sc["g1"][128 * t:128 * (t + 1), :])
        dv = ctx.val("sD")
        for b in range(2):
            ob = ps[:, 512 * b:512 * (b + 1)]
            ctx.wait("tensor", "sV")
            ctx.op("tensor", "matmul", out=ob, lhsT=onesf[:],
                   rhs=brow[0:1, 512 * b:512 * (b + 1)], start=True, stop=False)
            for c in range(DC):
                _mm3_acc(ctx,
                         arb[:, R0 + 2048 * c + 128 * t:R0 + 2048 * c + 128 * t + 128],
                         arb[:, R1 + 2048 * c + 128 * t:R1 + 2048 * c + 128 * t + 128],
                         woh[:, 1024 * c + 512 * b:1024 * c + 512 * (b + 1)],
                         wol[:, 1024 * c + 512 * b:1024 * c + 512 * (b + 1)],
                         ob, first=False, last=(c == DC - 1))
            ctx.op("tensor", "nop", inc="sT")
        ctx.wait("vector", "sT")
        ctx.wait("vector", "sD")
        ctx.wait("vector", "sSp")
        for b in range(2):
            ob = ps[:, 512 * b:512 * (b + 1)]
            sl = slice(512 * b, 512 * (b + 1))
            ctx.op("vector", "tensor_tensor", out=ot[:, sl], in0=ob,
                   in1=g1t[:, sl], op=A.mult)
            ctx.op("vector", "tensor_tensor", out=ot[:, sl], in0=ot[:, sl],
                   in1=x1t[:, sl], op=A.add)
        ctx.op("vector", "nop", inc="sV")
        ctx.wait("sync", "sV")
        ctx.dma("sync", st["out"][base + 128 * t:base + 128 * (t + 1), :], ot,
                inc="sSp")



def _mk_consts():
    identf = np.eye(128, dtype=np.float32)
    identb = np.eye(128, dtype=ml_dtypes.bfloat16)
    onesf = np.ones((1, 128), dtype=np.float32)
    onescol = np.ones((128, 1), dtype=np.float32)
    masks = np.zeros((128, 4 * 512), dtype=np.float32)
    for pp in range(4):
        m = np.zeros((128, 512), dtype=np.float32)
        m[:, 128 * (pp + 1):] = NEG
        m[:, 128 * pp:128 * (pp + 1)] = np.triu(
            np.full((128, 128), NEG, dtype=np.float32), 1)
        masks[:, 512 * pp:512 * (pp + 1)] = m
    return identf, identb, onesf, onescol, masks


def kernel(L0_attn_x, L0_attn_WQ, L0_attn_WK, L0_attn_WV, L0_ffn_w_gate,
           L0_ffn_w_up, L0_ffn_w_down, L1_attn_w_qkv, L1_attn_w_out,
           L1_attn_b_out, L1_attn_proj, L1_ffn_w_router, L1_ffn_W1, L1_ffn_b1,
           L1_ffn_W2, L1_ffn_b2, _dbg=None, _cores=None, _stop=None):
    x = np.ascontiguousarray(np.asarray(L0_attn_x, dtype=np.float32))
    identf, identb, onesf, onescol, masks = _mk_consts()
    T = lambda a: np.ascontiguousarray(np.asarray(a, dtype=np.float32).T)
    C = lambda a: np.ascontiguousarray(np.asarray(a, dtype=np.float32))
    w = dict(
        wqT=T(L0_attn_WQ), wkT=T(L0_attn_WK), wvT=T(L0_attn_WV),
        wgT=T(L0_ffn_w_gate), wuT=T(L0_ffn_w_up), wdT=T(L0_ffn_w_down),
        wqkvT=T(L1_attn_w_qkv), woT=T(L1_attn_w_out),
        projT=C(L1_attn_proj).reshape(D, FD),
        W1=C(L1_ffn_W1), W2=C(L1_ffn_W2),
        wrT=C(L1_ffn_w_router).reshape(D, 1),
        identf=identf, identb=identb, onesf=onesf, onescol=onescol,
        masks=masks.astype(ml_dtypes.bfloat16),
        b1fm=np.ascontiguousarray(C(L1_ffn_b1).reshape(IC, 128).T),
        b2fm=np.ascontiguousarray(C(L1_ffn_b2).reshape(DC, 128).T),
        boutr=C(L1_attn_b_out).reshape(1, D),
        bconst=np.full((128, 1), 1e-5, dtype=np.float32),
    )
    nc = build_core(dbg=_dbg, stop=_stop)
    ncores = NCORE if _cores is None else _cores
    in_maps = []
    for c in range(ncores):
        m = dict(w)
        m["x"] = np.ascontiguousarray(x[NSEQ * c:NSEQ * (c + 1)].reshape(NSEQ * S, D))
        in_maps.append(m)
    res = run_bass_kernel_spmd(nc, in_maps, core_ids=list(range(ncores)))
    if _dbg:
        return res
    outs = [np.asarray(r["out"]).reshape(NSEQ, S, D) for r in res.results]
    return np.concatenate(outs, axis=0).astype(np.float32)
